# version 27
# baseline (speedup 1.0000x reference)
import sys
sys.path.insert(0, '/opt/trn_rl_repo')
import contextlib
import numpy as np
import concourse.bass as bass
import concourse.mybir as mybir
from concourse.bass_utils import run_bass_kernel_spmd

F32 = mybir.dt.float32
BF16 = mybir.dt.bfloat16
U32 = mybir.dt.uint32
F32R = mybir.dt.float32r
AF = mybir.ActivationFunctionType
ALU = mybir.AluOpType
AX = mybir.AxisListType

D = 1024
DC = 8
H = 2048
HT = 16
E = 8
T2 = 512
CH = 128
NCORES = 8
OP_TRACE = {"tensor": [], "note": None}


def build_program(n_seq, L):
    OP_TRACE["tensor"] = []
    T = n_seq * L
    NT2 = T // T2
    T2_PER_SEQ = L // T2
    NTILES = T // T2 + E
    SLOTROWS = E * T + NTILES * T2 + 1
    ZROW = SLOTROWS - 1
    NCHUNK = T // CH
    assert L % T2 == 0 and T2 % CH == 0

    nc = bass.Bass("TRN2", target_bir_lowering=False)

    xt = nc.dram_tensor("xt", [D, T], F32, kind="ExternalInput").ap()
    w_in_t = nc.dram_tensor("w_in_t", [D, D], F32, kind="ExternalInput").ap()
    w_out_t = nc.dram_tensor("w_out_t", [D, D], F32, kind="ExternalInput").ap()
    lam_log = nc.dram_tensor("lam_log", [1, D], F32, kind="ExternalInput").ap()
    gamma_d = nc.dram_tensor("gamma_d", [1, D], F32, kind="ExternalInput").ap()
    b_in_d = nc.dram_tensor("b_in_d", [1, D], F32, kind="ExternalInput").ap()
    b_out_d = nc.dram_tensor("b_out_d", [1, D], F32, kind="ExternalInput").ap()
    wr_d = nc.dram_tensor("wr_d", [D, E], F32, kind="ExternalInput").ap()
    br_d = nc.dram_tensor("br_d", [E, 1], F32, kind="ExternalInput").ap()
    w1_d = nc.dram_tensor("w1_d", [E, D, H], F32, kind="ExternalInput").ap()
    b1_d = nc.dram_tensor("b1_d", [E, H], F32, kind="ExternalInput").ap()
    w2_d = nc.dram_tensor("w2_d", [E, H, D], F32, kind="ExternalInput").ap()
    b2_d = nc.dram_tensor("b2_d", [E, D], F32, kind="ExternalInput").ap()
    tri_d = nc.dram_tensor("tri_d", [CH, CH], F32, kind="ExternalInput").ap()
    iden_d = nc.dram_tensor("iden_d", [128, 128], F32, kind="ExternalInput").ap()
    iotaE_d = nc.dram_tensor("iotaE_d", [128, E], F32, kind="ExternalInput").ap()
    negsidx_d = nc.dram_tensor("negsidx_d", [128, 1], F32, kind="ExternalInput").ap()
    tix_d = nc.dram_tensor("tix_d", [1, 128], F32, kind="ExternalInput").ap()
    ones1_d = nc.dram_tensor("ones1_d", [1, 128], F32, kind="ExternalInput").ap()
    onescol_d = nc.dram_tensor("onescol_d", [128, 1], F32, kind="ExternalInput").ap()
    zrow_d = nc.dram_tensor("zrow_d", [1, E], F32, kind="ExternalInput").ap()
    zrowD_d = nc.dram_tensor("zrowD_d", [1, D], F32, kind="ExternalInput").ap()
    iotaQ_d = nc.dram_tensor("iotaQ_d", [128, 4], F32, kind="ExternalInput").ap()
    thresh_d = nc.dram_tensor("thresh_d", [1, E * 16], F32, kind="ExternalInput").ap()
    tau_d = nc.dram_tensor("tau_d", [1, NTILES * E], F32, kind="ExternalInput").ap()
    iotaE24_d = nc.dram_tensor("iotaE24_d", [1, NTILES * E], F32, kind="ExternalInput").ap()
    zaux_d = nc.dram_tensor("zaux_d", [1, 2], F32, kind="ExternalInput").ap()
    out_t = nc.dram_tensor("out", [T, D], F32, kind="ExternalOutput").ap()
    slotaux = nc.dram_tensor("slotaux", [SLOTROWS, 2], F32, kind="Internal").ap()
    x1tok = nc.dram_tensor("x1tok", [T, D], F32, kind="Internal").ap()
    slotbuf = nc.dram_tensor("slotbuf", [SLOTROWS, D], BF16, kind="Internal").ap()
    w1s = nc.dram_tensor("w1s", [E * 128, DC * H], BF16, kind="Internal").ap()
    w2s = nc.dram_tensor("w2s", [E * 128, HT * D], BF16, kind="Internal").ap()
    b1cols = nc.dram_tensor("b1cols", [E * 128, HT], F32, kind="Internal").ap()
    b2cols = nc.dram_tensor("b2cols", [E * 128, DC], F32, kind="Internal").ap()

    ctx = contextlib.ExitStack()
    EN = ctx.enter_context
    wbA = EN(nc.sbuf_tensor("wbA", [128, 16384], BF16))
    wbB = EN(nc.sbuf_tensor("wbB", [128, 16384], BF16))
    hm = EN(nc.sbuf_tensor("hm", [128, HT, T2], BF16))
    bigX = EN(nc.sbuf_tensor("bigX", [128, 2, DC, T2], F32))
    xsq = EN(nc.sbuf_tensor("xsq", [128, 2, T2], F32))
    xb = EN(nc.sbuf_tensor("xb", [128, 2, T2], BF16))
    xsqb = EN(nc.sbuf_tensor("xsqb", [128, 2, T2], BF16))
    onescol_b = EN(nc.sbuf_tensor("onescol_b", [128, 1], BF16))
    lnx = EN(nc.sbuf_tensor("lnx", [128, 2, DC, T2], BF16))
    gub = EN(nc.sbuf_tensor("gub", [128, 2, D], BF16))
    hsb = EN(nc.sbuf_tensor("hsb", [128, DC, T2], BF16))
    scr = EN(nc.sbuf_tensor("scr", [128, 2048], F32))
    logits_sb = EN(nc.sbuf_tensor("logits_sb", [E, 128], F32))
    rowbuf = EN(nc.sbuf_tensor("rowbuf", [1, 3 * T2], F32))
    rsc = EN(nc.sbuf_tensor("rsc", [128, 16], F32))
    rsc2 = EN(nc.sbuf_tensor("rsc2", [128, 4, E], F32))
    hinit = EN(nc.sbuf_tensor("hinit", [128, DC], F32))
    pos_sb = EN(nc.sbuf_tensor("pos_sb", [128, NCHUNK], F32))
    mp_sb = EN(nc.sbuf_tensor("mp_sb", [128, NCHUNK], F32))
    x1row = EN(nc.sbuf_tensor("x1row", [128, 2, D], F32))
    ln2row = EN(nc.sbuf_tensor("ln2row", [128, 2, D], BF16))
    ln2d = EN(nc.sbuf_tensor("ln2d", [128, D], BF16))
    lnTg = EN(nc.sbuf_tensor("lnTg", [128, 4, D], BF16))
    yrow = EN(nc.sbuf_tensor("yrow", [128, 2, D], BF16))
    idxs_f = EN(nc.sbuf_tensor("idxs_f", [128, 2, 14], F32))
    idxs_u = EN(nc.sbuf_tensor("idxs_u", [128, 2, 14], U32))
    auxg = EN(nc.sbuf_tensor("auxg", [128, 2, 2], F32))
    aux_sb = EN(nc.sbuf_tensor("aux_sb", [128, 2], F32))
    posU = EN(nc.sbuf_tensor("posU", [128, 2], U32))
    posUall = EN(nc.sbuf_tensor("posUall", [128, 64], U32))
    b1tile = EN(nc.sbuf_tensor("b1tile", [128, HT], F32))
    b2tile = EN(nc.sbuf_tensor("b2tile", [128, DC], F32))
    rc8 = EN(nc.sbuf_tensor("rc8", [1, E], F32))
    finrow = EN(nc.sbuf_tensor("finrow", [1, 4 * NTILES], F32))
    startrow = EN(nc.sbuf_tensor("startrow", [1, 16], F32))
    grow = EN(nc.sbuf_tensor("grow", [1, max(128, NTILES * E)], F32))
    tri_f = EN(nc.sbuf_tensor("tri_f", [128, 128], F32))
    tri_b = EN(nc.sbuf_tensor("tri_b", [128, 128], BF16))
    iden = EN(nc.sbuf_tensor("iden", [128, 128], F32))
    idenb = EN(nc.sbuf_tensor("idenb", [128, 128], BF16))
    iotaE = EN(nc.sbuf_tensor("iotaE", [128, E], F32))
    negsidx = EN(nc.sbuf_tensor("negsidx", [128, 1], F32))
    tix = EN(nc.sbuf_tensor("tix", [1, 128], F32))
    ones1 = EN(nc.sbuf_tensor("ones1", [1, 128], F32))
    onescol = EN(nc.sbuf_tensor("onescol", [128, 1], F32))
    iotaQ = EN(nc.sbuf_tensor("iotaQ", [128, 4], F32))
    thresh16 = EN(nc.sbuf_tensor("thresh16", [1, E * 16], F32))
    tau24 = EN(nc.sbuf_tensor("tau24", [1, NTILES * E], F32))
    iotaE24 = EN(nc.sbuf_tensor("iotaE24", [1, NTILES * E], F32))
    lamNegPow = EN(nc.sbuf_tensor("lamNegPow", [128, D], BF16))
    gbLam = EN(nc.sbuf_tensor("gbLam", [128, D], BF16))
    lamPow = EN(nc.sbuf_tensor("lamPow", [128, DC, CH], F32))
    lamC64 = EN(nc.sbuf_tensor("lamC64", [128, DC], F32))
    LCol = EN(nc.sbuf_tensor("LCol", [128, DC], F32))
    L8 = EN(nc.sbuf_tensor("L8", [DC, 128], F32))
    b1Col = EN(nc.sbuf_tensor("b1Col", [128, HT * E], F32))
    b2ColE = EN(nc.sbuf_tensor("b2ColE", [128, E * DC], F32))
    boCol = EN(nc.sbuf_tensor("boCol", [128, DC], F32))
    br_col = EN(nc.sbuf_tensor("br_col", [E, 1], F32))
    wr_b = EN(nc.sbuf_tensor("wr_b", [128, DC * E], BF16))
    pA = EN(nc.psum_tensor("pA", [128, 2, 512], F32))
    pB = EN(nc.psum_tensor("pB", [128, 2, 512], F32))
    pC = EN(nc.psum_tensor("pC", [128, 1024], F32))
    pD = EN(nc.psum_tensor("pD", [128, 2, 512], F32))
    semh = {}
    for name in ("sd", "sd2", "gd", "gd2", "gd3", "gd4", "gd5", "gd6", "gd7", "gd8", "sd3", "sd4", "pe", "act", "dve"):
        semh[name] = EN(nc.semaphore("s_" + name))
    breg = EN(nc.gpsimd.register("breg"))
    block = EN(nc.Block())

    ENG_SEM = {"sync": "sd", "gpsimd": "gd", "tensor": "pe", "vector": "dve",
               "scalar": "act"}
    QLANES = {"sync": ["sd", "sd2"], "gpsimd": ["gd", "gd2"]}
    cnt = {k: 0 for k in semh}
    Q = {k: [] for k in ENG_SEM}
    regions = {}
    pend = {k: {"w": [], "r": []} for k in ENG_SEM}
    satisfied = {k: {} for k in ENG_SEM}
    lane_rr = {"sync": 0, "gpsimd": 0}

    def R(k):
        if k not in regions:
            regions[k] = {"w": None, "r": {}}
        return regions[k]

    def op(eng, emit, reads=(), writes=(), inc=True, extra=(), sem=None, inc_k=1):
        sem = sem or ENG_SEM[eng]
        want = {}

        def addw(s, v):
            if v and want.get(s, 0) < v:
                want[s] = v
        for k in reads:
            reg = R(k)
            if reg["w"] is not None:
                addw(*reg["w"])
            assert all(k not in pend[e]["w"] for e in pend if e != eng), \
                f"read of pending write {k}"
        for k in writes:
            reg = R(k)
            if reg["w"] is not None:
                addw(*reg["w"])
            for s, v in reg["r"].items():
                addw(s, v)
            assert all(k not in pend[e]["w"] and k not in pend[e]["r"]
                       for e in pend if e != eng), f"write over pending {k}"
        for s, v in extra:
            addw(s, v)
        waits = tuple((s, v) for s, v in want.items()
                      if v > satisfied[eng].get(s, 0))
        for s, v in waits:
            satisfied[eng][s] = v
        post = None
        if inc:
            cnt[sem] += inc_k
            post = cnt[sem]
            for k in pend[eng]["w"] + list(writes):
                R(k)["w"] = (sem, post)
                R(k)["r"] = {}
            for k in pend[eng]["r"] + list(reads):
                R(k)["r"][sem] = post
            pend[eng] = {"w": [], "r": []}
        else:
            pend[eng]["w"] += list(writes)
            pend[eng]["r"] += list(reads)

        if eng == "tensor":
            OP_TRACE["tensor"].append((waits, tuple(reads), tuple(writes)))

        def fn(h, emit=emit, waits=waits, inc=inc, sem=sem, inc_k=inc_k):
            for s, v in waits:
                h.wait_ge(semh[s], v)
            ins = emit(h)
            if inc:
                ins.then_inc(semh[sem], inc_k)
        Q[eng].append(fn)
        return post

    def dma(queue, dst, src, reads=(), writes=(), indirect=None, lane=None):
        if lane is None:
            lane = QLANES[queue][lane_rr[queue]]
            lane_rr[queue] ^= 1
        prev = cnt[lane]
        if indirect is None:
            emit = lambda h, d=dst, s=src: h.dma_start(out=d, in_=s)
        else:
            kind, idx_ap = indirect[0], indirect[1]
            bc = indirect[2] if len(indirect) > 2 else None
            if kind == "gather":
                emit = lambda h, d=dst, s=src, i=idx_ap: h.indirect_dma_start(
                    out=d, out_offset=None, in_=s,
                    in_offset=bass.IndirectOffsetOnAxis(ap=i, axis=0))
            else:
                emit = lambda h, d=dst, s=src, i=idx_ap, bc=bc: h.indirect_dma_start(
                    out=d, out_offset=bass.IndirectOffsetOnAxis(ap=i, axis=0),
                    in_=s, in_offset=None,
                    **({} if bc is None else
                       {"bounds_check": bc, "oob_is_err": False}))
        return op(queue, emit, reads=reads, writes=writes, sem=lane, inc_k=16,
                  extra=([(lane, prev)] if prev else []))

    def final_wait(eng, sems_to_wait):
        vals = [(s, cnt[s]) for s in sems_to_wait if cnt[s] > 0]

        def fn(h, vals=tuple(vals)):
            for s, v in vals:
                h.wait_ge(semh[s], v)
        Q[eng].append(fn)

    xt_r = xt.rearrange("(c p) t -> p c t", c=DC)
    BX = lambda par, dc: ("bigX", par, dc)
    x1d = [wbB[:, 0:8192].bitcast(F32).rearrange("p (a b) -> p a b", a=DC),
           wbB[:, 8192:16384].bitcast(F32).rearrange("p (a b) -> p a b", a=DC)]
    X1K = lambda par, dc: ("x1d", par, dc)
    ALL_X1K = [("x1d", p, d) for p in range(2) for d in range(DC)]

    def load_x(j, par):
        dma("sync", bigX[:, par, 0:4, :], xt_r[:, 0:4, j * T2:(j + 1) * T2],
            writes=[BX(par, dc) for dc in range(4)])
        dma("sync", bigX[:, par, 4:8, :], xt_r[:, 4:8, j * T2:(j + 1) * T2],
            writes=[BX(par, dc) for dc in range(4, 8)])

    load_x(0, 0)
    load_x(1, 1)

    for dst, src, key in [
            (tri_f[:], tri_d, "tri_f"), (iden[:], iden_d, "iden"),
            (iotaE[:], iotaE_d, "iotaE"), (negsidx[:], negsidx_d, "negsidx"),
            (tix[:], tix_d, "tix"), (ones1[:], ones1_d, "ones1"),
            (onescol[:], onescol_d, "onescol"),
            (br_col[:], br_d, "br_col"), (rc8[:], zrow_d, "rc8"),
            (iotaQ[:], iotaQ_d, "iotaQ"), (thresh16[:], thresh_d, "thresh16"),
            (tau24[:], tau_d, "tau24"), (iotaE24[:], iotaE24_d, "iotaE24")]:
        dma("sync", dst, src, writes=[key])
    dma("sync", L8[:], lam_log.rearrange("o (a p) -> (o a) p", a=DC), writes=["L8"])

    op("vector", lambda h: h.tensor_copy(tri_b[:], tri_f[:]),
       reads=["tri_f"], writes=["tri_b"])
    op("vector", lambda h: h.tensor_copy(idenb[:], iden[:]),
       reads=["iden"], writes=["idenb"])
    op("vector", lambda h: h.tensor_copy(onescol_b[:], onescol[:]),
       reads=["onescol"], writes=["onescol_b"])

    dma("gpsimd", slotbuf[ZROW:ZROW + 1, :], zrowD_d, writes=[("slotzero",)])
    dma("gpsimd", slotaux[ZROW:ZROW + 1, :], zaux_d, writes=[("slotauxzero",)])
    dma("gpsimd", wbA[:, 0:8192].rearrange("p (c t) -> p c t", c=DC),
        w_in_t.rearrange("(c p) t -> p c t", c=DC), writes=[("wbA", 0)])
    dma("gpsimd", wbA[:, 8192:16384].rearrange("p (c t) -> p c t", c=DC),
        w_out_t.rearrange("(c p) t -> p c t", c=DC), writes=[("wbA", 1)])
    dma("gpsimd", wr_b[:].rearrange("p (c e) -> p c e", c=DC),
        wr_d.rearrange("(c p) e -> p c e", c=DC), writes=["wr_b"])

    op("tensor", lambda h: h.transpose(pC[:, 0:DC], L8[:], iden[0:DC, 0:DC]),
       reads=["L8", "iden"], writes=[("pC", 0)])
    op("scalar", lambda h: h.copy(LCol[:], pC[:, 0:DC]),
       reads=[("pC", 0)], writes=["LCol"])
    op("scalar", lambda h: h.activation(lamC64[:], LCol[:], AF.Exp, scale=64.0),
       reads=["LCol"], writes=["lamC64"])
    dma("sync", scr[0:DC, 1024:1152],
        b_out_d.rearrange("o (a p) -> (o a) p", a=DC), writes=[("scr", 1)])
    op("tensor", lambda h: h.transpose(pC[:, 16:24], scr[0:DC, 1024:1152],
                                       iden[0:DC, 0:DC]),
       reads=[("scr", 1), "iden"], writes=[("pC", 0)])
    op("scalar", lambda h: h.copy(boCol[:], pC[:, 16:24]),
       reads=[("pC", 0)], writes=["boCol"])
    for half in range(2):
        dma("sync", scr[0:E, 1024:2048], b1_d[:, half * 1024:(half + 1) * 1024],
            writes=[("scr", 1)])
        for hh in range(8):
            ht = half * 8 + hh
            op("tensor", lambda h, ht=ht, hh=hh: h.transpose(
                pC[:, 512 + ht * 8: 512 + (ht + 1) * 8],
                scr[0:E, 1024 + hh * 128:1024 + (hh + 1) * 128], iden[0:E, 0:E]),
                reads=[("scr", 1), "iden"], writes=[("pC", 1)], inc=(hh == 7))
    op("scalar", lambda h: h.copy(
        b1Col[:].rearrange("p (e t) -> p t e", e=E),
        pC[:, 512:640].rearrange("p (t e) -> p t e", t=HT)),
        reads=[("pC", 1)], writes=["b1Col"])
    dma("sync", b1cols.rearrange("(e p) t -> p e t", e=E),
        b1Col[:].rearrange("p (e t) -> p e t", e=E),
        reads=["b1Col"], writes=["b1cols"])
    dma("sync", scr[0:E, 1024:2048], b2_d, writes=[("scr", 1)])
    for dt in range(DC):
        op("tensor", lambda h, dt=dt: h.transpose(
            pC[:, 512 + dt * 8:512 + (dt + 1) * 8],
            scr[0:E, 1024 + dt * 128:1024 + (dt + 1) * 128], iden[0:E, 0:E]),
            reads=[("scr", 1), "iden"], writes=[("pC", 1)], inc=(dt == DC - 1))
    op("scalar", lambda h: h.copy(
        b2ColE[:].rearrange("p (e t) -> p t e", e=E),
        pC[:, 512:576].rearrange("p (t e) -> p t e", t=DC)),
        reads=[("pC", 1)], writes=["b2ColE"])
    dma("sync", b2cols.rearrange("(e p) t -> p e t", e=E),
        b2ColE[:].rearrange("p (e t) -> p e t", e=E),
        reads=["b2ColE"], writes=["b2cols"])

    dma("sync", rowbuf[:, 0:1024], lam_log, writes=["rowbuf"])
    op("tensor", lambda h: h.matmul(pA[:, 0, :], ones1[:], rowbuf[:, 0:512],
                                    start=True, stop=True),
       reads=["ones1", "rowbuf"], writes=[("pA", 0)])
    op("tensor", lambda h: h.matmul(pA[:, 1, :], ones1[:], rowbuf[:, 512:1024],
                                    start=True, stop=True),
       reads=["ones1", "rowbuf"], writes=[("pA", 1)])
    op("vector", lambda h: h.tensor_scalar(
        scr[:, 1024:2048], pA[:].rearrange("p a b -> p (a b)"), negsidx[:],
        None, ALU.mult),
        reads=[("pA", 0), ("pA", 1), "negsidx"], writes=[("scr", 1)])
    op("scalar", lambda h: h.activation(lamNegPow[:], scr[:, 1024:2048], AF.Exp),
       reads=[("scr", 1)], writes=["lamNegPow"])
    dma("sync", rowbuf[:, 0:1024], gamma_d, writes=["rowbuf"])
    dma("sync", scr[0:1, 1024:2048], b_in_d, writes=[("scr", 1)])
    op("tensor", lambda h: h.matmul(pD[:, 0, :], ones1[:], rowbuf[:, 0:512],
                                    start=True, stop=True),
       reads=["ones1", "rowbuf"], writes=[("pD", 0)])
    op("tensor", lambda h: h.matmul(pD[:, 1, :], ones1[:], rowbuf[:, 512:1024],
                                    start=True, stop=True),
       reads=["ones1", "rowbuf"], writes=[("pD", 1)])
    op("vector", lambda h: h.tensor_tensor(
        rowbuf[:, 0:1024], rowbuf[:, 0:1024], scr[0:1, 1024:2048], ALU.mult),
        reads=["rowbuf", ("scr", 1)], writes=["rowbuf"])
    op("tensor", lambda h: h.matmul(pB[:, 0, :], ones1[:], rowbuf[:, 0:512],
                                    start=True, stop=True),
       reads=["ones1", "rowbuf"], writes=[("pB", 0)])
    op("tensor", lambda h: h.matmul(pB[:, 1, :], ones1[:], rowbuf[:, 512:1024],
                                    start=True, stop=True),
       reads=["ones1", "rowbuf"], writes=[("pB", 1)])
    op("vector", lambda h: h.tensor_tensor(
        gbLam[:], pB[:].rearrange("p a b -> p (a b)"), lamNegPow[:], ALU.mult),
        reads=[("pB", 0), ("pB", 1), "lamNegPow"], writes=["gbLam"])
    op("vector", lambda h: h.tensor_tensor(
        lamNegPow[:], lamNegPow[:], pD[:].rearrange("p a b -> p (a b)"), ALU.mult),
        reads=["lamNegPow", ("pD", 0), ("pD", 1)], writes=["lamNegPow"])
    op("tensor", lambda h: h.matmul(pC[:, 0:128], ones1[:], tix[:],
                                    start=True, stop=True),
       reads=["ones1", "tix"], writes=[("pC", 0)])
    for k in range(DC):
        op("vector", lambda h, k=k: h.tensor_scalar(
            scr[:, 1024 + k * 128:1024 + (k + 1) * 128], pC[:, 0:128],
            LCol[:, k:k + 1], None, ALU.mult),
            reads=[("pC", 0), "LCol"], writes=[("scr", 1)])
    op("scalar", lambda h: h.activation(
        lamPow[:].rearrange("p a b -> p (a b)"), scr[:, 1024:2048], AF.Exp),
        reads=[("scr", 1)], writes=["lamPow"])
    conv_emits = []
    for e in range(E):
        conv_emits.append(lambda e=e: op("gpsimd", lambda h, e=e: h.dma_start(
            out=w1s[e * 128:(e + 1) * 128, :].rearrange("p (c h) -> p c h", c=DC),
            in_=w1_d[e].rearrange("(c p) h -> p c h", c=DC)),
            writes=[("w1s", e)], sem="gd3", inc_k=16))
        for half in range(2):
            conv_emits.append(lambda e=e, half=half: op(
                "gpsimd", lambda h, e=e, half=half: h.dma_start(
                    out=w2s[e * 128:(e + 1) * 128,
                            half * 8192:(half + 1) * 8192].rearrange("p (c t) -> p c t", c=8),
                    in_=w2_d[e].rearrange("(c p) t -> p c t", c=HT)[:, half * 8:(half + 1) * 8, :]),
                writes=[("w2s", e)], sem="gd3", inc_k=16))

    def emit_ln_p1ab(par):
        for dc in range(DC):
            op("scalar", lambda h, dc=dc, par=par: h.copy(
                xb[:, dc % 2, :], bigX[:, par, dc, :]),
                reads=[BX(par, dc)], writes=[("xb", dc % 2)])
            op("vector", lambda h, dc=dc: h.tensor_tensor(
                xsqb[:, dc % 2, :], xb[:, dc % 2, :], xb[:, dc % 2, :], ALU.mult),
                reads=[("xb", dc % 2)], writes=[("xsqb", dc % 2)])
            op("tensor", lambda h, dc=dc: h.matmul(
                pA[0:1, 0, :], onescol_b[:], xb[:, dc % 2, :],
                start=(dc == 0), stop=(dc == DC - 1)),
                reads=["onescol_b", ("xb", dc % 2)], writes=[("pA", 0)],
                inc=True)
            op("tensor", lambda h, dc=dc: h.matmul(
                pA[0:1, 1, :], onescol_b[:], xsqb[:, dc % 2, :],
                start=(dc == 0), stop=(dc == DC - 1)),
                reads=["onescol_b", ("xsqb", dc % 2)], writes=[("pA", 1)],
                inc=True)
        op("vector", lambda h: h.tensor_scalar(
            rowbuf[:, 0:T2], pA[0:1, 0, :], 1.0 / D, None, ALU.mult),
            reads=[("pA", 0)], writes=["rowbuf"])
        op("vector", lambda h: h.tensor_scalar(
            rowbuf[:, T2:2 * T2], pA[0:1, 1, :], 1.0 / D, 1e-5, ALU.mult, ALU.add),
            reads=[("pA", 1)], writes=["rowbuf"])
        op("vector", lambda h: h.tensor_tensor(
            rowbuf[:, 2 * T2:3 * T2], rowbuf[:, 0:T2], rowbuf[:, 0:T2], ALU.mult),
            reads=["rowbuf"], writes=["rowbuf"])
        op("vector", lambda h: h.tensor_tensor(
            rowbuf[:, T2:2 * T2], rowbuf[:, T2:2 * T2], rowbuf[:, 2 * T2:3 * T2],
            ALU.subtract),
            reads=["rowbuf"], writes=["rowbuf"])
        op("scalar", lambda h: h.activation(
            rowbuf[:, T2:2 * T2], rowbuf[:, T2:2 * T2], AF.Ln),
            reads=["rowbuf"], writes=["rowbuf"])
        op("scalar", lambda h: h.activation(
            rowbuf[:, 2 * T2:3 * T2], rowbuf[:, T2:2 * T2], AF.Exp, scale=-0.5),
            reads=["rowbuf"], writes=["rowbuf"])
        op("vector", lambda h: h.tensor_tensor(
            rowbuf[:, T2:2 * T2], rowbuf[:, 0:T2], rowbuf[:, 2 * T2:3 * T2],
            ALU.mult),
            reads=["rowbuf"], writes=["rowbuf"])

    def emit_ln_p1c():
        op("tensor", lambda h: h.matmul(
            pC[:, 0:512], ones1[:], rowbuf[:, 2 * T2:3 * T2], start=True, stop=True),
            reads=["ones1", "rowbuf"], writes=[("pC", 0)])
        op("tensor", lambda h: h.matmul(
            pC[:, 512:1024], ones1[:], rowbuf[:, T2:2 * T2], start=True, stop=True),
            reads=["ones1", "rowbuf"], writes=[("pC", 1)])

    def emit_ln_p2(par):
        rbc = pC[:, 0:512].unsqueeze(1).broadcast_to([128, DC, 512])
        mbc = pC[:, 512:1024].unsqueeze(1).broadcast_to([128, DC, 512])
        op("vector", lambda h, par=par, rbc=rbc: h.tensor_tensor(
            lnx[:, par], bigX[:, par], rbc, ALU.mult),
            reads=[("pC", 0)] + [BX(par, dc) for dc in range(DC)],
            writes=[("lnx", par, dc) for dc in range(DC)])
        op("vector", lambda h, par=par, mbc=mbc: h.tensor_tensor(
            lnx[:, par], lnx[:, par], mbc, ALU.subtract),
            reads=[("pC", 1)] + [("lnx", par, dc) for dc in range(DC)],
            writes=[("lnx", par, dc) for dc in range(DC)])

    def tail_a1(j, q):
        par = j % 2
        cidx = j * 4 + q
        qp = q % 2
        for g in range(2):
            for dd in range(4):
                dc = g * 4 + dd
                op("tensor", lambda h, dc=dc, dd=dd, q=q, par=par: h.transpose(
                    pD[:, 0, dd * 128:(dd + 1) * 128],
                    x1d[par][:, dc, q * 128:(q + 1) * 128], iden[:]),
                    reads=[X1K(par, dc), "iden"], writes=[("pD", 0)],
                    inc=(dd == 3))
            op("scalar", lambda h, g=g, qp=qp: h.copy(
                x1row[:, qp, g * 512:(g + 1) * 512], pD[:, 0, :]),
                reads=[("pD", 0)], writes=[("x1row", qp)])
        dma("sync", x1tok[cidx * 128:(cidx + 1) * 128, :], x1row[:, qp, :],
            reads=[("x1row", qp)], writes=[("x1tok",)])
        op("scalar", lambda h, qp=qp: h.activation(
            lnTg[:, 0, :], x1row[:, qp, :], AF.Identity,
            accum_out=rsc[:, 6 + qp:7 + qp]),
            reads=[("x1row", qp)], writes=[("lnTg", 0), ("rsc", 6 + qp)])
        op("scalar", lambda h, qp=qp: h.activation(
            lnTg[:, 1, :], x1row[:, qp, :], AF.Square,
            accum_out=rsc[:, 8 + qp:9 + qp]),
            reads=[("x1row", qp)], writes=[("lnTg", 1), ("rsc", 8 + qp)])

    def tail_a2(j, q):
        qp = q % 2
        op("vector", lambda h, qp=qp: h.tensor_scalar(
            rsc[:, 10 + qp:11 + qp], rsc[:, 6 + qp:7 + qp], 1.0 / D, None,
            ALU.mult),
            reads=[("rsc", 6 + qp)], writes=[("rsc", 10 + qp)])
        op("vector", lambda h, qp=qp: h.tensor_scalar(
            rsc[:, 12 + qp:13 + qp], rsc[:, 8 + qp:9 + qp], 1.0 / D, 1e-5,
            ALU.mult, ALU.add),
            reads=[("rsc", 8 + qp)], writes=[("rsc", 12 + qp)])
        op("vector", lambda h, qp=qp: h.tensor_tensor(
            rsc[:, 8 + qp:9 + qp], rsc[:, 10 + qp:11 + qp],
            rsc[:, 10 + qp:11 + qp], ALU.mult),
            reads=[("rsc", 10 + qp)], writes=[("rsc", 8 + qp)])
        op("vector", lambda h, qp=qp: h.tensor_tensor(
            rsc[:, 12 + qp:13 + qp], rsc[:, 12 + qp:13 + qp],
            rsc[:, 8 + qp:9 + qp], ALU.subtract),
            reads=[("rsc", 8 + qp), ("rsc", 12 + qp)], writes=[("rsc", 12 + qp)])
        op("scalar", lambda h, qp=qp: h.activation(
            rsc[:, 12 + qp:13 + qp], rsc[:, 12 + qp:13 + qp], AF.Ln),
            reads=[("rsc", 12 + qp)], writes=[("rsc", 12 + qp)])
        op("scalar", lambda h, qp=qp: h.activation(
            rsc[:, 12 + qp:13 + qp], rsc[:, 12 + qp:13 + qp], AF.Exp,
            scale=-0.5),
            reads=[("rsc", 12 + qp)], writes=[("rsc", 12 + qp)])
        op("vector", lambda h, qp=qp: h.tensor_tensor(
            rsc[:, 10 + qp:11 + qp], rsc[:, 10 + qp:11 + qp],
            rsc[:, 12 + qp:13 + qp], ALU.mult),
            reads=[("rsc", 12 + qp), ("rsc", 10 + qp)], writes=[("rsc", 10 + qp)])
        op("vector", lambda h, qp=qp: h.tensor_scalar(
            rsc[:, 10 + qp:11 + qp], rsc[:, 10 + qp:11 + qp], -1.0, None,
            ALU.mult),
            reads=[("rsc", 10 + qp)], writes=[("rsc", 10 + qp)])
        op("scalar", lambda h, qp=qp: h.activation(
            ln2row[:, qp, :], x1row[:, qp, :], AF.Identity,
            bias=rsc[:, 10 + qp:11 + qp], scale=rsc[:, 12 + qp:13 + qp]),
            reads=[("x1row", qp), ("rsc", 10 + qp), ("rsc", 12 + qp)],
            writes=[("ln2row", qp)])

    def tail_b1(j, q):
        for g in range(2):
            for dd in range(4):
                dc = g * 4 + dd
                op("tensor", lambda h, dc=dc, dd=dd, qp=q % 2: h.transpose(
                    pD[0:128, 1, 256 + dd * 64:256 + (dd + 1) * 64].bitcast(BF16),
                    ln2row[:, qp, dc * 128:(dc + 1) * 128], idenb[:]),
                    reads=[("ln2row", q % 2), "idenb"], writes=[("pD", 1)],
                    inc=(dd == 3))
            op("scalar", lambda h, g=g: h.copy(
                ln2d[:, g * 512:(g + 1) * 512].rearrange("p (a b) -> p a b", a=4),
                pD[0:128, 1, 256:512].bitcast(BF16).rearrange("p (a b) -> p a b", a=4)),
                reads=[("pD", 1)], writes=["ln2d"])

    def tail_b2(j, q):
        for dc in range(DC):
            op("tensor", lambda h, dc=dc: h.matmul(
                pD[0:E, 1, 0:128],
                wr_b[:, dc * E:(dc + 1) * E],
                ln2d[:, dc * 128:(dc + 1) * 128],
                start=(dc == 0), stop=(dc == DC - 1)),
                reads=["wr_b", "ln2d"], writes=[("pD", 1)],
                inc=(dc == DC - 1))
        op("scalar", lambda h: h.activation(
            logits_sb[:], pD[0:E, 1, 0:128],
            AF.Identity, bias=br_col[:]),
            reads=[("pD", 1), "br_col"], writes=["logits_sb"])
        op("tensor", lambda h: h.transpose(
            pD[:, 1, 128:136],
            logits_sb[:], iden[0:E, 0:E]),
            reads=["logits_sb", "iden"], writes=[("pD", 1)])

    def tail_b3a(j, q):
        cidx = j * 4 + q
        op("vector", lambda h: h.tensor_reduce(
            rsc[:, 0:1], pD[:, 1, 128:136], AX.X, ALU.max, negate=True),
            reads=[("pD", 1)], writes=[("rsc", 0)])
        op("scalar", lambda h: h.activation(
            lnTg[:, 2, 0:8], pD[:, 1, 128:136], AF.Exp,
            bias=rsc[:, 0:1], accum_out=rsc[:, 1:2]),
            reads=[("pD", 1), ("rsc", 0)], writes=[("lnTg", 2), ("rsc", 1)])
        op("vector", lambda h: h.reciprocal(rsc[:, 2:3], rsc[:, 1:2]),
           reads=[("rsc", 1)], writes=[("rsc", 2)])
        op("vector", lambda h: h.tensor_scalar(
            rsc2[:, 0, :], pD[:, 1, 128:136],
            rsc[:, 0:1], 0.0, ALU.add, ALU.is_ge),
            reads=[("pD", 1), ("rsc", 0)], writes=[("rsc2", 0)])
        op("vector", lambda h: h.tensor_scalar(
            rsc2[:, 1, :], rsc2[:, 0, :], -1e9, 1e9, ALU.mult, ALU.add),
            reads=[("rsc2", 0)], writes=[("rsc2", 1)])
        op("vector", lambda h: h.tensor_tensor(
            rsc2[:, 1, :], rsc2[:, 1, :], iotaE[:], ALU.add),
            reads=[("rsc2", 1), "iotaE"], writes=[("rsc2", 1)])
        op("vector", lambda h: h.tensor_reduce(
            rsc[:, 3:4], rsc2[:, 1, :], AX.X, ALU.min),
            reads=[("rsc2", 1)], writes=[("rsc", 3)])
        op("vector", lambda h: h.tensor_scalar(
            rsc2[:, 2, :], iotaE[:], rsc[:, 3:4], None, ALU.is_equal),
            reads=["iotaE", ("rsc", 3)], writes=[("rsc2", 2)])

    def tail_b3b(j, q):
        cidx = j * 4 + q
        op("tensor", lambda h: h.matmul(
            pD[0:128, 1, 136:144], ones1[:], rc8[:], start=True, stop=False),
            reads=["ones1", "rc8"], writes=[("pD", 1)])
        op("tensor", lambda h: h.matmul(
            pD[0:128, 1, 136:144], tri_f[:], rsc2[:, 2, :], start=False, stop=True),
            reads=["tri_f", ("rsc2", 2)], writes=[("pD", 1)])
        op("tensor", lambda h: h.matmul(
            pD[0:1, 1, 144:152], onescol[:], rsc2[:, 2, :], start=True, stop=True),
            reads=["onescol", ("rsc2", 2)], writes=[("pD", 1)])
        op("vector", lambda h: h.tensor_tensor(
            rsc2[:, 3, :], rsc2[:, 2, :], pD[0:128, 1, 136:144], ALU.mult),
            reads=[("rsc2", 2), ("pD", 1)], writes=[("rsc2", 3)])
        op("vector", lambda h: h.tensor_reduce(
            rsc[:, 4:5], rsc2[:, 3, :], AX.X, ALU.add),
            reads=[("rsc2", 3)], writes=[("rsc", 4)])
        op("vector", lambda h: h.tensor_scalar(
            rsc[:, 5:6], rsc[:, 4:5], 1.0, -1.0, ALU.mult, ALU.add),
            reads=[("rsc", 4)], writes=[("rsc", 5)])
        op("vector", lambda h, cidx=cidx: h.tensor_scalar(
            pos_sb[:, cidx:cidx + 1], rsc[:, 3:4], float(T), rsc[:, 5:6],
            ALU.mult, ALU.add),
            reads=[("rsc", 3), ("rsc", 5)], writes=["pos_sb"])
        op("vector", lambda h: h.tensor_tensor(
            rc8[:], rc8[:], pD[0:1, 1, 144:152], ALU.add),
            reads=["rc8", ("pD", 1)], writes=["rc8"])
        op("vector", lambda h, cidx=cidx: h.tensor_copy(
            posU[:, 0:1], pos_sb[:, cidx:cidx + 1]),
            reads=["pos_sb"], writes=[("posU2", 0)])
        op("vector", lambda h, cidx=cidx: h.tensor_scalar(
            aux_sb[:, 0:1], iotaQ[:, 0:1], 1.0, float(cidx * 128),
            ALU.mult, ALU.add),
            reads=["iotaQ"], writes=["aux_sb"])
        op("vector", lambda h: h.tensor_copy(aux_sb[:, 1:2], rsc[:, 2:3]),
           reads=[("rsc", 2)], writes=["aux_sb"])
        dma("gpsimd", slotaux, aux_sb[:],
            reads=["aux_sb", ("posU2", 0)], writes=[("slotaux",)],
            indirect=("scatter", posU[:, 0:1]))
        dma("gpsimd", slotbuf, ln2row[:, q % 2, :],
            reads=[("ln2row", q % 2), ("posU2", 0)], writes=[("slotbuf",)],
            indirect=("scatter", posU[:, 0:1]))

    tail_stream = []

    def push_tail(j):
        def bchain(q):
            return [lambda q=q: tail_b1(j, q), lambda q=q: tail_b2(j, q),
                    lambda q=q: tail_b3a(j, q), lambda q=q: tail_b3b(j, q)]
        for q in range(4):
            tail_stream.append(lambda q=q: tail_a1(j, q))
            tail_stream.append(lambda q=q: tail_a2(j, q))
            if q >= 1:
                tail_stream.extend(bchain(q - 1))
        tail_stream.extend(bchain(3))

    def flush_tail(n):
        for _ in range(min(n, len(tail_stream))):
            tail_stream.pop(0)()

    emit_ln_p1ab(0)
    emit_ln_p1c()
    emit_ln_p2(0)
    for j in range(NT2):
        par = j % 2
        seq_t2 = j % T2_PER_SEQ
        if j > 0:
            push_tail(j - 1)
        for q in range(T2 // CH):
            qpar = q % 2
            for half in range(2):
                for dc in range(DC):
                    op("tensor", lambda h, q=q, half=half, dc=dc, par=par: h.matmul(
                        pA[:, half, :],
                        lnx[:, par, dc, q * 128:(q + 1) * 128],
                        wbA[:, dc * 1024 + half * 512: dc * 1024 + (half + 1) * 512],
                        start=(dc == 0), stop=(dc == DC - 1)),
                        reads=[("lnx", par, dc), ("wbA", 0)], writes=[("pA", half)],
                        inc=(dc == DC - 1))
                op("vector", lambda h, half=half: h.tensor_tensor(
                    xsq[:, half, :],
                    pA[:, half, :], lamNegPow[:, half * 512:(half + 1) * 512],
                    ALU.mult),
                    reads=[("pA", half), "lamNegPow"], writes=[("xsq", half)])
            op("vector", lambda h, qpar=qpar: h.tensor_tensor(
                gub[:, qpar, :], xsq[:].rearrange("p a b -> p (a b)"),
                gbLam[:], ALU.add),
                reads=[("xsq", 0), ("xsq", 1), "gbLam"], writes=[("gub", qpar)])
            for k in range(DC):
                op("tensor", lambda h, k=k, qpar=qpar: h.matmul(
                    pC[:, k * 128:(k + 1) * 128],
                    gub[:, qpar, k * 128:(k + 1) * 128], tri_b[:],
                    start=True, stop=True),
                    reads=[("gub", qpar), "tri_b"], writes=[("pC", k // 4)])
            first = (seq_t2 == 0 and q == 0)
            hsel = hsb[:].rearrange("p a (n b) -> p a n b", b=CH)
            if not first:
                hinit_bc = hinit[:].unsqueeze(2).broadcast_to([128, DC, CH])
                op("vector", lambda h, hinit_bc=hinit_bc: h.tensor_tensor(
                    scr[:, 0:1024].rearrange("p (a b) -> p a b", a=DC),
                    pC[:].rearrange("p (a b) -> p a b", a=DC), hinit_bc, ALU.add),
                    reads=[("pC", 0), ("pC", 1), "hinit"], writes=[("scr", 0)])
                op("vector", lambda h, q=q, hsel=hsel: h.tensor_tensor(
                    hsel[:, :, q, :],
                    scr[:, 0:1024].rearrange("p (a b) -> p a b", a=DC),
                    lamPow[:], ALU.mult),
                    reads=[("scr", 0), "lamPow"], writes=["hsb"])
            else:
                op("vector", lambda h, q=q, hsel=hsel: h.tensor_tensor(
                    hsel[:, :, q, :],
                    pC[:].rearrange("p (a b) -> p a b", a=DC),
                    lamPow[:], ALU.mult),
                    reads=[("pC", 0), ("pC", 1), "lamPow"], writes=["hsb"])
            op("vector", lambda h, q=q: h.tensor_tensor(
                hinit[:],
                hsb[:, :, q * 128 + 127:q * 128 + 128].squeeze(2),
                lamC64[:], ALU.mult),
                reads=["hsb", "lamC64"], writes=["hinit"])
            flush_tail(4)
        for k in range(DC):
            if k == 1 and j + 1 < NT2:
                emit_ln_p1ab(1 - par)
            if k == 2:
                flush_tail(4)
            if k == 4:
                flush_tail(4)
                if j + 1 < NT2:
                    emit_ln_p1c()
            if k == 5 and j + 1 < NT2:
                emit_ln_p2(1 - par)
            for dc in range(DC):
                op("tensor", lambda h, k=k, dc=dc: h.matmul(
                    pB[:, k % 2, :],
                    wbA[:, 8192 + dc * 1024 + k * 128: 8192 + dc * 1024 + (k + 1) * 128],
                    hsb[:, dc, :], start=(dc == 0), stop=(dc == DC - 1)),
                    reads=[("wbA", 1), "hsb"], writes=[("pB", k % 2)],
                    inc=(dc == DC - 1))
            op("scalar", lambda h, k=k, par=par: h.activation(
                x1d[par][:, k, :], pB[:, k % 2, :],
                AF.Identity, bias=boCol[:, k:k + 1]),
                reads=[("pB", k % 2), "boCol"], writes=[X1K(par, k)])
            if k % 2 == 1:
                op("vector", lambda h, k=k, par=par: h.tensor_tensor(
                    x1d[par][:, k - 1:k + 1, :], x1d[par][:, k - 1:k + 1, :],
                    lnx[:, par, k - 1:k + 1, :], ALU.add),
                    reads=[X1K(par, k - 1), X1K(par, k),
                           ("lnx", par, k - 1), ("lnx", par, k)],
                    writes=[X1K(par, k - 1), X1K(par, k)])
                op("vector", lambda h, k=k, par=par: h.tensor_tensor(
                    x1d[par][:, k - 1:k + 1, :], x1d[par][:, k - 1:k + 1, :],
                    bigX[:, par, k - 1:k + 1, :], ALU.add),
                    reads=[X1K(par, k - 1), X1K(par, k),
                           BX(par, k - 1), BX(par, k)],
                    writes=[X1K(par, k - 1), X1K(par, k)])
        if j + 2 < NT2:
            load_x(j + 2, par)
        if j >= 2:
            for _ in range(2):
                if conv_emits:
                    conv_emits.pop(0)()

    while conv_emits:
        conv_emits.pop(0)()
    _conv_total = cnt["gd3"]
    for e in range(E):
        regions[("w1s", e)]["w"] = ("gd3", _conv_total)
        regions[("w2s", e)]["w"] = ("gd3", _conv_total)

    push_tail(NT2 - 1)
    flush_tail(10 ** 9)

    op("vector", lambda h: h.tensor_tensor(
        grow[:, 0:128].rearrange("o (e k) -> o e k", e=E),
        rc8[:].unsqueeze(2).broadcast_to([1, E, 16]),
        thresh16[:].rearrange("o (e k) -> o e k", e=E), ALU.is_gt),
        reads=["rc8", "thresh16"], writes=["grow"])
    op("vector", lambda h: h.tensor_reduce(
        finrow[:, 0:E], grow[:, 0:128].rearrange("o (e k) -> o e k", e=E),
        AX.X, ALU.add),
        reads=["grow"], writes=["finrow"])
    op("vector", lambda h: h.tensor_scalar(
        finrow[:, 0:E], finrow[:, 0:E], float(T2), None, ALU.mult),
        reads=["finrow"], writes=["finrow"])
    op("vector", lambda h: h.tensor_scalar(
        startrow[:, 0:1], finrow[:, 0:1], 0.0, None, ALU.mult),
        reads=["finrow"], writes=["startrow"])
    for e in range(E):
        op("vector", lambda h, e=e: h.tensor_tensor(
            startrow[:, e + 1:e + 2], startrow[:, e:e + 1],
            finrow[:, e:e + 1], ALU.add),
            reads=["startrow", "finrow"], writes=["startrow"])
    op("vector", lambda h: h.tensor_tensor(
        grow[:, 0:NTILES * E].rearrange("o (t e) -> o t e", e=E),
        tau24[:].rearrange("o (t e) -> o t e", e=E),
        startrow[:, 0:E].unsqueeze(1).broadcast_to([1, NTILES, E]), ALU.is_ge),
        reads=["tau24", "startrow"], writes=["grow"])
    op("vector", lambda h: h.tensor_reduce(
        finrow[:, NTILES:2 * NTILES], grow[:, 0:NTILES * E].rearrange("o (t e) -> o t e", e=E),
        AX.X, ALU.add),
        reads=["grow"], writes=["finrow"])
    op("vector", lambda h: h.tensor_scalar(
        finrow[:, NTILES:2 * NTILES], finrow[:, NTILES:2 * NTILES], 1.0, -1.0,
        ALU.mult, ALU.add),
        reads=["finrow"], writes=["finrow"])
    op("vector", lambda h: h.tensor_tensor(
        grow[:, 0:NTILES * E].rearrange("o (t e) -> o t e", e=E),
        iotaE24[:].rearrange("o (t e) -> o t e", e=E),
        finrow[:, NTILES:2 * NTILES].unsqueeze(2).broadcast_to([1, NTILES, E]),
        ALU.is_equal),
        reads=["iotaE24", "finrow"], writes=["grow"])
    op("vector", lambda h: h.tensor_tensor(
        grow[:, 0:NTILES * E].rearrange("o (t e) -> o t e", e=E),
        grow[:, 0:NTILES * E].rearrange("o (t e) -> o t e", e=E),
        startrow[:, 0:E].unsqueeze(1).broadcast_to([1, NTILES, E]), ALU.mult),
        reads=["grow", "startrow"], writes=["grow"])
    op("vector", lambda h: h.tensor_reduce(
        finrow[:, 2 * NTILES:3 * NTILES],
        grow[:, 0:NTILES * E].rearrange("o (t e) -> o t e", e=E), AX.X, ALU.add),
        reads=["grow"], writes=["finrow"])
    op("vector", lambda h: h.tensor_tensor(
        grow[:, 0:NTILES * E].rearrange("o (t e) -> o t e", e=E),
        iotaE24[:].rearrange("o (t e) -> o t e", e=E),
        finrow[:, NTILES:2 * NTILES].unsqueeze(2).broadcast_to([1, NTILES, E]),
        ALU.is_equal),
        reads=["iotaE24", "finrow"], writes=["grow"])
    op("vector", lambda h: h.tensor_tensor(
        grow[:, 0:NTILES * E].rearrange("o (t e) -> o t e", e=E),
        grow[:, 0:NTILES * E].rearrange("o (t e) -> o t e", e=E),
        rc8[:].unsqueeze(1).broadcast_to([1, NTILES, E]), ALU.mult),
        reads=["grow", "rc8"], writes=["grow"])
    op("vector", lambda h: h.tensor_reduce(
        finrow[:, 0:NTILES],
        grow[:, 0:NTILES * E].rearrange("o (t e) -> o t e", e=E), AX.X, ALU.add),
        reads=["grow"], writes=["finrow"])
    op("vector", lambda h: h.tensor_scalar(
        grow[:, 0:NTILES], finrow[:, NTILES:2 * NTILES], float(T), None, ALU.mult),
        reads=["finrow"], writes=["grow"])
    op("vector", lambda h: h.tensor_tensor(
        finrow[:, 0:NTILES], finrow[:, 0:NTILES], grow[:, 0:NTILES], ALU.add),
        reads=["finrow", "grow"], writes=["finrow"])
    op("vector", lambda h: h.tensor_scalar(
        finrow[:, 3 * NTILES:4 * NTILES], finrow[:, NTILES:2 * NTILES],
        float(T), None, ALU.mult),
        reads=["finrow"], writes=["finrow"])
    op("vector", lambda h: h.tensor_tensor(
        finrow[:, 3 * NTILES:4 * NTILES], finrow[:, 3 * NTILES:4 * NTILES],
        tau24[:].rearrange("o (t e) -> o t e", e=E)[:, :, 0], ALU.add),
        reads=["finrow", "tau24"], writes=["finrow"])
    op("vector", lambda h: h.tensor_tensor(
        finrow[:, 3 * NTILES:4 * NTILES], finrow[:, 3 * NTILES:4 * NTILES],
        finrow[:, 2 * NTILES:3 * NTILES], ALU.subtract),
        reads=["finrow"], writes=["finrow"])

    def tile_front(tau):
        tp = tau % 2
        op("tensor", lambda h, tau=tau: h.matmul(
            pC[:, 0:1], ones1[:], finrow[:, NTILES + tau:NTILES + tau + 1],
            start=True, stop=True),
            reads=["ones1", "finrow"], writes=[("pC", 0)])
        op("tensor", lambda h, tau=tau: h.matmul(
            pC[:, 1:2], ones1[:], finrow[:, 3 * NTILES + tau:3 * NTILES + tau + 1],
            start=True, stop=True),
            reads=["ones1", "finrow"], writes=[("pC", 0)])
        op("tensor", lambda h, tau=tau: h.matmul(
            pC[:, 2:3], ones1[:], finrow[:, tau:tau + 1],
            start=True, stop=True),
            reads=["ones1", "finrow"], writes=[("pC", 0)])
        op("vector", lambda h: h.tensor_scalar(
            idxs_f[:, tp, 0:1], pC[:, 0:1], 128.0, iotaQ[:, 0:1], ALU.mult, ALU.add),
            reads=[("pC", 0), "iotaQ"], writes=[("idxs_f", tp)])
        for q in range(4):
            op("vector", lambda h, q=q: h.tensor_scalar(
                idxs_f[:, tp, 9:10], pC[:, 1:2], 1.0, iotaQ[:, q:q + 1],
                ALU.mult, ALU.add),
                reads=[("pC", 0), "iotaQ"], writes=[("idxs_f", tp)])
            op("vector", lambda h, q=q: h.tensor_scalar(
                idxs_f[:, tp, 10:11], idxs_f[:, tp, 9:10], float(E * T), None,
                ALU.is_lt),
                reads=[("idxs_f", tp)], writes=[("idxs_f", tp)])
            op("vector", lambda h, q=q: h.tensor_tensor(
                idxs_f[:, tp, 1 + q:2 + q], idxs_f[:, tp, 9:10], idxs_f[:, tp, 10:11],
                ALU.mult),
                reads=[("idxs_f", tp)], writes=[("idxs_f", tp)])
            op("vector", lambda h, q=q: h.tensor_scalar(
                idxs_f[:, tp, 11:12], idxs_f[:, tp, 10:11], -1.0, 1.0, ALU.mult, ALU.add),
                reads=[("idxs_f", tp)], writes=[("idxs_f", tp)])
            op("vector", lambda h, q=q, tau=tau: h.tensor_scalar(
                idxs_f[:, tp, 10:11], iotaQ[:, q:q + 1], 1.0,
                float(E * T + tau * T2), ALU.mult, ALU.add),
                reads=["iotaQ"], writes=[("idxs_f", tp)])
            op("vector", lambda h, q=q: h.tensor_tensor(
                idxs_f[:, tp, 11:12], idxs_f[:, tp, 11:12], idxs_f[:, tp, 10:11], ALU.mult),
                reads=[("idxs_f", tp)], writes=[("idxs_f", tp)])
            op("vector", lambda h, q=q: h.tensor_tensor(
                idxs_f[:, tp, 1 + q:2 + q], idxs_f[:, tp, 1 + q:2 + q], idxs_f[:, tp, 11:12],
                ALU.add),
                reads=[("idxs_f", tp)], writes=[("idxs_f", tp)])
            op("vector", lambda h, q=q: h.tensor_scalar(
                idxs_f[:, tp, 10:11], idxs_f[:, tp, 9:10], pC[:, 2:3], None, ALU.is_lt),
                reads=[("idxs_f", tp), ("pC", 0)], writes=[("idxs_f", tp)])
            op("vector", lambda h, q=q: h.tensor_tensor(
                idxs_f[:, tp, 5 + q:6 + q], idxs_f[:, tp, 9:10], idxs_f[:, tp, 10:11],
                ALU.mult),
                reads=[("idxs_f", tp)], writes=[("idxs_f", tp)])
            op("vector", lambda h, q=q: h.tensor_scalar(
                idxs_f[:, tp, 11:12], idxs_f[:, tp, 10:11], float(-ZROW), float(ZROW),
                ALU.mult, ALU.add),
                reads=[("idxs_f", tp)], writes=[("idxs_f", tp)])
            op("vector", lambda h, q=q: h.tensor_tensor(
                idxs_f[:, tp, 5 + q:6 + q], idxs_f[:, tp, 5 + q:6 + q], idxs_f[:, tp, 11:12],
                ALU.add),
                reads=[("idxs_f", tp)], writes=[("idxs_f", tp)])
        op("vector", lambda h: h.tensor_copy(idxs_u[:, tp, 0:12], idxs_f[:, tp, 0:12]),
           reads=[("idxs_f", tp)],
           writes=[("idxs_u", tp), ("tokU", tp, 0), ("tokU", tp, 1)])

    def tile_w1(tau):
        tp = tau % 2
        dma("gpsimd", wbA[:], w1s, reads=[("idxs_u", tp)] + [("w1s", e) for e in range(E)],
            writes=[("wbA", 0), ("wbA", 1)], indirect=("gather", idxs_u[:, tp, 0:1]),
            lane="gd")
        dma("gpsimd", b1tile[:], b1cols, reads=[("idxs_u", tp), "b1cols"],
            writes=["b1tile"], indirect=("gather", idxs_u[:, tp, 0:1]), lane="gd5")

    def tile_w2(tau):
        tp = tau % 2
        dma("gpsimd", wbB[:], w2s, reads=[("idxs_u", tp)] + [("w2s", e) for e in range(E)],
            writes=[("wbB",)] + ALL_X1K, indirect=("gather", idxs_u[:, tp, 0:1]),
            lane="gd2")
        dma("gpsimd", b2tile[:], b2cols, reads=[("idxs_u", tp), "b2cols"],
            writes=["b2tile"], indirect=("gather", idxs_u[:, tp, 0:1]), lane="gd5")

    def tile_tok_gather(tau):
        tp = tau % 2
        for q in range(4):
            qp = q
            dma("gpsimd", lnTg[:, qp, :], slotbuf,
                reads=[("idxs_u", tp), ("slotbuf",), ("slotzero",)], writes=[("lnTg", qp)],
                indirect=("gather", idxs_u[:, tp, 5 + q:6 + q]),
                lane=("gd3", "gd4")[q % 2])

    def tile_tokens(tau):
        tp = tau % 2
        banks = [pD[0:128, 0, 0:512], pD[0:128, 1, 0:512],
                 pC[:, 512:1024], pC[:, 0:512]]
        keys = [("pD", 0), ("pD", 1), ("pC", 1), ("pC", 0)]
        for q in range(4):
            for dc in range(DC):
                op("tensor", lambda h, dc=dc, q=q, banks=banks: h.transpose(
                    banks[q][:, dc * 64:(dc + 1) * 64].bitcast(BF16),
                    lnTg[:, q, dc * 128:(dc + 1) * 128], idenb[:]),
                    reads=[("lnTg", q), "idenb"], writes=[keys[q]],
                    inc=(dc == DC - 1))
        for q in range(4):
            op("scalar", lambda h, q=q, tp=tp, banks=banks: h.copy(
                lnx[:, tp, :, q * 128:(q + 1) * 128],
                banks[q].bitcast(BF16).rearrange("p (a b) -> p a b", a=DC)),
                reads=[keys[q]], writes=[("lnx", tp, dc) for dc in range(DC)])

    def tile_mm1(tau):
        tp = tau % 2
        for ht in range(HT):
            hp = ht % 2
            for dc in range(DC):
                op("tensor", lambda h, ht=ht, dc=dc, hp=hp, tp=tp: h.matmul(
                    pA[:, hp, :],
                    wbA[:, dc * H + ht * 128: dc * H + (ht + 1) * 128],
                    lnx[:, tp, dc, :], start=(dc == 0), stop=(dc == DC - 1)),
                    reads=[("wbA", 0), ("wbA", 1), ("lnx", tp, dc)],
                    writes=[("pA", hp)], inc=(dc == DC - 1))
            op("scalar", lambda h, ht=ht, hp=hp: h.activation(
                hm[:, ht, :], pA[:, hp, :], AF.Relu,
                bias=b1tile[:, ht:ht + 1]),
                reads=[("pA", hp), "b1tile"], writes=[("hm", ht)])

    def tile_mm2(tau):
        for dt in range(DC):
            dp = dt % 2
            for ht in range(HT):
                op("tensor", lambda h, dt=dt, ht=ht, dp=dp: h.matmul(
                    pB[:, dp, :],
                    wbB[:, ht * D + dt * 128: ht * D + (dt + 1) * 128],
                    hm[:, ht, :], start=(ht == 0), stop=(ht == HT - 1)),
                    reads=[("wbB",), ("hm", ht)], writes=[("pB", dp)],
                    inc=(ht == HT - 1))
            op("scalar", lambda h, dt=dt, dp=dp: h.activation(
                hsb[:, dt, :], pB[:, dp, :], AF.Identity,
                bias=b2tile[:, dt:dt + 1]),
                reads=[("pB", dp), "b2tile"], writes=["hsb"])

    def tile_y(tau):
        tp = tau % 2
        for q in range(4):
            qp = q % 2
            for dt in range(DC):
                op("tensor", lambda h, dt=dt, q=q: h.transpose(
                    pD[0:128, 1, dt * 64:(dt + 1) * 64].bitcast(BF16),
                    hsb[:, dt, q * 128:(q + 1) * 128], idenb[:]),
                    reads=["hsb", "idenb"], writes=[("pD", 1)],
                    inc=(dt == DC - 1))
            op("scalar", lambda h, q=q, qp=qp: h.copy(
                yrow[:, qp, :],
                pD[0:128, 1, 0:512].bitcast(BF16)),
                reads=[("pD", 1)], writes=[("yrow", qp)])
            dma("gpsimd", auxg[:, qp, :], slotaux,
                reads=[("idxs_u", tp), ("slotaux",), ("slotauxzero",)],
                writes=[("auxg", qp)],
                indirect=("gather", idxs_u[:, tp, 5 + q:6 + q]), lane="gd5")
            op("vector", lambda h, qp=qp, tp=tp: h.tensor_scalar(
                idxs_u[:, tp, 9 + qp:10 + qp], auxg[:, qp, 0:1],
                float(T - 1), None, ALU.min),
                reads=[("auxg", qp)], writes=[("tokU", tp, qp)])
            op("vector", lambda h, qp=qp, tp=tp: h.tensor_copy(
                idxs_u[:, tp, 12 + qp:13 + qp], auxg[:, qp, 0:1]),
                reads=[("auxg", qp)], writes=[("tokR", tp, qp)])
            x1g = bigX[:, qp, 2 * (q // 2):2 * (q // 2) + 2, :].rearrange(
                "p a b -> p (a b)")
            xk = [BX(qp, 2 * (q // 2)), BX(qp, 2 * (q // 2) + 1)]
            dma("gpsimd", x1g, x1tok,
                reads=[("tokU", tp, qp), ("x1tok",)],
                writes=[("x1g", q)] + xk,
                indirect=("gather", idxs_u[:, tp, 9 + qp:10 + qp]),
                lane=("gd6", "gd7")[qp])
            op("scalar", lambda h, q=q, qp=qp: h.activation(
                ln2row[:, qp, :], yrow[:, qp, :], AF.Identity,
                scale=auxg[:, qp, 1:2]),
                reads=[("yrow", qp), ("auxg", qp)], writes=[("ln2row", qp)])
            op("vector", lambda h, q=q, qp=qp, x1g=x1g: h.tensor_tensor(
                x1g, x1g, ln2row[:, qp, :], ALU.add),
                reads=[("x1g", q), ("ln2row", qp)],
                writes=[("x1g", q)])
            dma("gpsimd", out_t, x1g,
                reads=[("x1g", q), ("tokR", tp, qp)], writes=[("out", 0)],
                indirect=("scatter", idxs_u[:, tp, 12 + qp:13 + qp], breg),
                lane="gd8")

    tile_front(0)
    tile_w1(0)
    tile_w2(0)
    tile_tok_gather(0)
    tile_tokens(0)
    for tau in range(NTILES):
        tile_mm1(tau)
        if tau + 1 < NTILES:
            tile_front(tau + 1)
            tile_w1(tau + 1)
            tile_tok_gather(tau + 1)
        tile_mm2(tau)
        tile_y(tau)
        if tau + 1 < NTILES:
            tile_tokens(tau + 1)
            tile_w2(tau + 1)

    final_wait("sync", ["sd", "sd2", "sd3", "sd4", "gd", "gd2", "gd3", "gd4",
                        "gd5", "gd6", "gd7", "gd8"])

    @block.sync
    def _(sync):
        for fn in Q["sync"]:
            fn(sync)

    @block.tensor
    def _(tensor):
        for fn in Q["tensor"]:
            fn(tensor)

    @block.vector
    def _(vector):
        for fn in Q["vector"]:
            fn(vector)

    @block.scalar
    def _(scalar):
        for fn in Q["scalar"]:
            fn(scalar)

    @block.gpsimd
    def _(gpsimd):
        gpsimd.reg_mov(breg, T - 1)
        for fn in Q["gpsimd"]:
            fn(gpsimd)

    ctx.close()
    return nc


def make_inputs(core_x, params, NTILES):
    T = core_x.shape[0]
    tau = (np.arange(NTILES, dtype=np.float32) * T2)[:, None]
    d = {
        "xt": np.ascontiguousarray(core_x.T),
        "w_in_t": np.ascontiguousarray(params["W_in"].T),
        "w_out_t": np.ascontiguousarray(params["W_out"].T),
        "lam_log": params["lambda_log"].reshape(1, D),
        "gamma_d": params["gamma"].reshape(1, D),
        "b_in_d": params["b_in"].reshape(1, D),
        "b_out_d": params["b_out"].reshape(1, D),
        "wr_d": params["Wr"],
        "br_d": params["br"].reshape(E, 1),
        "w1_d": params["W1"],
        "b1_d": params["b1"],
        "w2_d": params["W2"],
        "b2_d": params["b2"],
        "tri_d": np.triu(np.ones((CH, CH), np.float32)),
        "iden_d": np.eye(128, dtype=np.float32),
        "iotaE_d": np.broadcast_to(np.arange(E, dtype=np.float32), (128, E)),
        "negsidx_d": (63.0 - np.arange(128, dtype=np.float32)).reshape(128, 1),
        "tix_d": (np.arange(128, dtype=np.float32) - 63.0).reshape(1, 128),
        "ones1_d": np.ones((1, 128), np.float32),
        "onescol_d": np.ones((128, 1), np.float32),
        "zrow_d": np.zeros((1, E), np.float32),
        "zrowD_d": np.zeros((1, D), np.float32),
        "zaux_d": np.array([[float(core_x.shape[0]), 0.0]], np.float32),
        "iotaQ_d": (np.arange(128, dtype=np.float32)[:, None]
                    + 128.0 * np.arange(4, dtype=np.float32)[None, :]),
        "thresh_d": np.broadcast_to(
            (np.arange(16, dtype=np.float32) * T2)[None, :], (E, 16)
        ).reshape(1, E * 16),
        "tau_d": np.broadcast_to(tau, (NTILES, E)).reshape(1, NTILES * E),
        "iotaE24_d": np.broadcast_to(
            np.arange(E, dtype=np.float32)[None, :], (NTILES, E)
        ).reshape(1, NTILES * E),
    }
    return {k: np.ascontiguousarray(np.asarray(v, dtype=np.float32)) for k, v in d.items()}


def kernel(L0_attn_x, L0_attn_lambda_log, L0_attn_gamma, L0_attn_W_in, L0_attn_b_in,
           L0_attn_W_out, L0_attn_b_out, L0_ffn_Wr, L0_ffn_br, L0_ffn_W1, L0_ffn_b1,
           L0_ffn_W2, L0_ffn_b2):
    x = np.asarray(L0_attn_x, dtype=np.float32)
    B, L, Dm = x.shape
    assert Dm == D and B % NCORES == 0
    n_seq = B // NCORES
    NTILES = (n_seq * L) // T2 + E
    params = {
        "lambda_log": np.asarray(L0_attn_lambda_log, np.float32),
        "gamma": np.asarray(L0_attn_gamma, np.float32),
        "W_in": np.asarray(L0_attn_W_in, np.float32),
        "b_in": np.asarray(L0_attn_b_in, np.float32),
        "W_out": np.asarray(L0_attn_W_out, np.float32),
        "b_out": np.asarray(L0_attn_b_out, np.float32),
        "Wr": np.asarray(L0_ffn_Wr, np.float32),
        "br": np.asarray(L0_ffn_br, np.float32),
        "W1": np.asarray(L0_ffn_W1, np.float32),
        "b1": np.asarray(L0_ffn_b1, np.float32),
        "W2": np.asarray(L0_ffn_W2, np.float32),
        "b2": np.asarray(L0_ffn_b2, np.float32),
    }
    nc = build_program(n_seq, L)
    in_maps = []
    for c in range(NCORES):
        core_x = x[c * n_seq:(c + 1) * n_seq].reshape(n_seq * L, D)
        in_maps.append(make_inputs(core_x, params, NTILES))
    res = run_bass_kernel_spmd(nc, in_maps, core_ids=list(range(NCORES)))
    outs = []
    for c in range(NCORES):
        o = np.asarray(res.results[c]["out"])
        outs.append(o.reshape(n_seq, L, D))
    return np.concatenate(outs, axis=0).astype(np.float32)


# version 29
# speedup vs baseline: 1.0285x; 1.0285x over previous
import sys
sys.path.insert(0, '/opt/trn_rl_repo')
import contextlib
import numpy as np
import concourse.bass as bass
import concourse.mybir as mybir
from concourse.bass_utils import run_bass_kernel_spmd

F32 = mybir.dt.float32
BF16 = mybir.dt.bfloat16
U32 = mybir.dt.uint32
F32R = mybir.dt.float32r
AF = mybir.ActivationFunctionType
ALU = mybir.AluOpType
AX = mybir.AxisListType

D = 1024
DC = 8
H = 2048
HT = 16
E = 8
T2 = 512
CH = 128
NCORES = 8
OP_TRACE = {"tensor": [], "note": None}


def build_program(n_seq, L):
    OP_TRACE["tensor"] = []
    T = n_seq * L
    NT2 = T // T2
    T2_PER_SEQ = L // T2
    NTILES = T // T2 + E
    SLOTROWS = E * T + NTILES * T2 + 1
    ZROW = SLOTROWS - 1
    NCHUNK = T // CH
    assert L % T2 == 0 and T2 % CH == 0

    nc = bass.Bass("TRN2", target_bir_lowering=False)

    xt = nc.dram_tensor("xt", [D, T], F32, kind="ExternalInput").ap()
    w_in_t = nc.dram_tensor("w_in_t", [D, D], F32, kind="ExternalInput").ap()
    w_out_t = nc.dram_tensor("w_out_t", [D, D], F32, kind="ExternalInput").ap()
    lam_log = nc.dram_tensor("lam_log", [1, D], F32, kind="ExternalInput").ap()
    gamma_d = nc.dram_tensor("gamma_d", [1, D], F32, kind="ExternalInput").ap()
    b_in_d = nc.dram_tensor("b_in_d", [1, D], F32, kind="ExternalInput").ap()
    b_out_d = nc.dram_tensor("b_out_d", [1, D], F32, kind="ExternalInput").ap()
    wr_d = nc.dram_tensor("wr_d", [D, E], F32, kind="ExternalInput").ap()
    br_d = nc.dram_tensor("br_d", [E, 1], F32, kind="ExternalInput").ap()
    w1_d = nc.dram_tensor("w1_d", [E, D, H], F32, kind="ExternalInput").ap()
    b1_d = nc.dram_tensor("b1_d", [E, H], F32, kind="ExternalInput").ap()
    w2_d = nc.dram_tensor("w2_d", [E, H, D], F32, kind="ExternalInput").ap()
    b2_d = nc.dram_tensor("b2_d", [E, D], F32, kind="ExternalInput").ap()
    tri_d = nc.dram_tensor("tri_d", [CH, CH], F32, kind="ExternalInput").ap()
    iden_d = nc.dram_tensor("iden_d", [128, 128], F32, kind="ExternalInput").ap()
    iotaE_d = nc.dram_tensor("iotaE_d", [128, E], F32, kind="ExternalInput").ap()
    negsidx_d = nc.dram_tensor("negsidx_d", [128, 1], F32, kind="ExternalInput").ap()
    tix_d = nc.dram_tensor("tix_d", [1, 128], F32, kind="ExternalInput").ap()
    ones1_d = nc.dram_tensor("ones1_d", [1, 128], F32, kind="ExternalInput").ap()
    onescol_d = nc.dram_tensor("onescol_d", [128, 1], F32, kind="ExternalInput").ap()
    zrow_d = nc.dram_tensor("zrow_d", [1, E], F32, kind="ExternalInput").ap()
    zrowD_d = nc.dram_tensor("zrowD_d", [1, D], F32, kind="ExternalInput").ap()
    iotaQ_d = nc.dram_tensor("iotaQ_d", [128, 4], F32, kind="ExternalInput").ap()
    thresh_d = nc.dram_tensor("thresh_d", [1, E * 16], F32, kind="ExternalInput").ap()
    tau_d = nc.dram_tensor("tau_d", [1, NTILES * E], F32, kind="ExternalInput").ap()
    iotaE24_d = nc.dram_tensor("iotaE24_d", [1, NTILES * E], F32, kind="ExternalInput").ap()
    zaux_d = nc.dram_tensor("zaux_d", [1, 2], F32, kind="ExternalInput").ap()
    out_t = nc.dram_tensor("out", [T, D], F32, kind="ExternalOutput").ap()
    slotaux = nc.dram_tensor("slotaux", [SLOTROWS, 2], F32, kind="Internal").ap()
    x1tok = nc.dram_tensor("x1tok", [T, D], F32, kind="Internal").ap()
    slotbuf = nc.dram_tensor("slotbuf", [SLOTROWS, D], BF16, kind="Internal").ap()
    w1s = nc.dram_tensor("w1s", [E * 128, DC * H], BF16, kind="Internal").ap()
    w2s = nc.dram_tensor("w2s", [E * 128, HT * D], BF16, kind="Internal").ap()
    b1cols = nc.dram_tensor("b1cols", [E * 128, HT], F32, kind="Internal").ap()
    b2cols = nc.dram_tensor("b2cols", [E * 128, DC], F32, kind="Internal").ap()

    ctx = contextlib.ExitStack()
    EN = ctx.enter_context
    wbA = EN(nc.sbuf_tensor("wbA", [128, 16384], BF16))
    wbB = EN(nc.sbuf_tensor("wbB", [128, 16384], BF16))
    hm = EN(nc.sbuf_tensor("hm", [128, HT, T2], BF16))
    bigX = EN(nc.sbuf_tensor("bigX", [128, 2, DC, T2], F32))
    xsq = EN(nc.sbuf_tensor("xsq", [128, 2, T2], F32))
    xb = EN(nc.sbuf_tensor("xb", [128, 2, T2], BF16))
    xsqb = EN(nc.sbuf_tensor("xsqb", [128, 2, T2], BF16))
    onescol_b = EN(nc.sbuf_tensor("onescol_b", [128, 1], BF16))
    lnx = EN(nc.sbuf_tensor("lnx", [128, 2, DC, T2], BF16))
    gub = EN(nc.sbuf_tensor("gub", [128, 2, D], BF16))
    hsb = EN(nc.sbuf_tensor("hsb", [128, DC, T2], BF16))
    scr = EN(nc.sbuf_tensor("scr", [128, 2048], F32))
    logits_sb = EN(nc.sbuf_tensor("logits_sb", [E, 128], F32))
    rowbuf = EN(nc.sbuf_tensor("rowbuf", [1, 3 * T2], F32))
    rsc = EN(nc.sbuf_tensor("rsc", [128, 16], F32))
    rsc2 = EN(nc.sbuf_tensor("rsc2", [128, 4, E], F32))
    hinit = EN(nc.sbuf_tensor("hinit", [128, DC], F32))
    pos_sb = EN(nc.sbuf_tensor("pos_sb", [128, NCHUNK], F32))
    mp_sb = EN(nc.sbuf_tensor("mp_sb", [128, NCHUNK], F32))
    x1row = EN(nc.sbuf_tensor("x1row", [128, 2, D], F32))
    ln2row = EN(nc.sbuf_tensor("ln2row", [128, 2, D], BF16))
    ln2d = EN(nc.sbuf_tensor("ln2d", [128, D], BF16))
    lnTg = EN(nc.sbuf_tensor("lnTg", [128, 4, D], BF16))
    yrow = EN(nc.sbuf_tensor("yrow", [128, 2, D], BF16))
    idxs_f = EN(nc.sbuf_tensor("idxs_f", [128, 2, 14], F32))
    idxs_u = EN(nc.sbuf_tensor("idxs_u", [128, 2, 14], U32))
    auxg = EN(nc.sbuf_tensor("auxg", [128, 2, 2], F32))
    aux_sb = EN(nc.sbuf_tensor("aux_sb", [128, 2], F32))
    posU = EN(nc.sbuf_tensor("posU", [128, 2], U32))
    posUall = EN(nc.sbuf_tensor("posUall", [128, 64], U32))
    b1tile = EN(nc.sbuf_tensor("b1tile", [128, HT], F32))
    b2tile = EN(nc.sbuf_tensor("b2tile", [128, DC], F32))
    rc8 = EN(nc.sbuf_tensor("rc8", [1, E], F32))
    finrow = EN(nc.sbuf_tensor("finrow", [1, 4 * NTILES], F32))
    startrow = EN(nc.sbuf_tensor("startrow", [1, 16], F32))
    grow = EN(nc.sbuf_tensor("grow", [1, max(128, NTILES * E)], F32))
    tri_f = EN(nc.sbuf_tensor("tri_f", [128, 128], F32))
    tri_b = EN(nc.sbuf_tensor("tri_b", [128, 128], BF16))
    iden = EN(nc.sbuf_tensor("iden", [128, 128], F32))
    idenb = EN(nc.sbuf_tensor("idenb", [128, 128], BF16))
    iotaE = EN(nc.sbuf_tensor("iotaE", [128, E], F32))
    negsidx = EN(nc.sbuf_tensor("negsidx", [128, 1], F32))
    tix = EN(nc.sbuf_tensor("tix", [1, 128], F32))
    ones1 = EN(nc.sbuf_tensor("ones1", [1, 128], F32))
    onescol = EN(nc.sbuf_tensor("onescol", [128, 1], F32))
    iotaQ = EN(nc.sbuf_tensor("iotaQ", [128, 4], F32))
    thresh16 = EN(nc.sbuf_tensor("thresh16", [1, E * 16], F32))
    tau24 = EN(nc.sbuf_tensor("tau24", [1, NTILES * E], F32))
    iotaE24 = EN(nc.sbuf_tensor("iotaE24", [1, NTILES * E], F32))
    lamNegPow = EN(nc.sbuf_tensor("lamNegPow", [128, D], BF16))
    gbLam = EN(nc.sbuf_tensor("gbLam", [128, D], BF16))
    lamPow = EN(nc.sbuf_tensor("lamPow", [128, DC, CH], F32))
    lamC64 = EN(nc.sbuf_tensor("lamC64", [128, DC], F32))
    LCol = EN(nc.sbuf_tensor("LCol", [128, DC], F32))
    L8 = EN(nc.sbuf_tensor("L8", [DC, 128], F32))
    b1Col = EN(nc.sbuf_tensor("b1Col", [128, HT * E], F32))
    b2ColE = EN(nc.sbuf_tensor("b2ColE", [128, E * DC], F32))
    boCol = EN(nc.sbuf_tensor("boCol", [128, DC], F32))
    br_col = EN(nc.sbuf_tensor("br_col", [E, 1], F32))
    wr_b = EN(nc.sbuf_tensor("wr_b", [128, DC * E], BF16))
    pA = EN(nc.psum_tensor("pA", [128, 2, 512], F32))
    pB = EN(nc.psum_tensor("pB", [128, 2, 512], F32))
    pC = EN(nc.psum_tensor("pC", [128, 1024], F32))
    pD = EN(nc.psum_tensor("pD", [128, 2, 512], F32))
    semh = {}
    for name in ("sd", "sd2", "gd", "gd2", "gd3", "gd4", "gd5", "gd6", "gd7", "gd8", "sd3", "sd4", "pe", "act", "dve"):
        semh[name] = EN(nc.semaphore("s_" + name))
    breg = EN(nc.gpsimd.register("breg"))
    block = EN(nc.Block())

    ENG_SEM = {"sync": "sd", "gpsimd": "gd", "tensor": "pe", "vector": "dve",
               "scalar": "act"}
    QLANES = {"sync": ["sd", "sd2"], "gpsimd": ["gd", "gd2"]}
    cnt = {k: 0 for k in semh}
    Q = {k: [] for k in ENG_SEM}
    regions = {}
    pend = {k: {"w": [], "r": []} for k in ENG_SEM}
    satisfied = {k: {} for k in ENG_SEM}
    lane_rr = {"sync": 0, "gpsimd": 0}

    def R(k):
        if k not in regions:
            regions[k] = {"w": None, "r": {}}
        return regions[k]

    def op(eng, emit, reads=(), writes=(), inc=True, extra=(), sem=None, inc_k=1):
        sem = sem or ENG_SEM[eng]
        want = {}

        def addw(s, v):
            if v and want.get(s, 0) < v:
                want[s] = v
        for k in reads:
            reg = R(k)
            if reg["w"] is not None:
                addw(*reg["w"])
            assert all(k not in pend[e]["w"] for e in pend if e != eng), \
                f"read of pending write {k}"
        for k in writes:
            reg = R(k)
            if reg["w"] is not None:
                addw(*reg["w"])
            for s, v in reg["r"].items():
                addw(s, v)
            assert all(k not in pend[e]["w"] and k not in pend[e]["r"]
                       for e in pend if e != eng), f"write over pending {k}"
        for s, v in extra:
            addw(s, v)
        waits = tuple((s, v) for s, v in want.items()
                      if v > satisfied[eng].get(s, 0))
        for s, v in waits:
            satisfied[eng][s] = v
        post = None
        if inc:
            cnt[sem] += inc_k
            post = cnt[sem]
            for k in pend[eng]["w"] + list(writes):
                R(k)["w"] = (sem, post)
                R(k)["r"] = {}
            for k in pend[eng]["r"] + list(reads):
                R(k)["r"][sem] = post
            pend[eng] = {"w": [], "r": []}
        else:
            pend[eng]["w"] += list(writes)
            pend[eng]["r"] += list(reads)

        if eng == "tensor":
            OP_TRACE["tensor"].append((waits, tuple(reads), tuple(writes)))

        def fn(h, emit=emit, waits=waits, inc=inc, sem=sem, inc_k=inc_k):
            for s, v in waits:
                h.wait_ge(semh[s], v)
            ins = emit(h)
            if inc:
                ins.then_inc(semh[sem], inc_k)
        Q[eng].append(fn)
        return post

    def dma(queue, dst, src, reads=(), writes=(), indirect=None, lane=None):
        if lane is None:
            lane = QLANES[queue][lane_rr[queue]]
            lane_rr[queue] ^= 1
        prev = cnt[lane]
        if indirect is None:
            emit = lambda h, d=dst, s=src: h.dma_start(out=d, in_=s)
        else:
            kind, idx_ap = indirect[0], indirect[1]
            bc = indirect[2] if len(indirect) > 2 else None
            if kind == "gather":
                emit = lambda h, d=dst, s=src, i=idx_ap, bc=bc: h.indirect_dma_start(
                    out=d, out_offset=None, in_=s,
                    in_offset=bass.IndirectOffsetOnAxis(ap=i, axis=0),
                    **({} if bc is None else
                       {"bounds_check": bc, "oob_is_err": False}))
            else:
                emit = lambda h, d=dst, s=src, i=idx_ap, bc=bc: h.indirect_dma_start(
                    out=d, out_offset=bass.IndirectOffsetOnAxis(ap=i, axis=0),
                    in_=s, in_offset=None,
                    **({} if bc is None else
                       {"bounds_check": bc, "oob_is_err": False}))
        return op(queue, emit, reads=reads, writes=writes, sem=lane, inc_k=16,
                  extra=([(lane, prev)] if prev else []))

    def final_wait(eng, sems_to_wait):
        vals = [(s, cnt[s]) for s in sems_to_wait if cnt[s] > 0]

        def fn(h, vals=tuple(vals)):
            for s, v in vals:
                h.wait_ge(semh[s], v)
        Q[eng].append(fn)

    xt_r = xt.rearrange("(c p) t -> p c t", c=DC)
    BX = lambda par, dc: ("bigX", par, dc)
    x1d = [wbB[:, 0:8192].bitcast(F32).rearrange("p (a b) -> p a b", a=DC),
           wbB[:, 8192:16384].bitcast(F32).rearrange("p (a b) -> p a b", a=DC)]
    X1K = lambda par, dc: ("x1d", par, dc)
    ALL_X1K = [("x1d", p, d) for p in range(2) for d in range(DC)]

    def load_x(j, par):
        dma("sync", bigX[:, par, 0:4, :], xt_r[:, 0:4, j * T2:(j + 1) * T2],
            writes=[BX(par, dc) for dc in range(4)])
        dma("sync", bigX[:, par, 4:8, :], xt_r[:, 4:8, j * T2:(j + 1) * T2],
            writes=[BX(par, dc) for dc in range(4, 8)])

    load_x(0, 0)
    load_x(1, 1)

    for dst, src, key in [
            (tri_f[:], tri_d, "tri_f"), (iden[:], iden_d, "iden"),
            (iotaE[:], iotaE_d, "iotaE"), (negsidx[:], negsidx_d, "negsidx"),
            (tix[:], tix_d, "tix"), (ones1[:], ones1_d, "ones1"),
            (onescol[:], onescol_d, "onescol"),
            (br_col[:], br_d, "br_col"), (rc8[:], zrow_d, "rc8"),
            (iotaQ[:], iotaQ_d, "iotaQ"), (thresh16[:], thresh_d, "thresh16"),
            (tau24[:], tau_d, "tau24"), (iotaE24[:], iotaE24_d, "iotaE24")]:
        dma("sync", dst, src, writes=[key])
    dma("sync", L8[:], lam_log.rearrange("o (a p) -> (o a) p", a=DC), writes=["L8"])

    op("vector", lambda h: h.tensor_copy(tri_b[:], tri_f[:]),
       reads=["tri_f"], writes=["tri_b"])
    op("vector", lambda h: h.tensor_copy(idenb[:], iden[:]),
       reads=["iden"], writes=["idenb"])
    op("vector", lambda h: h.tensor_copy(onescol_b[:], onescol[:]),
       reads=["onescol"], writes=["onescol_b"])

    dma("gpsimd", slotbuf[ZROW:ZROW + 1, :], zrowD_d, writes=[("slotzero",)])
    dma("gpsimd", slotaux[ZROW:ZROW + 1, :], zaux_d, writes=[("slotauxzero",)])
    dma("gpsimd", wbA[:, 0:8192].rearrange("p (c t) -> p c t", c=DC),
        w_in_t.rearrange("(c p) t -> p c t", c=DC), writes=[("wbA", 0)])
    dma("gpsimd", wbA[:, 8192:16384].rearrange("p (c t) -> p c t", c=DC),
        w_out_t.rearrange("(c p) t -> p c t", c=DC), writes=[("wbA", 1)])
    dma("gpsimd", wr_b[:].rearrange("p (c e) -> p c e", c=DC),
        wr_d.rearrange("(c p) e -> p c e", c=DC), writes=["wr_b"])

    op("tensor", lambda h: h.transpose(pC[:, 0:DC], L8[:], iden[0:DC, 0:DC]),
       reads=["L8", "iden"], writes=[("pC", 0)])
    op("scalar", lambda h: h.copy(LCol[:], pC[:, 0:DC]),
       reads=[("pC", 0)], writes=["LCol"])
    op("scalar", lambda h: h.activation(lamC64[:], LCol[:], AF.Exp, scale=64.0),
       reads=["LCol"], writes=["lamC64"])
    dma("sync", scr[0:DC, 1024:1152],
        b_out_d.rearrange("o (a p) -> (o a) p", a=DC), writes=[("scr", 1)])
    op("tensor", lambda h: h.transpose(pC[:, 16:24], scr[0:DC, 1024:1152],
                                       iden[0:DC, 0:DC]),
       reads=[("scr", 1), "iden"], writes=[("pC", 0)])
    op("scalar", lambda h: h.copy(boCol[:], pC[:, 16:24]),
       reads=[("pC", 0)], writes=["boCol"])
    for half in range(2):
        dma("sync", scr[0:E, 1024:2048], b1_d[:, half * 1024:(half + 1) * 1024],
            writes=[("scr", 1)])
        for hh in range(8):
            ht = half * 8 + hh
            op("tensor", lambda h, ht=ht, hh=hh: h.transpose(
                pC[:, 512 + ht * 8: 512 + (ht + 1) * 8],
                scr[0:E, 1024 + hh * 128:1024 + (hh + 1) * 128], iden[0:E, 0:E]),
                reads=[("scr", 1), "iden"], writes=[("pC", 1)], inc=(hh == 7))
    op("scalar", lambda h: h.copy(
        b1Col[:].rearrange("p (e t) -> p t e", e=E),
        pC[:, 512:640].rearrange("p (t e) -> p t e", t=HT)),
        reads=[("pC", 1)], writes=["b1Col"])
    dma("sync", b1cols.rearrange("(e p) t -> p e t", e=E),
        b1Col[:].rearrange("p (e t) -> p e t", e=E),
        reads=["b1Col"], writes=["b1cols"])
    dma("sync", scr[0:E, 1024:2048], b2_d, writes=[("scr", 1)])
    for dt in range(DC):
        op("tensor", lambda h, dt=dt: h.transpose(
            pC[:, 512 + dt * 8:512 + (dt + 1) * 8],
            scr[0:E, 1024 + dt * 128:1024 + (dt + 1) * 128], iden[0:E, 0:E]),
            reads=[("scr", 1), "iden"], writes=[("pC", 1)], inc=(dt == DC - 1))
    op("scalar", lambda h: h.copy(
        b2ColE[:].rearrange("p (e t) -> p t e", e=E),
        pC[:, 512:576].rearrange("p (t e) -> p t e", t=DC)),
        reads=[("pC", 1)], writes=["b2ColE"])
    dma("sync", b2cols.rearrange("(e p) t -> p e t", e=E),
        b2ColE[:].rearrange("p (e t) -> p e t", e=E),
        reads=["b2ColE"], writes=["b2cols"])

    dma("sync", rowbuf[:, 0:1024], lam_log, writes=["rowbuf"])
    op("tensor", lambda h: h.matmul(pA[:, 0, :], ones1[:], rowbuf[:, 0:512],
                                    start=True, stop=True),
       reads=["ones1", "rowbuf"], writes=[("pA", 0)])
    op("tensor", lambda h: h.matmul(pA[:, 1, :], ones1[:], rowbuf[:, 512:1024],
                                    start=True, stop=True),
       reads=["ones1", "rowbuf"], writes=[("pA", 1)])
    op("vector", lambda h: h.tensor_scalar(
        scr[:, 1024:2048], pA[:].rearrange("p a b -> p (a b)"), negsidx[:],
        None, ALU.mult),
        reads=[("pA", 0), ("pA", 1), "negsidx"], writes=[("scr", 1)])
    op("scalar", lambda h: h.activation(lamNegPow[:], scr[:, 1024:2048], AF.Exp),
       reads=[("scr", 1)], writes=["lamNegPow"])
    dma("sync", rowbuf[:, 0:1024], gamma_d, writes=["rowbuf"])
    dma("sync", scr[0:1, 1024:2048], b_in_d, writes=[("scr", 1)])
    op("tensor", lambda h: h.matmul(pD[:, 0, :], ones1[:], rowbuf[:, 0:512],
                                    start=True, stop=True),
       reads=["ones1", "rowbuf"], writes=[("pD", 0)])
    op("tensor", lambda h: h.matmul(pD[:, 1, :], ones1[:], rowbuf[:, 512:1024],
                                    start=True, stop=True),
       reads=["ones1", "rowbuf"], writes=[("pD", 1)])
    op("vector", lambda h: h.tensor_tensor(
        rowbuf[:, 0:1024], rowbuf[:, 0:1024], scr[0:1, 1024:2048], ALU.mult),
        reads=["rowbuf", ("scr", 1)], writes=["rowbuf"])
    op("tensor", lambda h: h.matmul(pB[:, 0, :], ones1[:], rowbuf[:, 0:512],
                                    start=True, stop=True),
       reads=["ones1", "rowbuf"], writes=[("pB", 0)])
    op("tensor", lambda h: h.matmul(pB[:, 1, :], ones1[:], rowbuf[:, 512:1024],
                                    start=True, stop=True),
       reads=["ones1", "rowbuf"], writes=[("pB", 1)])
    op("vector", lambda h: h.tensor_tensor(
        gbLam[:], pB[:].rearrange("p a b -> p (a b)"), lamNegPow[:], ALU.mult),
        reads=[("pB", 0), ("pB", 1), "lamNegPow"], writes=["gbLam"])
    op("vector", lambda h: h.tensor_tensor(
        lamNegPow[:], lamNegPow[:], pD[:].rearrange("p a b -> p (a b)"), ALU.mult),
        reads=["lamNegPow", ("pD", 0), ("pD", 1)], writes=["lamNegPow"])
    op("tensor", lambda h: h.matmul(pC[:, 0:128], ones1[:], tix[:],
                                    start=True, stop=True),
       reads=["ones1", "tix"], writes=[("pC", 0)])
    for k in range(DC):
        op("vector", lambda h, k=k: h.tensor_scalar(
            scr[:, 1024 + k * 128:1024 + (k + 1) * 128], pC[:, 0:128],
            LCol[:, k:k + 1], None, ALU.mult),
            reads=[("pC", 0), "LCol"], writes=[("scr", 1)])
    op("scalar", lambda h: h.activation(
        lamPow[:].rearrange("p a b -> p (a b)"), scr[:, 1024:2048], AF.Exp),
        reads=[("scr", 1)], writes=["lamPow"])
    conv_emits = []
    for e in range(E):
        conv_emits.append(lambda e=e: op("gpsimd", lambda h, e=e: h.dma_start(
            out=w1s[e * 128:(e + 1) * 128, :].rearrange("p (c h) -> p c h", c=DC),
            in_=w1_d[e].rearrange("(c p) h -> p c h", c=DC)),
            writes=[("w1s", e)], sem="gd3", inc_k=16))
        for half in range(2):
            conv_emits.append(lambda e=e, half=half: op(
                "gpsimd", lambda h, e=e, half=half: h.dma_start(
                    out=w2s[e * 128:(e + 1) * 128,
                            half * 8192:(half + 1) * 8192].rearrange("p (c t) -> p c t", c=8),
                    in_=w2_d[e].rearrange("(c p) t -> p c t", c=HT)[:, half * 8:(half + 1) * 8, :]),
                writes=[("w2s", e)], sem="gd3", inc_k=16))

    def emit_ln_p1ab(par):
        for dc in range(DC):
            op("scalar", lambda h, dc=dc, par=par: h.copy(
                xb[:, dc % 2, :], bigX[:, par, dc, :]),
                reads=[BX(par, dc)], writes=[("xb", dc % 2)])
            op("vector", lambda h, dc=dc: h.tensor_tensor(
                xsqb[:, dc % 2, :], xb[:, dc % 2, :], xb[:, dc % 2, :], ALU.mult),
                reads=[("xb", dc % 2)], writes=[("xsqb", dc % 2)])
            op("tensor", lambda h, dc=dc: h.matmul(
                pA[0:1, 0, :], onescol_b[:], xb[:, dc % 2, :],
                start=(dc == 0), stop=(dc == DC - 1)),
                reads=["onescol_b", ("xb", dc % 2)], writes=[("pA", 0)],
                inc=True)
            op("tensor", lambda h, dc=dc: h.matmul(
                pA[0:1, 1, :], onescol_b[:], xsqb[:, dc % 2, :],
                start=(dc == 0), stop=(dc == DC - 1)),
                reads=["onescol_b", ("xsqb", dc % 2)], writes=[("pA", 1)],
                inc=True)
        op("vector", lambda h: h.tensor_scalar(
            rowbuf[:, 0:T2], pA[0:1, 0, :], 1.0 / D, None, ALU.mult),
            reads=[("pA", 0)], writes=["rowbuf"])
        op("vector", lambda h: h.tensor_scalar(
            rowbuf[:, T2:2 * T2], pA[0:1, 1, :], 1.0 / D, 1e-5, ALU.mult, ALU.add),
            reads=[("pA", 1)], writes=["rowbuf"])
        op("vector", lambda h: h.tensor_tensor(
            rowbuf[:, 2 * T2:3 * T2], rowbuf[:, 0:T2], rowbuf[:, 0:T2], ALU.mult),
            reads=["rowbuf"], writes=["rowbuf"])
        op("vector", lambda h: h.tensor_tensor(
            rowbuf[:, T2:2 * T2], rowbuf[:, T2:2 * T2], rowbuf[:, 2 * T2:3 * T2],
            ALU.subtract),
            reads=["rowbuf"], writes=["rowbuf"])
        op("scalar", lambda h: h.activation(
            rowbuf[:, T2:2 * T2], rowbuf[:, T2:2 * T2], AF.Ln),
            reads=["rowbuf"], writes=["rowbuf"])
        op("scalar", lambda h: h.activation(
            rowbuf[:, 2 * T2:3 * T2], rowbuf[:, T2:2 * T2], AF.Exp, scale=-0.5),
            reads=["rowbuf"], writes=["rowbuf"])
        op("vector", lambda h: h.tensor_tensor(
            rowbuf[:, T2:2 * T2], rowbuf[:, 0:T2], rowbuf[:, 2 * T2:3 * T2],
            ALU.mult),
            reads=["rowbuf"], writes=["rowbuf"])

    def emit_ln_p1c():
        op("tensor", lambda h: h.matmul(
            pC[:, 0:512], ones1[:], rowbuf[:, 2 * T2:3 * T2], start=True, stop=True),
            reads=["ones1", "rowbuf"], writes=[("pC", 0)])
        op("tensor", lambda h: h.matmul(
            pC[:, 512:1024], ones1[:], rowbuf[:, T2:2 * T2], start=True, stop=True),
            reads=["ones1", "rowbuf"], writes=[("pC", 1)])

    def emit_ln_p2(par):
        rbc = pC[:, 0:512].unsqueeze(1).broadcast_to([128, DC, 512])
        mbc = pC[:, 512:1024].unsqueeze(1).broadcast_to([128, DC, 512])
        op("vector", lambda h, par=par, rbc=rbc: h.tensor_tensor(
            lnx[:, par], bigX[:, par], rbc, ALU.mult),
            reads=[("pC", 0)] + [BX(par, dc) for dc in range(DC)],
            writes=[("lnx", par, dc) for dc in range(DC)])
        op("vector", lambda h, par=par, mbc=mbc: h.tensor_tensor(
            lnx[:, par], lnx[:, par], mbc, ALU.subtract),
            reads=[("pC", 1)] + [("lnx", par, dc) for dc in range(DC)],
            writes=[("lnx", par, dc) for dc in range(DC)])

    def tail_a1(j, q):
        par = j % 2
        cidx = j * 4 + q
        qp = q % 2
        for g in range(2):
            for dd in range(4):
                dc = g * 4 + dd
                op("tensor", lambda h, dc=dc, dd=dd, q=q, par=par: h.transpose(
                    pD[:, 0, dd * 128:(dd + 1) * 128],
                    x1d[par][:, dc, q * 128:(q + 1) * 128], iden[:]),
                    reads=[X1K(par, dc), "iden"], writes=[("pD", 0)],
                    inc=(dd == 3))
            op("scalar", lambda h, g=g, qp=qp: h.copy(
                x1row[:, qp, g * 512:(g + 1) * 512], pD[:, 0, :]),
                reads=[("pD", 0)], writes=[("x1row", qp)])
        dma("sync", x1tok[cidx * 128:(cidx + 1) * 128, :], x1row[:, qp, :],
            reads=[("x1row", qp)], writes=[("x1tok",)])
        op("scalar", lambda h, qp=qp: h.activation(
            lnTg[:, 0, :], x1row[:, qp, :], AF.Identity,
            accum_out=rsc[:, 6 + qp:7 + qp]),
            reads=[("x1row", qp)], writes=[("lnTg", 0), ("rsc", 6 + qp)])
        op("scalar", lambda h, qp=qp: h.activation(
            lnTg[:, 1, :], x1row[:, qp, :], AF.Square,
            accum_out=rsc[:, 8 + qp:9 + qp]),
            reads=[("x1row", qp)], writes=[("lnTg", 1), ("rsc", 8 + qp)])

    def tail_a2(j, q):
        qp = q % 2
        op("vector", lambda h, qp=qp: h.tensor_scalar(
            rsc[:, 10 + qp:11 + qp], rsc[:, 6 + qp:7 + qp], 1.0 / D, None,
            ALU.mult),
            reads=[("rsc", 6 + qp)], writes=[("rsc", 10 + qp)])
        op("vector", lambda h, qp=qp: h.tensor_scalar(
            rsc[:, 12 + qp:13 + qp], rsc[:, 8 + qp:9 + qp], 1.0 / D, 1e-5,
            ALU.mult, ALU.add),
            reads=[("rsc", 8 + qp)], writes=[("rsc", 12 + qp)])
        op("vector", lambda h, qp=qp: h.tensor_tensor(
            rsc[:, 8 + qp:9 + qp], rsc[:, 10 + qp:11 + qp],
            rsc[:, 10 + qp:11 + qp], ALU.mult),
            reads=[("rsc", 10 + qp)], writes=[("rsc", 8 + qp)])
        op("vector", lambda h, qp=qp: h.tensor_tensor(
            rsc[:, 12 + qp:13 + qp], rsc[:, 12 + qp:13 + qp],
            rsc[:, 8 + qp:9 + qp], ALU.subtract),
            reads=[("rsc", 8 + qp), ("rsc", 12 + qp)], writes=[("rsc", 12 + qp)])
        op("scalar", lambda h, qp=qp: h.activation(
            rsc[:, 12 + qp:13 + qp], rsc[:, 12 + qp:13 + qp], AF.Ln),
            reads=[("rsc", 12 + qp)], writes=[("rsc", 12 + qp)])
        op("scalar", lambda h, qp=qp: h.activation(
            rsc[:, 12 + qp:13 + qp], rsc[:, 12 + qp:13 + qp], AF.Exp,
            scale=-0.5),
            reads=[("rsc", 12 + qp)], writes=[("rsc", 12 + qp)])
        op("vector", lambda h, qp=qp: h.tensor_tensor(
            rsc[:, 10 + qp:11 + qp], rsc[:, 10 + qp:11 + qp],
            rsc[:, 12 + qp:13 + qp], ALU.mult),
            reads=[("rsc", 12 + qp), ("rsc", 10 + qp)], writes=[("rsc", 10 + qp)])
        op("vector", lambda h, qp=qp: h.tensor_scalar(
            rsc[:, 10 + qp:11 + qp], rsc[:, 10 + qp:11 + qp], -1.0, None,
            ALU.mult),
            reads=[("rsc", 10 + qp)], writes=[("rsc", 10 + qp)])
        op("scalar", lambda h, qp=qp: h.activation(
            ln2row[:, qp, :], x1row[:, qp, :], AF.Identity,
            bias=rsc[:, 10 + qp:11 + qp], scale=rsc[:, 12 + qp:13 + qp]),
            reads=[("x1row", qp), ("rsc", 10 + qp), ("rsc", 12 + qp)],
            writes=[("ln2row", qp)])

    def tail_b1(j, q):
        for g in range(2):
            for dd in range(4):
                dc = g * 4 + dd
                op("tensor", lambda h, dc=dc, dd=dd, qp=q % 2: h.transpose(
                    pD[0:128, 1, 256 + dd * 64:256 + (dd + 1) * 64].bitcast(BF16),
                    ln2row[:, qp, dc * 128:(dc + 1) * 128], idenb[:]),
                    reads=[("ln2row", q % 2), "idenb"], writes=[("pD", 1)],
                    inc=(dd == 3))
            op("scalar", lambda h, g=g: h.copy(
                ln2d[:, g * 512:(g + 1) * 512].rearrange("p (a b) -> p a b", a=4),
                pD[0:128, 1, 256:512].bitcast(BF16).rearrange("p (a b) -> p a b", a=4)),
                reads=[("pD", 1)], writes=["ln2d"])

    def tail_b2(j, q):
        for dc in range(DC):
            op("tensor", lambda h, dc=dc: h.matmul(
                pD[0:E, 1, 0:128],
                wr_b[:, dc * E:(dc + 1) * E],
                ln2d[:, dc * 128:(dc + 1) * 128],
                start=(dc == 0), stop=(dc == DC - 1)),
                reads=["wr_b", "ln2d"], writes=[("pD", 1)],
                inc=(dc == DC - 1))
        op("scalar", lambda h: h.activation(
            logits_sb[:], pD[0:E, 1, 0:128],
            AF.Identity, bias=br_col[:]),
            reads=[("pD", 1), "br_col"], writes=["logits_sb"])
        op("tensor", lambda h: h.transpose(
            pD[:, 1, 128:136],
            logits_sb[:], iden[0:E, 0:E]),
            reads=["logits_sb", "iden"], writes=[("pD", 1)])

    def tail_b3a(j, q):
        cidx = j * 4 + q
        op("vector", lambda h: h.tensor_reduce(
            rsc[:, 0:1], pD[:, 1, 128:136], AX.X, ALU.max, negate=True),
            reads=[("pD", 1)], writes=[("rsc", 0)])
        op("scalar", lambda h: h.activation(
            lnTg[:, 2, 0:8], pD[:, 1, 128:136], AF.Exp,
            bias=rsc[:, 0:1], accum_out=rsc[:, 1:2]),
            reads=[("pD", 1), ("rsc", 0)], writes=[("lnTg", 2), ("rsc", 1)])
        op("vector", lambda h: h.reciprocal(rsc[:, 2:3], rsc[:, 1:2]),
           reads=[("rsc", 1)], writes=[("rsc", 2)])
        op("vector", lambda h: h.tensor_scalar(
            rsc2[:, 0, :], pD[:, 1, 128:136],
            rsc[:, 0:1], 0.0, ALU.add, ALU.is_ge),
            reads=[("pD", 1), ("rsc", 0)], writes=[("rsc2", 0)])
        op("vector", lambda h: h.tensor_scalar(
            rsc2[:, 1, :], rsc2[:, 0, :], -1e9, 1e9, ALU.mult, ALU.add),
            reads=[("rsc2", 0)], writes=[("rsc2", 1)])
        op("vector", lambda h: h.tensor_tensor(
            rsc2[:, 1, :], rsc2[:, 1, :], iotaE[:], ALU.add),
            reads=[("rsc2", 1), "iotaE"], writes=[("rsc2", 1)])
        op("vector", lambda h: h.tensor_reduce(
            rsc[:, 3:4], rsc2[:, 1, :], AX.X, ALU.min),
            reads=[("rsc2", 1)], writes=[("rsc", 3)])
        op("vector", lambda h: h.tensor_scalar(
            rsc2[:, 2, :], iotaE[:], rsc[:, 3:4], None, ALU.is_equal),
            reads=["iotaE", ("rsc", 3)], writes=[("rsc2", 2)])

    def tail_b3b(j, q):
        cidx = j * 4 + q
        op("tensor", lambda h: h.matmul(
            pD[0:128, 1, 136:144], ones1[:], rc8[:], start=True, stop=False),
            reads=["ones1", "rc8"], writes=[("pD", 1)])
        op("tensor", lambda h: h.matmul(
            pD[0:128, 1, 136:144], tri_f[:], rsc2[:, 2, :], start=False, stop=True),
            reads=["tri_f", ("rsc2", 2)], writes=[("pD", 1)])
        op("tensor", lambda h: h.matmul(
            pD[0:1, 1, 144:152], onescol[:], rsc2[:, 2, :], start=True, stop=True),
            reads=["onescol", ("rsc2", 2)], writes=[("pD", 1)])
        op("vector", lambda h: h.tensor_tensor(
            rsc2[:, 3, :], rsc2[:, 2, :], pD[0:128, 1, 136:144], ALU.mult),
            reads=[("rsc2", 2), ("pD", 1)], writes=[("rsc2", 3)])
        op("vector", lambda h: h.tensor_reduce(
            rsc[:, 4:5], rsc2[:, 3, :], AX.X, ALU.add),
            reads=[("rsc2", 3)], writes=[("rsc", 4)])
        op("vector", lambda h: h.tensor_scalar(
            rsc[:, 5:6], rsc[:, 4:5], 1.0, -1.0, ALU.mult, ALU.add),
            reads=[("rsc", 4)], writes=[("rsc", 5)])
        op("vector", lambda h, cidx=cidx: h.tensor_scalar(
            pos_sb[:, cidx:cidx + 1], rsc[:, 3:4], float(T), rsc[:, 5:6],
            ALU.mult, ALU.add),
            reads=[("rsc", 3), ("rsc", 5)], writes=["pos_sb"])
        op("vector", lambda h: h.tensor_tensor(
            rc8[:], rc8[:], pD[0:1, 1, 144:152], ALU.add),
            reads=["rc8", ("pD", 1)], writes=["rc8"])
        op("vector", lambda h, cidx=cidx: h.tensor_copy(
            posU[:, 0:1], pos_sb[:, cidx:cidx + 1]),
            reads=["pos_sb"], writes=[("posU2", 0)])
        op("vector", lambda h, cidx=cidx: h.tensor_scalar(
            aux_sb[:, 0:1], iotaQ[:, 0:1], 1.0, float(cidx * 128),
            ALU.mult, ALU.add),
            reads=["iotaQ"], writes=["aux_sb"])
        op("vector", lambda h: h.tensor_copy(aux_sb[:, 1:2], rsc[:, 2:3]),
           reads=[("rsc", 2)], writes=["aux_sb"])
        dma("gpsimd", slotaux, aux_sb[:],
            reads=["aux_sb", ("posU2", 0)], writes=[("slotaux",)],
            indirect=("scatter", posU[:, 0:1]))
        dma("gpsimd", slotbuf, ln2row[:, q % 2, :],
            reads=[("ln2row", q % 2), ("posU2", 0)], writes=[("slotbuf",)],
            indirect=("scatter", posU[:, 0:1]))

    tail_stream = []

    def push_tail(j):
        def bchain(q):
            return [lambda q=q: tail_b1(j, q), lambda q=q: tail_b2(j, q),
                    lambda q=q: tail_b3a(j, q), lambda q=q: tail_b3b(j, q)]
        for q in range(4):
            tail_stream.append(lambda q=q: tail_a1(j, q))
            tail_stream.append(lambda q=q: tail_a2(j, q))
            if q >= 1:
                tail_stream.extend(bchain(q - 1))
        tail_stream.extend(bchain(3))

    def flush_tail(n):
        for _ in range(min(n, len(tail_stream))):
            tail_stream.pop(0)()

    emit_ln_p1ab(0)
    emit_ln_p1c()
    emit_ln_p2(0)
    for j in range(NT2):
        par = j % 2
        seq_t2 = j % T2_PER_SEQ
        if j > 0:
            push_tail(j - 1)
        for q in range(T2 // CH):
            qpar = q % 2
            for half in range(2):
                for dc in range(DC):
                    op("tensor", lambda h, q=q, half=half, dc=dc, par=par: h.matmul(
                        pA[:, half, :],
                        lnx[:, par, dc, q * 128:(q + 1) * 128],
                        wbA[:, dc * 1024 + half * 512: dc * 1024 + (half + 1) * 512],
                        start=(dc == 0), stop=(dc == DC - 1)),
                        reads=[("lnx", par, dc), ("wbA", 0)], writes=[("pA", half)],
                        inc=(dc == DC - 1))
                op("vector", lambda h, half=half: h.tensor_tensor(
                    xsq[:, half, :],
                    pA[:, half, :], lamNegPow[:, half * 512:(half + 1) * 512],
                    ALU.mult),
                    reads=[("pA", half), "lamNegPow"], writes=[("xsq", half)])
            op("vector", lambda h, qpar=qpar: h.tensor_tensor(
                gub[:, qpar, :], xsq[:].rearrange("p a b -> p (a b)"),
                gbLam[:], ALU.add),
                reads=[("xsq", 0), ("xsq", 1), "gbLam"], writes=[("gub", qpar)])
            for k in range(DC):
                op("tensor", lambda h, k=k, qpar=qpar: h.matmul(
                    pC[:, k * 128:(k + 1) * 128],
                    gub[:, qpar, k * 128:(k + 1) * 128], tri_b[:],
                    start=True, stop=True),
                    reads=[("gub", qpar), "tri_b"], writes=[("pC", k // 4)])
            first = (seq_t2 == 0 and q == 0)
            hsel = hsb[:].rearrange("p a (n b) -> p a n b", b=CH)
            if not first:
                hinit_bc = hinit[:].unsqueeze(2).broadcast_to([128, DC, CH])
                op("vector", lambda h, hinit_bc=hinit_bc: h.tensor_tensor(
                    scr[:, 0:1024].rearrange("p (a b) -> p a b", a=DC),
                    pC[:].rearrange("p (a b) -> p a b", a=DC), hinit_bc, ALU.add),
                    reads=[("pC", 0), ("pC", 1), "hinit"], writes=[("scr", 0)])
                op("vector", lambda h, q=q, hsel=hsel: h.tensor_tensor(
                    hsel[:, :, q, :],
                    scr[:, 0:1024].rearrange("p (a b) -> p a b", a=DC),
                    lamPow[:], ALU.mult),
                    reads=[("scr", 0), "lamPow"], writes=["hsb"])
            else:
                op("vector", lambda h, q=q, hsel=hsel: h.tensor_tensor(
                    hsel[:, :, q, :],
                    pC[:].rearrange("p (a b) -> p a b", a=DC),
                    lamPow[:], ALU.mult),
                    reads=[("pC", 0), ("pC", 1), "lamPow"], writes=["hsb"])
            op("vector", lambda h, q=q: h.tensor_tensor(
                hinit[:],
                hsb[:, :, q * 128 + 127:q * 128 + 128].squeeze(2),
                lamC64[:], ALU.mult),
                reads=["hsb", "lamC64"], writes=["hinit"])
            flush_tail(4)
        for k in range(DC):
            if k == 1 and j + 1 < NT2:
                emit_ln_p1ab(1 - par)
            if k == 2:
                flush_tail(4)
            if k == 4:
                flush_tail(4)
                if j + 1 < NT2:
                    emit_ln_p1c()
            if k == 5 and j + 1 < NT2:
                emit_ln_p2(1 - par)
            for dc in range(DC):
                op("tensor", lambda h, k=k, dc=dc: h.matmul(
                    pB[:, k % 2, :],
                    wbA[:, 8192 + dc * 1024 + k * 128: 8192 + dc * 1024 + (k + 1) * 128],
                    hsb[:, dc, :], start=(dc == 0), stop=(dc == DC - 1)),
                    reads=[("wbA", 1), "hsb"], writes=[("pB", k % 2)],
                    inc=(dc == DC - 1))
            op("scalar", lambda h, k=k, par=par: h.activation(
                x1d[par][:, k, :], pB[:, k % 2, :],
                AF.Identity, bias=boCol[:, k:k + 1]),
                reads=[("pB", k % 2), "boCol"], writes=[X1K(par, k)])
            if k % 2 == 1:
                op("vector", lambda h, k=k, par=par: h.tensor_tensor(
                    x1d[par][:, k - 1:k + 1, :], x1d[par][:, k - 1:k + 1, :],
                    lnx[:, par, k - 1:k + 1, :], ALU.add),
                    reads=[X1K(par, k - 1), X1K(par, k),
                           ("lnx", par, k - 1), ("lnx", par, k)],
                    writes=[X1K(par, k - 1), X1K(par, k)])
                op("vector", lambda h, k=k, par=par: h.tensor_tensor(
                    x1d[par][:, k - 1:k + 1, :], x1d[par][:, k - 1:k + 1, :],
                    bigX[:, par, k - 1:k + 1, :], ALU.add),
                    reads=[X1K(par, k - 1), X1K(par, k),
                           BX(par, k - 1), BX(par, k)],
                    writes=[X1K(par, k - 1), X1K(par, k)])
        if j + 2 < NT2:
            load_x(j + 2, par)
        if j >= 2:
            for _ in range(2):
                if conv_emits:
                    conv_emits.pop(0)()

    while conv_emits:
        conv_emits.pop(0)()
    _conv_total = cnt["gd3"]
    for e in range(E):
        regions[("w1s", e)]["w"] = ("gd3", _conv_total)
        regions[("w2s", e)]["w"] = ("gd3", _conv_total)

    push_tail(NT2 - 1)
    flush_tail(10 ** 9)

    op("vector", lambda h: h.tensor_tensor(
        grow[:, 0:128].rearrange("o (e k) -> o e k", e=E),
        rc8[:].unsqueeze(2).broadcast_to([1, E, 16]),
        thresh16[:].rearrange("o (e k) -> o e k", e=E), ALU.is_gt),
        reads=["rc8", "thresh16"], writes=["grow"])
    op("vector", lambda h: h.tensor_reduce(
        finrow[:, 0:E], grow[:, 0:128].rearrange("o (e k) -> o e k", e=E),
        AX.X, ALU.add),
        reads=["grow"], writes=["finrow"])
    op("vector", lambda h: h.tensor_scalar(
        finrow[:, 0:E], finrow[:, 0:E], float(T2), None, ALU.mult),
        reads=["finrow"], writes=["finrow"])
    op("vector", lambda h: h.tensor_scalar(
        startrow[:, 0:1], finrow[:, 0:1], 0.0, None, ALU.mult),
        reads=["finrow"], writes=["startrow"])
    for e in range(E):
        op("vector", lambda h, e=e: h.tensor_tensor(
            startrow[:, e + 1:e + 2], startrow[:, e:e + 1],
            finrow[:, e:e + 1], ALU.add),
            reads=["startrow", "finrow"], writes=["startrow"])
    op("vector", lambda h: h.tensor_tensor(
        grow[:, 0:NTILES * E].rearrange("o (t e) -> o t e", e=E),
        tau24[:].rearrange("o (t e) -> o t e", e=E),
        startrow[:, 0:E].unsqueeze(1).broadcast_to([1, NTILES, E]), ALU.is_ge),
        reads=["tau24", "startrow"], writes=["grow"])
    op("vector", lambda h: h.tensor_reduce(
        finrow[:, NTILES:2 * NTILES], grow[:, 0:NTILES * E].rearrange("o (t e) -> o t e", e=E),
        AX.X, ALU.add),
        reads=["grow"], writes=["finrow"])
    op("vector", lambda h: h.tensor_scalar(
        finrow[:, NTILES:2 * NTILES], finrow[:, NTILES:2 * NTILES], 1.0, -1.0,
        ALU.mult, ALU.add),
        reads=["finrow"], writes=["finrow"])
    op("vector", lambda h: h.tensor_tensor(
        grow[:, 0:NTILES * E].rearrange("o (t e) -> o t e", e=E),
        iotaE24[:].rearrange("o (t e) -> o t e", e=E),
        finrow[:, NTILES:2 * NTILES].unsqueeze(2).broadcast_to([1, NTILES, E]),
        ALU.is_equal),
        reads=["iotaE24", "finrow"], writes=["grow"])
    op("vector", lambda h: h.tensor_tensor(
        grow[:, 0:NTILES * E].rearrange("o (t e) -> o t e", e=E),
        grow[:, 0:NTILES * E].rearrange("o (t e) -> o t e", e=E),
        startrow[:, 0:E].unsqueeze(1).broadcast_to([1, NTILES, E]), ALU.mult),
        reads=["grow", "startrow"], writes=["grow"])
    op("vector", lambda h: h.tensor_reduce(
        finrow[:, 2 * NTILES:3 * NTILES],
        grow[:, 0:NTILES * E].rearrange("o (t e) -> o t e", e=E), AX.X, ALU.add),
        reads=["grow"], writes=["finrow"])
    op("vector", lambda h: h.tensor_tensor(
        grow[:, 0:NTILES * E].rearrange("o (t e) -> o t e", e=E),
        iotaE24[:].rearrange("o (t e) -> o t e", e=E),
        finrow[:, NTILES:2 * NTILES].unsqueeze(2).broadcast_to([1, NTILES, E]),
        ALU.is_equal),
        reads=["iotaE24", "finrow"], writes=["grow"])
    op("vector", lambda h: h.tensor_tensor(
        grow[:, 0:NTILES * E].rearrange("o (t e) -> o t e", e=E),
        grow[:, 0:NTILES * E].rearrange("o (t e) -> o t e", e=E),
        rc8[:].unsqueeze(1).broadcast_to([1, NTILES, E]), ALU.mult),
        reads=["grow", "rc8"], writes=["grow"])
    op("vector", lambda h: h.tensor_reduce(
        finrow[:, 0:NTILES],
        grow[:, 0:NTILES * E].rearrange("o (t e) -> o t e", e=E), AX.X, ALU.add),
        reads=["grow"], writes=["finrow"])
    op("vector", lambda h: h.tensor_scalar(
        grow[:, 0:NTILES], finrow[:, NTILES:2 * NTILES], float(T), None, ALU.mult),
        reads=["finrow"], writes=["grow"])
    op("vector", lambda h: h.tensor_tensor(
        finrow[:, 0:NTILES], finrow[:, 0:NTILES], grow[:, 0:NTILES], ALU.add),
        reads=["finrow", "grow"], writes=["finrow"])
    op("vector", lambda h: h.tensor_scalar(
        finrow[:, 3 * NTILES:4 * NTILES], finrow[:, NTILES:2 * NTILES],
        float(T), None, ALU.mult),
        reads=["finrow"], writes=["finrow"])
    op("vector", lambda h: h.tensor_tensor(
        finrow[:, 3 * NTILES:4 * NTILES], finrow[:, 3 * NTILES:4 * NTILES],
        tau24[:].rearrange("o (t e) -> o t e", e=E)[:, :, 0], ALU.add),
        reads=["finrow", "tau24"], writes=["finrow"])
    op("vector", lambda h: h.tensor_tensor(
        finrow[:, 3 * NTILES:4 * NTILES], finrow[:, 3 * NTILES:4 * NTILES],
        finrow[:, 2 * NTILES:3 * NTILES], ALU.subtract),
        reads=["finrow"], writes=["finrow"])

    def tile_front(tau):
        tp = tau % 2
        op("tensor", lambda h, tau=tau: h.matmul(
            pC[:, 0:1], ones1[:], finrow[:, NTILES + tau:NTILES + tau + 1],
            start=True, stop=True),
            reads=["ones1", "finrow"], writes=[("pC", 0)])
        op("tensor", lambda h, tau=tau: h.matmul(
            pC[:, 1:2], ones1[:], finrow[:, 3 * NTILES + tau:3 * NTILES + tau + 1],
            start=True, stop=True),
            reads=["ones1", "finrow"], writes=[("pC", 0)])
        op("tensor", lambda h, tau=tau: h.matmul(
            pC[:, 2:3], ones1[:], finrow[:, tau:tau + 1],
            start=True, stop=True),
            reads=["ones1", "finrow"], writes=[("pC", 0)])
        if tau > 0:
            op("tensor", lambda h, tau=tau: h.matmul(
                pC[:, 3:4], ones1[:], finrow[:, NTILES + tau - 1:NTILES + tau],
                start=True, stop=True),
                reads=["ones1", "finrow"], writes=[("pC", 0)])
        op("vector", lambda h: h.tensor_scalar(
            idxs_f[:, tp, 0:1], pC[:, 0:1], 128.0, iotaQ[:, 0:1], ALU.mult, ALU.add),
            reads=[("pC", 0), "iotaQ"], writes=[("idxs_f", tp)])
        if tau > 0:
            op("vector", lambda h: h.tensor_scalar(
                idxs_f[:, tp, 11:12], pC[:, 3:4], 1.0, None, ALU.mult),
                reads=[("pC", 0)], writes=[("idxs_f", tp)])
            op("vector", lambda h: h.tensor_tensor(
                idxs_f[:, tp, 11:12], idxs_f[:, tp, 11:12], pC[:, 0:1],
                ALU.is_equal),
                reads=[("idxs_f", tp), ("pC", 0)], writes=[("idxs_f", tp)])
            op("vector", lambda h: h.tensor_scalar(
                idxs_f[:, tp, 11:12], idxs_f[:, tp, 11:12], 16384.0, None,
                ALU.mult),
                reads=[("idxs_f", tp)], writes=[("idxs_f", tp)])
            op("vector", lambda h: h.tensor_tensor(
                idxs_f[:, tp, 0:1], idxs_f[:, tp, 0:1], idxs_f[:, tp, 11:12],
                ALU.add),
                reads=[("idxs_f", tp)], writes=[("idxs_f", tp)])
        for q in range(4):
            op("vector", lambda h, q=q: h.tensor_scalar(
                idxs_f[:, tp, 9:10], pC[:, 1:2], 1.0, iotaQ[:, q:q + 1],
                ALU.mult, ALU.add),
                reads=[("pC", 0), "iotaQ"], writes=[("idxs_f", tp)])
            op("vector", lambda h, q=q: h.tensor_scalar(
                idxs_f[:, tp, 10:11], idxs_f[:, tp, 9:10], float(E * T), None,
                ALU.is_lt),
                reads=[("idxs_f", tp)], writes=[("idxs_f", tp)])
            op("vector", lambda h, q=q: h.tensor_tensor(
                idxs_f[:, tp, 1 + q:2 + q], idxs_f[:, tp, 9:10], idxs_f[:, tp, 10:11],
                ALU.mult),
                reads=[("idxs_f", tp)], writes=[("idxs_f", tp)])
            op("vector", lambda h, q=q: h.tensor_scalar(
                idxs_f[:, tp, 11:12], idxs_f[:, tp, 10:11], -1.0, 1.0, ALU.mult, ALU.add),
                reads=[("idxs_f", tp)], writes=[("idxs_f", tp)])
            op("vector", lambda h, q=q, tau=tau: h.tensor_scalar(
                idxs_f[:, tp, 10:11], iotaQ[:, q:q + 1], 1.0,
                float(E * T + tau * T2), ALU.mult, ALU.add),
                reads=["iotaQ"], writes=[("idxs_f", tp)])
            op("vector", lambda h, q=q: h.tensor_tensor(
                idxs_f[:, tp, 11:12], idxs_f[:, tp, 11:12], idxs_f[:, tp, 10:11], ALU.mult),
                reads=[("idxs_f", tp)], writes=[("idxs_f", tp)])
            op("vector", lambda h, q=q: h.tensor_tensor(
                idxs_f[:, tp, 1 + q:2 + q], idxs_f[:, tp, 1 + q:2 + q], idxs_f[:, tp, 11:12],
                ALU.add),
                reads=[("idxs_f", tp)], writes=[("idxs_f", tp)])
            op("vector", lambda h, q=q: h.tensor_scalar(
                idxs_f[:, tp, 10:11], idxs_f[:, tp, 9:10], pC[:, 2:3], None, ALU.is_lt),
                reads=[("idxs_f", tp), ("pC", 0)], writes=[("idxs_f", tp)])
            op("vector", lambda h, q=q: h.tensor_tensor(
                idxs_f[:, tp, 5 + q:6 + q], idxs_f[:, tp, 9:10], idxs_f[:, tp, 10:11],
                ALU.mult),
                reads=[("idxs_f", tp)], writes=[("idxs_f", tp)])
            op("vector", lambda h, q=q: h.tensor_scalar(
                idxs_f[:, tp, 11:12], idxs_f[:, tp, 10:11], float(-ZROW), float(ZROW),
                ALU.mult, ALU.add),
                reads=[("idxs_f", tp)], writes=[("idxs_f", tp)])
            op("vector", lambda h, q=q: h.tensor_tensor(
                idxs_f[:, tp, 5 + q:6 + q], idxs_f[:, tp, 5 + q:6 + q], idxs_f[:, tp, 11:12],
                ALU.add),
                reads=[("idxs_f", tp)], writes=[("idxs_f", tp)])
        op("vector", lambda h: h.tensor_copy(idxs_u[:, tp, 0:12], idxs_f[:, tp, 0:12]),
           reads=[("idxs_f", tp)],
           writes=[("idxs_u", tp), ("tokU", tp, 0), ("tokU", tp, 1)])

    def tile_w1(tau):
        tp = tau % 2
        dma("gpsimd", wbA[:], w1s, reads=[("idxs_u", tp)] + [("w1s", e) for e in range(E)],
            writes=[("wbA", 0), ("wbA", 1)],
            indirect=("gather", idxs_u[:, tp, 0:1], breg), lane="gd")
        dma("gpsimd", b1tile[:], b1cols, reads=[("idxs_u", tp), "b1cols"],
            writes=["b1tile"], indirect=("gather", idxs_u[:, tp, 0:1], breg),
            lane="gd5")

    def tile_w2(tau):
        tp = tau % 2
        dma("gpsimd", wbB[:], w2s, reads=[("idxs_u", tp)] + [("w2s", e) for e in range(E)],
            writes=[("wbB",)] + ALL_X1K,
            indirect=("gather", idxs_u[:, tp, 0:1], breg), lane="gd2")
        dma("gpsimd", b2tile[:], b2cols, reads=[("idxs_u", tp), "b2cols"],
            writes=["b2tile"], indirect=("gather", idxs_u[:, tp, 0:1], breg),
            lane="gd5")

    def tile_tok_gather(tau):
        tp = tau % 2
        for q in range(4):
            qp = q
            dma("gpsimd", lnTg[:, qp, :], slotbuf,
                reads=[("idxs_u", tp), ("slotbuf",), ("slotzero",)], writes=[("lnTg", qp)],
                indirect=("gather", idxs_u[:, tp, 5 + q:6 + q]),
                lane=("gd3", "gd4")[q % 2])

    def tile_tokens(tau):
        tp = tau % 2
        banks = [pD[0:128, 0, 0:512], pD[0:128, 1, 0:512],
                 pC[:, 512:1024], pC[:, 0:512]]
        keys = [("pD", 0), ("pD", 1), ("pC", 1), ("pC", 0)]
        for q in range(4):
            for dc in range(DC):
                op("tensor", lambda h, dc=dc, q=q, banks=banks: h.transpose(
                    banks[q][:, dc * 64:(dc + 1) * 64].bitcast(BF16),
                    lnTg[:, q, dc * 128:(dc + 1) * 128], idenb[:]),
                    reads=[("lnTg", q), "idenb"], writes=[keys[q]],
                    inc=(dc == DC - 1))
        for q in range(4):
            op("scalar", lambda h, q=q, tp=tp, banks=banks: h.copy(
                lnx[:, tp, :, q * 128:(q + 1) * 128],
                banks[q].bitcast(BF16).rearrange("p (a b) -> p a b", a=DC)),
                reads=[keys[q]], writes=[("lnx", tp, dc) for dc in range(DC)])

    def tile_mm1(tau):
        tp = tau % 2
        for ht in range(HT):
            hp = ht % 2
            for dc in range(DC):
                op("tensor", lambda h, ht=ht, dc=dc, hp=hp, tp=tp: h.matmul(
                    pA[:, hp, :],
                    wbA[:, dc * H + ht * 128: dc * H + (ht + 1) * 128],
                    lnx[:, tp, dc, :], start=(dc == 0), stop=(dc == DC - 1)),
                    reads=[("wbA", 0), ("wbA", 1), ("lnx", tp, dc)],
                    writes=[("pA", hp)], inc=(dc == DC - 1))
            op("scalar", lambda h, ht=ht, hp=hp: h.activation(
                hm[:, ht, :], pA[:, hp, :], AF.Relu,
                bias=b1tile[:, ht:ht + 1]),
                reads=[("pA", hp), "b1tile"], writes=[("hm", ht)])

    def tile_mm2(tau):
        for dt in range(DC):
            dp = dt % 2
            for ht in range(HT):
                op("tensor", lambda h, dt=dt, ht=ht, dp=dp: h.matmul(
                    pB[:, dp, :],
                    wbB[:, ht * D + dt * 128: ht * D + (dt + 1) * 128],
                    hm[:, ht, :], start=(ht == 0), stop=(ht == HT - 1)),
                    reads=[("wbB",), ("hm", ht)], writes=[("pB", dp)],
                    inc=(ht == HT - 1))
            op("scalar", lambda h, dt=dt, dp=dp: h.activation(
                hsb[:, dt, :], pB[:, dp, :], AF.Identity,
                bias=b2tile[:, dt:dt + 1]),
                reads=[("pB", dp), "b2tile"], writes=["hsb"])

    def tile_y(tau):
        tp = tau % 2
        for q in range(4):
            qp = q % 2
            for dt in range(DC):
                op("tensor", lambda h, dt=dt, q=q: h.transpose(
                    pD[0:128, 1, dt * 64:(dt + 1) * 64].bitcast(BF16),
                    hsb[:, dt, q * 128:(q + 1) * 128], idenb[:]),
                    reads=["hsb", "idenb"], writes=[("pD", 1)],
                    inc=(dt == DC - 1))
            op("scalar", lambda h, q=q, qp=qp: h.copy(
                yrow[:, qp, :],
                pD[0:128, 1, 0:512].bitcast(BF16)),
                reads=[("pD", 1)], writes=[("yrow", qp)])
            dma("gpsimd", auxg[:, qp, :], slotaux,
                reads=[("idxs_u", tp), ("slotaux",), ("slotauxzero",)],
                writes=[("auxg", qp)],
                indirect=("gather", idxs_u[:, tp, 5 + q:6 + q]), lane="gd5")
            op("vector", lambda h, qp=qp, tp=tp: h.tensor_scalar(
                idxs_u[:, tp, 9 + qp:10 + qp], auxg[:, qp, 0:1],
                float(T - 1), None, ALU.min),
                reads=[("auxg", qp)], writes=[("tokU", tp, qp)])
            op("vector", lambda h, qp=qp, tp=tp: h.tensor_copy(
                idxs_u[:, tp, 12 + qp:13 + qp], auxg[:, qp, 0:1]),
                reads=[("auxg", qp)], writes=[("tokR", tp, qp)])
            x1g = bigX[:, qp, 2 * (q // 2):2 * (q // 2) + 2, :].rearrange(
                "p a b -> p (a b)")
            xk = [BX(qp, 2 * (q // 2)), BX(qp, 2 * (q // 2) + 1)]
            dma("gpsimd", x1g, x1tok,
                reads=[("tokU", tp, qp), ("x1tok",)],
                writes=[("x1g", q)] + xk,
                indirect=("gather", idxs_u[:, tp, 9 + qp:10 + qp]),
                lane=("gd6", "gd7")[qp])
            op("scalar", lambda h, q=q, qp=qp: h.activation(
                ln2row[:, qp, :], yrow[:, qp, :], AF.Identity,
                scale=auxg[:, qp, 1:2]),
                reads=[("yrow", qp), ("auxg", qp)], writes=[("ln2row", qp)])
            op("vector", lambda h, q=q, qp=qp, x1g=x1g: h.tensor_tensor(
                x1g, x1g, ln2row[:, qp, :], ALU.add),
                reads=[("x1g", q), ("ln2row", qp)],
                writes=[("x1g", q)])
            dma("gpsimd", out_t, x1g,
                reads=[("x1g", q), ("tokR", tp, qp)], writes=[("out", 0)],
                indirect=("scatter", idxs_u[:, tp, 12 + qp:13 + qp], breg),
                lane="gd8")

    tile_front(0)
    tile_w1(0)
    tile_w2(0)
    tile_tok_gather(0)
    tile_tokens(0)
    for tau in range(NTILES):
        tile_mm1(tau)
        if tau + 1 < NTILES:
            tile_front(tau + 1)
            tile_w1(tau + 1)
            tile_tok_gather(tau + 1)
        tile_mm2(tau)
        tile_y(tau)
        if tau + 1 < NTILES:
            tile_tokens(tau + 1)
            tile_w2(tau + 1)

    final_wait("sync", ["sd", "sd2", "sd3", "sd4", "gd", "gd2", "gd3", "gd4",
                        "gd5", "gd6", "gd7", "gd8"])

    @block.sync
    def _(sync):
        for fn in Q["sync"]:
            fn(sync)

    @block.tensor
    def _(tensor):
        for fn in Q["tensor"]:
            fn(tensor)

    @block.vector
    def _(vector):
        for fn in Q["vector"]:
            fn(vector)

    @block.scalar
    def _(scalar):
        for fn in Q["scalar"]:
            fn(scalar)

    @block.gpsimd
    def _(gpsimd):
        gpsimd.reg_mov(breg, T - 1)
        for fn in Q["gpsimd"]:
            fn(gpsimd)

    ctx.close()
    return nc


def make_inputs(core_x, params, NTILES):
    T = core_x.shape[0]
    tau = (np.arange(NTILES, dtype=np.float32) * T2)[:, None]
    d = {
        "xt": np.ascontiguousarray(core_x.T),
        "w_in_t": np.ascontiguousarray(params["W_in"].T),
        "w_out_t": np.ascontiguousarray(params["W_out"].T),
        "lam_log": params["lambda_log"].reshape(1, D),
        "gamma_d": params["gamma"].reshape(1, D),
        "b_in_d": params["b_in"].reshape(1, D),
        "b_out_d": params["b_out"].reshape(1, D),
        "wr_d": params["Wr"],
        "br_d": params["br"].reshape(E, 1),
        "w1_d": params["W1"],
        "b1_d": params["b1"],
        "w2_d": params["W2"],
        "b2_d": params["b2"],
        "tri_d": np.triu(np.ones((CH, CH), np.float32)),
        "iden_d": np.eye(128, dtype=np.float32),
        "iotaE_d": np.broadcast_to(np.arange(E, dtype=np.float32), (128, E)),
        "negsidx_d": (63.0 - np.arange(128, dtype=np.float32)).reshape(128, 1),
        "tix_d": (np.arange(128, dtype=np.float32) - 63.0).reshape(1, 128),
        "ones1_d": np.ones((1, 128), np.float32),
        "onescol_d": np.ones((128, 1), np.float32),
        "zrow_d": np.zeros((1, E), np.float32),
        "zrowD_d": np.zeros((1, D), np.float32),
        "zaux_d": np.array([[float(core_x.shape[0]), 0.0]], np.float32),
        "iotaQ_d": (np.arange(128, dtype=np.float32)[:, None]
                    + 128.0 * np.arange(4, dtype=np.float32)[None, :]),
        "thresh_d": np.broadcast_to(
            (np.arange(16, dtype=np.float32) * T2)[None, :], (E, 16)
        ).reshape(1, E * 16),
        "tau_d": np.broadcast_to(tau, (NTILES, E)).reshape(1, NTILES * E),
        "iotaE24_d": np.broadcast_to(
            np.arange(E, dtype=np.float32)[None, :], (NTILES, E)
        ).reshape(1, NTILES * E),
    }
    return {k: np.ascontiguousarray(np.asarray(v, dtype=np.float32)) for k, v in d.items()}


def kernel(L0_attn_x, L0_attn_lambda_log, L0_attn_gamma, L0_attn_W_in, L0_attn_b_in,
           L0_attn_W_out, L0_attn_b_out, L0_ffn_Wr, L0_ffn_br, L0_ffn_W1, L0_ffn_b1,
           L0_ffn_W2, L0_ffn_b2):
    x = np.asarray(L0_attn_x, dtype=np.float32)
    B, L, Dm = x.shape
    assert Dm == D and B % NCORES == 0
    n_seq = B // NCORES
    NTILES = (n_seq * L) // T2 + E
    params = {
        "lambda_log": np.asarray(L0_attn_lambda_log, np.float32),
        "gamma": np.asarray(L0_attn_gamma, np.float32),
        "W_in": np.asarray(L0_attn_W_in, np.float32),
        "b_in": np.asarray(L0_attn_b_in, np.float32),
        "W_out": np.asarray(L0_attn_W_out, np.float32),
        "b_out": np.asarray(L0_attn_b_out, np.float32),
        "Wr": np.asarray(L0_ffn_Wr, np.float32),
        "br": np.asarray(L0_ffn_br, np.float32),
        "W1": np.asarray(L0_ffn_W1, np.float32),
        "b1": np.asarray(L0_ffn_b1, np.float32),
        "W2": np.asarray(L0_ffn_W2, np.float32),
        "b2": np.asarray(L0_ffn_b2, np.float32),
    }
    nc = build_program(n_seq, L)
    in_maps = []
    for c in range(NCORES):
        core_x = x[c * n_seq:(c + 1) * n_seq].reshape(n_seq * L, D)
        in_maps.append(make_inputs(core_x, params, NTILES))
    res = run_bass_kernel_spmd(nc, in_maps, core_ids=list(range(NCORES)))
    outs = []
    for c in range(NCORES):
        o = np.asarray(res.results[c]["out"])
        outs.append(o.reshape(n_seq, L, D))
    return np.concatenate(outs, axis=0).astype(np.float32)


# version 30 (final)
# speedup vs baseline: 1.1058x; 1.0751x over previous
import sys
sys.path.insert(0, '/opt/trn_rl_repo')
import contextlib
import numpy as np
import concourse.bass as bass
import concourse.mybir as mybir
from concourse.bass_utils import run_bass_kernel_spmd

F32 = mybir.dt.float32
BF16 = mybir.dt.bfloat16
U32 = mybir.dt.uint32
F32R = mybir.dt.float32r
AF = mybir.ActivationFunctionType
ALU = mybir.AluOpType
AX = mybir.AxisListType

D = 1024
DC = 8
H = 2048
HT = 16
E = 8
T2 = 512
CH = 128
NCORES = 8
OP_TRACE = {"tensor": [], "note": None}


def build_program(n_seq, L):
    OP_TRACE["tensor"] = []
    T = n_seq * L
    NT2 = T // T2
    T2_PER_SEQ = L // T2
    NTILES = T // T2 + E
    SLOTROWS = E * T + NTILES * T2 + 1
    ZROW = SLOTROWS - 1
    NCHUNK = T // CH
    assert L % T2 == 0 and T2 % CH == 0

    nc = bass.Bass("TRN2", target_bir_lowering=False)

    xt = nc.dram_tensor("xt", [D, T], F32, kind="ExternalInput").ap()
    w_in_t = nc.dram_tensor("w_in_t", [D, D], F32, kind="ExternalInput").ap()
    w_out_t = nc.dram_tensor("w_out_t", [D, D], F32, kind="ExternalInput").ap()
    lam_log = nc.dram_tensor("lam_log", [1, D], F32, kind="ExternalInput").ap()
    gamma_d = nc.dram_tensor("gamma_d", [1, D], F32, kind="ExternalInput").ap()
    b_in_d = nc.dram_tensor("b_in_d", [1, D], F32, kind="ExternalInput").ap()
    b_out_d = nc.dram_tensor("b_out_d", [1, D], F32, kind="ExternalInput").ap()
    wr_d = nc.dram_tensor("wr_d", [D, E], F32, kind="ExternalInput").ap()
    br_d = nc.dram_tensor("br_d", [E, 1], F32, kind="ExternalInput").ap()
    w1_d = nc.dram_tensor("w1_d", [E, D, H], F32, kind="ExternalInput").ap()
    b1_d = nc.dram_tensor("b1_d", [E, H], F32, kind="ExternalInput").ap()
    w2_d = nc.dram_tensor("w2_d", [E, H, D], F32, kind="ExternalInput").ap()
    b2_d = nc.dram_tensor("b2_d", [E, D], F32, kind="ExternalInput").ap()
    tri_d = nc.dram_tensor("tri_d", [CH, CH], F32, kind="ExternalInput").ap()
    iden_d = nc.dram_tensor("iden_d", [128, 128], F32, kind="ExternalInput").ap()
    iotaE_d = nc.dram_tensor("iotaE_d", [128, E], F32, kind="ExternalInput").ap()
    negsidx_d = nc.dram_tensor("negsidx_d", [128, 1], F32, kind="ExternalInput").ap()
    tix_d = nc.dram_tensor("tix_d", [1, 128], F32, kind="ExternalInput").ap()
    ones1_d = nc.dram_tensor("ones1_d", [1, 128], F32, kind="ExternalInput").ap()
    onescol_d = nc.dram_tensor("onescol_d", [128, 1], F32, kind="ExternalInput").ap()
    zrow_d = nc.dram_tensor("zrow_d", [1, E], F32, kind="ExternalInput").ap()
    zrowD_d = nc.dram_tensor("zrowD_d", [1, D], F32, kind="ExternalInput").ap()
    iotaQ_d = nc.dram_tensor("iotaQ_d", [128, 4], F32, kind="ExternalInput").ap()
    thresh_d = nc.dram_tensor("thresh_d", [1, E * 16], F32, kind="ExternalInput").ap()
    tau_d = nc.dram_tensor("tau_d", [1, NTILES * E], F32, kind="ExternalInput").ap()
    iotaE24_d = nc.dram_tensor("iotaE24_d", [1, NTILES * E], F32, kind="ExternalInput").ap()
    zaux_d = nc.dram_tensor("zaux_d", [1, 2], F32, kind="ExternalInput").ap()
    out_t = nc.dram_tensor("out", [T, D], F32, kind="ExternalOutput").ap()
    slotaux = nc.dram_tensor("slotaux", [SLOTROWS, 2], F32, kind="Internal").ap()
    x1tok = nc.dram_tensor("x1tok", [T, D], F32, kind="Internal").ap()
    slotbuf = nc.dram_tensor("slotbuf", [SLOTROWS, D], BF16, kind="Internal").ap()
    w1s = nc.dram_tensor("w1s", [E * 128, DC * H], BF16, kind="Internal").ap()
    w2s = nc.dram_tensor("w2s", [E * 128, HT * D], BF16, kind="Internal").ap()
    b1cols = nc.dram_tensor("b1cols", [E * 128, HT], F32, kind="Internal").ap()
    b2cols = nc.dram_tensor("b2cols", [E * 128, DC], F32, kind="Internal").ap()

    ctx = contextlib.ExitStack()
    EN = ctx.enter_context
    wbA = EN(nc.sbuf_tensor("wbA", [128, 16384], BF16))
    wbB = EN(nc.sbuf_tensor("wbB", [128, 16384], BF16))
    hm = EN(nc.sbuf_tensor("hm", [128, HT, T2], BF16))
    bigX = EN(nc.sbuf_tensor("bigX", [128, 2, DC, T2], F32))
    xsq = EN(nc.sbuf_tensor("xsq", [128, 2, T2], F32))
    xb = EN(nc.sbuf_tensor("xb", [128, 2, T2], BF16))
    xsqb = EN(nc.sbuf_tensor("xsqb", [128, 2, T2], BF16))
    onescol_b = EN(nc.sbuf_tensor("onescol_b", [128, 1], BF16))
    lnx = EN(nc.sbuf_tensor("lnx", [128, 2, DC, T2], BF16))
    gub = EN(nc.sbuf_tensor("gub", [128, 2, D], BF16))
    hsb = EN(nc.sbuf_tensor("hsb", [128, DC, T2], BF16))
    scr = EN(nc.sbuf_tensor("scr", [128, 2048], F32))
    logits_sb = EN(nc.sbuf_tensor("logits_sb", [E, 128], F32))
    rowbuf = EN(nc.sbuf_tensor("rowbuf", [1, 3 * T2], F32))
    rsc = EN(nc.sbuf_tensor("rsc", [128, 16], F32))
    rsc2 = EN(nc.sbuf_tensor("rsc2", [128, 4, E], F32))
    hinit = EN(nc.sbuf_tensor("hinit", [128, DC], F32))
    pos_sb = EN(nc.sbuf_tensor("pos_sb", [128, NCHUNK], F32))
    mp_sb = EN(nc.sbuf_tensor("mp_sb", [128, NCHUNK], F32))
    x1row = EN(nc.sbuf_tensor("x1row", [128, 2, D], F32))
    ln2row = EN(nc.sbuf_tensor("ln2row", [128, 2, D], BF16))
    ln2d = EN(nc.sbuf_tensor("ln2d", [128, D], BF16))
    lnTg = EN(nc.sbuf_tensor("lnTg", [128, 4, D], BF16))
    yrow = EN(nc.sbuf_tensor("yrow", [128, 2, D], BF16))
    idxs_f = EN(nc.sbuf_tensor("idxs_f", [128, 2, 20], F32))
    idxs_u = EN(nc.sbuf_tensor("idxs_u", [128, 2, 20], U32))
    auxg = EN(nc.sbuf_tensor("auxg", [128, 2, 4, 2], F32))
    aux_sb = EN(nc.sbuf_tensor("aux_sb", [128, 2], F32))
    posU = EN(nc.sbuf_tensor("posU", [128, 2], U32))
    posUall = EN(nc.sbuf_tensor("posUall", [128, 64], U32))
    b1tile = EN(nc.sbuf_tensor("b1tile", [128, HT], F32))
    b2tile = EN(nc.sbuf_tensor("b2tile", [128, DC], F32))
    rc8 = EN(nc.sbuf_tensor("rc8", [1, E], F32))
    finrow = EN(nc.sbuf_tensor("finrow", [1, 4 * NTILES], F32))
    startrow = EN(nc.sbuf_tensor("startrow", [1, 16], F32))
    grow = EN(nc.sbuf_tensor("grow", [1, max(128, NTILES * E)], F32))
    tri_f = EN(nc.sbuf_tensor("tri_f", [128, 128], F32))
    tri_b = EN(nc.sbuf_tensor("tri_b", [128, 128], BF16))
    iden = EN(nc.sbuf_tensor("iden", [128, 128], F32))
    idenb = EN(nc.sbuf_tensor("idenb", [128, 128], BF16))
    iotaE = EN(nc.sbuf_tensor("iotaE", [128, E], F32))
    negsidx = EN(nc.sbuf_tensor("negsidx", [128, 1], F32))
    tix = EN(nc.sbuf_tensor("tix", [1, 128], F32))
    ones1 = EN(nc.sbuf_tensor("ones1", [1, 128], F32))
    onescol = EN(nc.sbuf_tensor("onescol", [128, 1], F32))
    iotaQ = EN(nc.sbuf_tensor("iotaQ", [128, 4], F32))
    thresh16 = EN(nc.sbuf_tensor("thresh16", [1, E * 16], F32))
    tau24 = EN(nc.sbuf_tensor("tau24", [1, NTILES * E], F32))
    iotaE24 = EN(nc.sbuf_tensor("iotaE24", [1, NTILES * E], F32))
    lamNegPow = EN(nc.sbuf_tensor("lamNegPow", [128, D], BF16))
    gbLam = EN(nc.sbuf_tensor("gbLam", [128, D], BF16))
    lamPow = EN(nc.sbuf_tensor("lamPow", [128, DC, CH], F32))
    lamC64 = EN(nc.sbuf_tensor("lamC64", [128, DC], F32))
    LCol = EN(nc.sbuf_tensor("LCol", [128, DC], F32))
    L8 = EN(nc.sbuf_tensor("L8", [DC, 128], F32))
    b1Col = EN(nc.sbuf_tensor("b1Col", [128, HT * E], F32))
    b2ColE = EN(nc.sbuf_tensor("b2ColE", [128, E * DC], F32))
    boCol = EN(nc.sbuf_tensor("boCol", [128, DC], F32))
    br_col = EN(nc.sbuf_tensor("br_col", [E, 1], F32))
    wr_b = EN(nc.sbuf_tensor("wr_b", [128, DC * E], BF16))
    pA = EN(nc.psum_tensor("pA", [128, 2, 512], F32))
    pB = EN(nc.psum_tensor("pB", [128, 2, 512], F32))
    pC = EN(nc.psum_tensor("pC", [128, 1024], F32))
    pD = EN(nc.psum_tensor("pD", [128, 2, 512], F32))
    semh = {}
    for name in ("sd", "sd2", "gd", "gd2", "gd3", "gd4", "gd5", "gd6", "gd7", "gd8", "sd3", "sd4", "pe", "act", "dve"):
        semh[name] = EN(nc.semaphore("s_" + name))
    breg = EN(nc.gpsimd.register("breg"))
    block = EN(nc.Block())

    ENG_SEM = {"sync": "sd", "gpsimd": "gd", "tensor": "pe", "vector": "dve",
               "scalar": "act"}
    QLANES = {"sync": ["sd", "sd2"], "gpsimd": ["gd", "gd2"]}
    cnt = {k: 0 for k in semh}
    Q = {k: [] for k in ENG_SEM}
    regions = {}
    pend = {k: {"w": [], "r": []} for k in ENG_SEM}
    satisfied = {k: {} for k in ENG_SEM}
    lane_rr = {"sync": 0, "gpsimd": 0}

    def R(k):
        if k not in regions:
            regions[k] = {"w": None, "r": {}}
        return regions[k]

    def op(eng, emit, reads=(), writes=(), inc=True, extra=(), sem=None, inc_k=1):
        sem = sem or ENG_SEM[eng]
        want = {}

        def addw(s, v):
            if v and want.get(s, 0) < v:
                want[s] = v
        for k in reads:
            reg = R(k)
            if reg["w"] is not None:
                addw(*reg["w"])
            assert all(k not in pend[e]["w"] for e in pend if e != eng), \
                f"read of pending write {k}"
        for k in writes:
            reg = R(k)
            if reg["w"] is not None:
                addw(*reg["w"])
            for s, v in reg["r"].items():
                addw(s, v)
            assert all(k not in pend[e]["w"] and k not in pend[e]["r"]
                       for e in pend if e != eng), f"write over pending {k}"
        for s, v in extra:
            addw(s, v)
        waits = tuple((s, v) for s, v in want.items()
                      if v > satisfied[eng].get(s, 0))
        for s, v in waits:
            satisfied[eng][s] = v
        post = None
        if inc:
            cnt[sem] += inc_k
            post = cnt[sem]
            for k in pend[eng]["w"] + list(writes):
                R(k)["w"] = (sem, post)
                R(k)["r"] = {}
            for k in pend[eng]["r"] + list(reads):
                R(k)["r"][sem] = post
            pend[eng] = {"w": [], "r": []}
        else:
            pend[eng]["w"] += list(writes)
            pend[eng]["r"] += list(reads)

        if eng == "tensor":
            OP_TRACE["tensor"].append((waits, tuple(reads), tuple(writes)))

        def fn(h, emit=emit, waits=waits, inc=inc, sem=sem, inc_k=inc_k):
            for s, v in waits:
                h.wait_ge(semh[s], v)
            ins = emit(h)
            if inc:
                ins.then_inc(semh[sem], inc_k)
        Q[eng].append(fn)
        return post

    def dma(queue, dst, src, reads=(), writes=(), indirect=None, lane=None):
        if lane is None:
            lane = QLANES[queue][lane_rr[queue]]
            lane_rr[queue] ^= 1
        prev = cnt[lane]
        if indirect is None:
            emit = lambda h, d=dst, s=src: h.dma_start(out=d, in_=s)
        else:
            kind, idx_ap = indirect[0], indirect[1]
            bc = indirect[2] if len(indirect) > 2 else None
            if kind == "gather":
                emit = lambda h, d=dst, s=src, i=idx_ap, bc=bc: h.indirect_dma_start(
                    out=d, out_offset=None, in_=s,
                    in_offset=bass.IndirectOffsetOnAxis(ap=i, axis=0),
                    **({} if bc is None else
                       {"bounds_check": bc, "oob_is_err": False}))
            else:
                emit = lambda h, d=dst, s=src, i=idx_ap, bc=bc: h.indirect_dma_start(
                    out=d, out_offset=bass.IndirectOffsetOnAxis(ap=i, axis=0),
                    in_=s, in_offset=None,
                    **({} if bc is None else
                       {"bounds_check": bc, "oob_is_err": False}))
        return op(queue, emit, reads=reads, writes=writes, sem=lane, inc_k=16,
                  extra=([(lane, prev)] if prev else []))

    def final_wait(eng, sems_to_wait):
        vals = [(s, cnt[s]) for s in sems_to_wait if cnt[s] > 0]

        def fn(h, vals=tuple(vals)):
            for s, v in vals:
                h.wait_ge(semh[s], v)
        Q[eng].append(fn)

    xt_r = xt.rearrange("(c p) t -> p c t", c=DC)
    BX = lambda par, dc: ("bigX", par, dc)
    x1d = [wbB[:, 0:8192].bitcast(F32).rearrange("p (a b) -> p a b", a=DC),
           wbB[:, 8192:16384].bitcast(F32).rearrange("p (a b) -> p a b", a=DC)]
    X1K = lambda par, dc: ("x1d", par, dc)
    ALL_X1K = [("x1d", p, d) for p in range(2) for d in range(DC)]

    def load_x(j, par):
        dma("sync", bigX[:, par, 0:4, :], xt_r[:, 0:4, j * T2:(j + 1) * T2],
            writes=[BX(par, dc) for dc in range(4)])
        dma("sync", bigX[:, par, 4:8, :], xt_r[:, 4:8, j * T2:(j + 1) * T2],
            writes=[BX(par, dc) for dc in range(4, 8)])

    load_x(0, 0)
    load_x(1, 1)

    for dst, src, key in [
            (tri_f[:], tri_d, "tri_f"), (iden[:], iden_d, "iden"),
            (iotaE[:], iotaE_d, "iotaE"), (negsidx[:], negsidx_d, "negsidx"),
            (tix[:], tix_d, "tix"), (ones1[:], ones1_d, "ones1"),
            (onescol[:], onescol_d, "onescol"),
            (br_col[:], br_d, "br_col"), (rc8[:], zrow_d, "rc8"),
            (iotaQ[:], iotaQ_d, "iotaQ"), (thresh16[:], thresh_d, "thresh16"),
            (tau24[:], tau_d, "tau24"), (iotaE24[:], iotaE24_d, "iotaE24")]:
        dma("sync", dst, src, writes=[key])
    dma("sync", L8[:], lam_log.rearrange("o (a p) -> (o a) p", a=DC), writes=["L8"])

    op("vector", lambda h: h.tensor_copy(tri_b[:], tri_f[:]),
       reads=["tri_f"], writes=["tri_b"])
    op("vector", lambda h: h.tensor_copy(idenb[:], iden[:]),
       reads=["iden"], writes=["idenb"])
    op("vector", lambda h: h.tensor_copy(onescol_b[:], onescol[:]),
       reads=["onescol"], writes=["onescol_b"])

    dma("gpsimd", slotbuf[ZROW:ZROW + 1, :], zrowD_d, writes=[("slotzero",)])
    dma("gpsimd", slotaux[ZROW:ZROW + 1, :], zaux_d, writes=[("slotauxzero",)])
    dma("gpsimd", wbA[:, 0:8192].rearrange("p (c t) -> p c t", c=DC),
        w_in_t.rearrange("(c p) t -> p c t", c=DC), writes=[("wbA", 0)])
    dma("gpsimd", wbA[:, 8192:16384].rearrange("p (c t) -> p c t", c=DC),
        w_out_t.rearrange("(c p) t -> p c t", c=DC), writes=[("wbA", 1)])
    dma("gpsimd", wr_b[:].rearrange("p (c e) -> p c e", c=DC),
        wr_d.rearrange("(c p) e -> p c e", c=DC), writes=["wr_b"])

    op("tensor", lambda h: h.transpose(pC[:, 0:DC], L8[:], iden[0:DC, 0:DC]),
       reads=["L8", "iden"], writes=[("pC", 0)])
    op("scalar", lambda h: h.copy(LCol[:], pC[:, 0:DC]),
       reads=[("pC", 0)], writes=["LCol"])
    op("scalar", lambda h: h.activation(lamC64[:], LCol[:], AF.Exp, scale=64.0),
       reads=["LCol"], writes=["lamC64"])
    dma("sync", scr[0:DC, 1024:1152],
        b_out_d.rearrange("o (a p) -> (o a) p", a=DC), writes=[("scr", 1)])
    op("tensor", lambda h: h.transpose(pC[:, 16:24], scr[0:DC, 1024:1152],
                                       iden[0:DC, 0:DC]),
       reads=[("scr", 1), "iden"], writes=[("pC", 0)])
    op("scalar", lambda h: h.copy(boCol[:], pC[:, 16:24]),
       reads=[("pC", 0)], writes=["boCol"])
    for half in range(2):
        dma("sync", scr[0:E, 1024:2048], b1_d[:, half * 1024:(half + 1) * 1024],
            writes=[("scr", 1)])
        for hh in range(8):
            ht = half * 8 + hh
            op("tensor", lambda h, ht=ht, hh=hh: h.transpose(
                pC[:, 512 + ht * 8: 512 + (ht + 1) * 8],
                scr[0:E, 1024 + hh * 128:1024 + (hh + 1) * 128], iden[0:E, 0:E]),
                reads=[("scr", 1), "iden"], writes=[("pC", 1)], inc=(hh == 7))
    op("scalar", lambda h: h.copy(
        b1Col[:].rearrange("p (e t) -> p t e", e=E),
        pC[:, 512:640].rearrange("p (t e) -> p t e", t=HT)),
        reads=[("pC", 1)], writes=["b1Col"])
    dma("sync", b1cols.rearrange("(e p) t -> p e t", e=E),
        b1Col[:].rearrange("p (e t) -> p e t", e=E),
        reads=["b1Col"], writes=["b1cols"])
    dma("sync", scr[0:E, 1024:2048], b2_d, writes=[("scr", 1)])
    for dt in range(DC):
        op("tensor", lambda h, dt=dt: h.transpose(
            pC[:, 512 + dt * 8:512 + (dt + 1) * 8],
            scr[0:E, 1024 + dt * 128:1024 + (dt + 1) * 128], iden[0:E, 0:E]),
            reads=[("scr", 1), "iden"], writes=[("pC", 1)], inc=(dt == DC - 1))
    op("scalar", lambda h: h.copy(
        b2ColE[:].rearrange("p (e t) -> p t e", e=E),
        pC[:, 512:576].rearrange("p (t e) -> p t e", t=DC)),
        reads=[("pC", 1)], writes=["b2ColE"])
    dma("sync", b2cols.rearrange("(e p) t -> p e t", e=E),
        b2ColE[:].rearrange("p (e t) -> p e t", e=E),
        reads=["b2ColE"], writes=["b2cols"])

    dma("sync", rowbuf[:, 0:1024], lam_log, writes=["rowbuf"])
    op("tensor", lambda h: h.matmul(pA[:, 0, :], ones1[:], rowbuf[:, 0:512],
                                    start=True, stop=True),
       reads=["ones1", "rowbuf"], writes=[("pA", 0)])
    op("tensor", lambda h: h.matmul(pA[:, 1, :], ones1[:], rowbuf[:, 512:1024],
                                    start=True, stop=True),
       reads=["ones1", "rowbuf"], writes=[("pA", 1)])
    op("vector", lambda h: h.tensor_scalar(
        scr[:, 1024:2048], pA[:].rearrange("p a b -> p (a b)"), negsidx[:],
        None, ALU.mult),
        reads=[("pA", 0), ("pA", 1), "negsidx"], writes=[("scr", 1)])
    op("scalar", lambda h: h.activation(lamNegPow[:], scr[:, 1024:2048], AF.Exp),
       reads=[("scr", 1)], writes=["lamNegPow"])
    dma("sync", rowbuf[:, 0:1024], gamma_d, writes=["rowbuf"])
    dma("sync", scr[0:1, 1024:2048], b_in_d, writes=[("scr", 1)])
    op("tensor", lambda h: h.matmul(pD[:, 0, :], ones1[:], rowbuf[:, 0:512],
                                    start=True, stop=True),
       reads=["ones1", "rowbuf"], writes=[("pD", 0)])
    op("tensor", lambda h: h.matmul(pD[:, 1, :], ones1[:], rowbuf[:, 512:1024],
                                    start=True, stop=True),
       reads=["ones1", "rowbuf"], writes=[("pD", 1)])
    op("vector", lambda h: h.tensor_tensor(
        rowbuf[:, 0:1024], rowbuf[:, 0:1024], scr[0:1, 1024:2048], ALU.mult),
        reads=["rowbuf", ("scr", 1)], writes=["rowbuf"])
    op("tensor", lambda h: h.matmul(pB[:, 0, :], ones1[:], rowbuf[:, 0:512],
                                    start=True, stop=True),
       reads=["ones1", "rowbuf"], writes=[("pB", 0)])
    op("tensor", lambda h: h.matmul(pB[:, 1, :], ones1[:], rowbuf[:, 512:1024],
                                    start=True, stop=True),
       reads=["ones1", "rowbuf"], writes=[("pB", 1)])
    op("vector", lambda h: h.tensor_tensor(
        gbLam[:], pB[:].rearrange("p a b -> p (a b)"), lamNegPow[:], ALU.mult),
        reads=[("pB", 0), ("pB", 1), "lamNegPow"], writes=["gbLam"])
    op("vector", lambda h: h.tensor_tensor(
        lamNegPow[:], lamNegPow[:], pD[:].rearrange("p a b -> p (a b)"), ALU.mult),
        reads=["lamNegPow", ("pD", 0), ("pD", 1)], writes=["lamNegPow"])
    op("tensor", lambda h: h.matmul(pC[:, 0:128], ones1[:], tix[:],
                                    start=True, stop=True),
       reads=["ones1", "tix"], writes=[("pC", 0)])
    for k in range(DC):
        op("vector", lambda h, k=k: h.tensor_scalar(
            scr[:, 1024 + k * 128:1024 + (k + 1) * 128], pC[:, 0:128],
            LCol[:, k:k + 1], None, ALU.mult),
            reads=[("pC", 0), "LCol"], writes=[("scr", 1)])
    op("scalar", lambda h: h.activation(
        lamPow[:].rearrange("p a b -> p (a b)"), scr[:, 1024:2048], AF.Exp),
        reads=[("scr", 1)], writes=["lamPow"])
    conv_emits = []
    for e in range(E):
        conv_emits.append(lambda e=e: op("gpsimd", lambda h, e=e: h.dma_start(
            out=w1s[e * 128:(e + 1) * 128, :].rearrange("p (c h) -> p c h", c=DC),
            in_=w1_d[e].rearrange("(c p) h -> p c h", c=DC)),
            writes=[("w1s", e)], sem="gd3", inc_k=16))
        for half in range(2):
            conv_emits.append(lambda e=e, half=half: op(
                "gpsimd", lambda h, e=e, half=half: h.dma_start(
                    out=w2s[e * 128:(e + 1) * 128,
                            half * 8192:(half + 1) * 8192].rearrange("p (c t) -> p c t", c=8),
                    in_=w2_d[e].rearrange("(c p) t -> p c t", c=HT)[:, half * 8:(half + 1) * 8, :]),
                writes=[("w2s", e)], sem="gd3", inc_k=16))

    def emit_ln_p1ab(par):
        for dc in range(DC):
            op("scalar", lambda h, dc=dc, par=par: h.copy(
                xb[:, dc % 2, :], bigX[:, par, dc, :]),
                reads=[BX(par, dc)], writes=[("xb", dc % 2)])
            op("vector", lambda h, dc=dc: h.tensor_tensor(
                xsqb[:, dc % 2, :], xb[:, dc % 2, :], xb[:, dc % 2, :], ALU.mult),
                reads=[("xb", dc % 2)], writes=[("xsqb", dc % 2)])
            op("tensor", lambda h, dc=dc: h.matmul(
                pA[0:1, 0, :], onescol_b[:], xb[:, dc % 2, :],
                start=(dc == 0), stop=(dc == DC - 1)),
                reads=["onescol_b", ("xb", dc % 2)], writes=[("pA", 0)],
                inc=True)
            op("tensor", lambda h, dc=dc: h.matmul(
                pA[0:1, 1, :], onescol_b[:], xsqb[:, dc % 2, :],
                start=(dc == 0), stop=(dc == DC - 1)),
                reads=["onescol_b", ("xsqb", dc % 2)], writes=[("pA", 1)],
                inc=True)
        op("vector", lambda h: h.tensor_scalar(
            rowbuf[:, 0:T2], pA[0:1, 0, :], 1.0 / D, None, ALU.mult),
            reads=[("pA", 0)], writes=["rowbuf"])
        op("vector", lambda h: h.tensor_scalar(
            rowbuf[:, T2:2 * T2], pA[0:1, 1, :], 1.0 / D, 1e-5, ALU.mult, ALU.add),
            reads=[("pA", 1)], writes=["rowbuf"])
        op("vector", lambda h: h.tensor_tensor(
            rowbuf[:, 2 * T2:3 * T2], rowbuf[:, 0:T2], rowbuf[:, 0:T2], ALU.mult),
            reads=["rowbuf"], writes=["rowbuf"])
        op("vector", lambda h: h.tensor_tensor(
            rowbuf[:, T2:2 * T2], rowbuf[:, T2:2 * T2], rowbuf[:, 2 * T2:3 * T2],
            ALU.subtract),
            reads=["rowbuf"], writes=["rowbuf"])
        op("scalar", lambda h: h.activation(
            rowbuf[:, T2:2 * T2], rowbuf[:, T2:2 * T2], AF.Ln),
            reads=["rowbuf"], writes=["rowbuf"])
        op("scalar", lambda h: h.activation(
            rowbuf[:, 2 * T2:3 * T2], rowbuf[:, T2:2 * T2], AF.Exp, scale=-0.5),
            reads=["rowbuf"], writes=["rowbuf"])
        op("vector", lambda h: h.tensor_tensor(
            rowbuf[:, T2:2 * T2], rowbuf[:, 0:T2], rowbuf[:, 2 * T2:3 * T2],
            ALU.mult),
            reads=["rowbuf"], writes=["rowbuf"])

    def emit_ln_p1c():
        op("tensor", lambda h: h.matmul(
            pC[:, 0:512], ones1[:], rowbuf[:, 2 * T2:3 * T2], start=True, stop=True),
            reads=["ones1", "rowbuf"], writes=[("pC", 0)])
        op("tensor", lambda h: h.matmul(
            pC[:, 512:1024], ones1[:], rowbuf[:, T2:2 * T2], start=True, stop=True),
            reads=["ones1", "rowbuf"], writes=[("pC", 1)])

    def emit_ln_p2(par):
        rbc = pC[:, 0:512].unsqueeze(1).broadcast_to([128, DC, 512])
        mbc = pC[:, 512:1024].unsqueeze(1).broadcast_to([128, DC, 512])
        op("vector", lambda h, par=par, rbc=rbc: h.tensor_tensor(
            lnx[:, par], bigX[:, par], rbc, ALU.mult),
            reads=[("pC", 0)] + [BX(par, dc) for dc in range(DC)],
            writes=[("lnx", par, dc) for dc in range(DC)])
        op("vector", lambda h, par=par, mbc=mbc: h.tensor_tensor(
            lnx[:, par], lnx[:, par], mbc, ALU.subtract),
            reads=[("pC", 1)] + [("lnx", par, dc) for dc in range(DC)],
            writes=[("lnx", par, dc) for dc in range(DC)])

    def tail_a1(j, q):
        par = j % 2
        cidx = j * 4 + q
        qp = q % 2
        for g in range(2):
            for dd in range(4):
                dc = g * 4 + dd
                op("tensor", lambda h, dc=dc, dd=dd, q=q, par=par: h.transpose(
                    pD[:, 0, dd * 128:(dd + 1) * 128],
                    x1d[par][:, dc, q * 128:(q + 1) * 128], iden[:]),
                    reads=[X1K(par, dc), "iden"], writes=[("pD", 0)],
                    inc=(dd == 3))
            op("scalar", lambda h, g=g, qp=qp: h.copy(
                x1row[:, qp, g * 512:(g + 1) * 512], pD[:, 0, :]),
                reads=[("pD", 0)], writes=[("x1row", qp)])
        dma("sync", x1tok[cidx * 128:(cidx + 1) * 128, :], x1row[:, qp, :],
            reads=[("x1row", qp)], writes=[("x1tok",)])
        op("scalar", lambda h, qp=qp: h.activation(
            lnTg[:, 0, :], x1row[:, qp, :], AF.Identity,
            accum_out=rsc[:, 6 + qp:7 + qp]),
            reads=[("x1row", qp)], writes=[("lnTg", 0), ("rsc", 6 + qp)])
        op("scalar", lambda h, qp=qp: h.activation(
            lnTg[:, 1, :], x1row[:, qp, :], AF.Square,
            accum_out=rsc[:, 8 + qp:9 + qp]),
            reads=[("x1row", qp)], writes=[("lnTg", 1), ("rsc", 8 + qp)])

    def tail_a2(j, q):
        qp = q % 2
        op("vector", lambda h, qp=qp: h.tensor_scalar(
            rsc[:, 10 + qp:11 + qp], rsc[:, 6 + qp:7 + qp], 1.0 / D, None,
            ALU.mult),
            reads=[("rsc", 6 + qp)], writes=[("rsc", 10 + qp)])
        op("vector", lambda h, qp=qp: h.tensor_scalar(
            rsc[:, 12 + qp:13 + qp], rsc[:, 8 + qp:9 + qp], 1.0 / D, 1e-5,
            ALU.mult, ALU.add),
            reads=[("rsc", 8 + qp)], writes=[("rsc", 12 + qp)])
        op("vector", lambda h, qp=qp: h.tensor_tensor(
            rsc[:, 8 + qp:9 + qp], rsc[:, 10 + qp:11 + qp],
            rsc[:, 10 + qp:11 + qp], ALU.mult),
            reads=[("rsc", 10 + qp)], writes=[("rsc", 8 + qp)])
        op("vector", lambda h, qp=qp: h.tensor_tensor(
            rsc[:, 12 + qp:13 + qp], rsc[:, 12 + qp:13 + qp],
            rsc[:, 8 + qp:9 + qp], ALU.subtract),
            reads=[("rsc", 8 + qp), ("rsc", 12 + qp)], writes=[("rsc", 12 + qp)])
        op("scalar", lambda h, qp=qp: h.activation(
            rsc[:, 12 + qp:13 + qp], rsc[:, 12 + qp:13 + qp], AF.Ln),
            reads=[("rsc", 12 + qp)], writes=[("rsc", 12 + qp)])
        op("scalar", lambda h, qp=qp: h.activation(
            rsc[:, 12 + qp:13 + qp], rsc[:, 12 + qp:13 + qp], AF.Exp,
            scale=-0.5),
            reads=[("rsc", 12 + qp)], writes=[("rsc", 12 + qp)])
        op("vector", lambda h, qp=qp: h.tensor_tensor(
            rsc[:, 10 + qp:11 + qp], rsc[:, 10 + qp:11 + qp],
            rsc[:, 12 + qp:13 + qp], ALU.mult),
            reads=[("rsc", 12 + qp), ("rsc", 10 + qp)], writes=[("rsc", 10 + qp)])
        op("vector", lambda h, qp=qp: h.tensor_scalar(
            rsc[:, 10 + qp:11 + qp], rsc[:, 10 + qp:11 + qp], -1.0, None,
            ALU.mult),
            reads=[("rsc", 10 + qp)], writes=[("rsc", 10 + qp)])
        op("scalar", lambda h, qp=qp: h.activation(
            ln2row[:, qp, :], x1row[:, qp, :], AF.Identity,
            bias=rsc[:, 10 + qp:11 + qp], scale=rsc[:, 12 + qp:13 + qp]),
            reads=[("x1row", qp), ("rsc", 10 + qp), ("rsc", 12 + qp)],
            writes=[("ln2row", qp)])

    def tail_b1(j, q):
        for g in range(2):
            for dd in range(4):
                dc = g * 4 + dd
                op("tensor", lambda h, dc=dc, dd=dd, qp=q % 2: h.transpose(
                    pD[0:128, 1, 256 + dd * 64:256 + (dd + 1) * 64].bitcast(BF16),
                    ln2row[:, qp, dc * 128:(dc + 1) * 128], idenb[:]),
                    reads=[("ln2row", q % 2), "idenb"], writes=[("pD", 1)],
                    inc=(dd == 3))
            op("scalar", lambda h, g=g: h.copy(
                ln2d[:, g * 512:(g + 1) * 512].rearrange("p (a b) -> p a b", a=4),
                pD[0:128, 1, 256:512].bitcast(BF16).rearrange("p (a b) -> p a b", a=4)),
                reads=[("pD", 1)], writes=["ln2d"])

    def tail_b2(j, q):
        for dc in range(DC):
            op("tensor", lambda h, dc=dc: h.matmul(
                pD[0:E, 1, 0:128],
                wr_b[:, dc * E:(dc + 1) * E],
                ln2d[:, dc * 128:(dc + 1) * 128],
                start=(dc == 0), stop=(dc == DC - 1)),
                reads=["wr_b", "ln2d"], writes=[("pD", 1)],
                inc=(dc == DC - 1))
        op("scalar", lambda h: h.activation(
            logits_sb[:], pD[0:E, 1, 0:128],
            AF.Identity, bias=br_col[:]),
            reads=[("pD", 1), "br_col"], writes=["logits_sb"])
        op("tensor", lambda h: h.transpose(
            pD[:, 1, 128:136],
            logits_sb[:], iden[0:E, 0:E]),
            reads=["logits_sb", "iden"], writes=[("pD", 1)])

    def tail_b3a(j, q):
        cidx = j * 4 + q
        op("vector", lambda h: h.tensor_reduce(
            rsc[:, 0:1], pD[:, 1, 128:136], AX.X, ALU.max, negate=True),
            reads=[("pD", 1)], writes=[("rsc", 0)])
        op("scalar", lambda h: h.activation(
            lnTg[:, 2, 0:8], pD[:, 1, 128:136], AF.Exp,
            bias=rsc[:, 0:1], accum_out=rsc[:, 1:2]),
            reads=[("pD", 1), ("rsc", 0)], writes=[("lnTg", 2), ("rsc", 1)])
        op("vector", lambda h: h.reciprocal(rsc[:, 2:3], rsc[:, 1:2]),
           reads=[("rsc", 1)], writes=[("rsc", 2)])
        op("vector", lambda h: h.tensor_scalar(
            rsc2[:, 0, :], pD[:, 1, 128:136],
            rsc[:, 0:1], 0.0, ALU.add, ALU.is_ge),
            reads=[("pD", 1), ("rsc", 0)], writes=[("rsc2", 0)])
        op("vector", lambda h: h.tensor_scalar(
            rsc2[:, 1, :], rsc2[:, 0, :], -1e9, 1e9, ALU.mult, ALU.add),
            reads=[("rsc2", 0)], writes=[("rsc2", 1)])
        op("vector", lambda h: h.tensor_tensor(
            rsc2[:, 1, :], rsc2[:, 1, :], iotaE[:], ALU.add),
            reads=[("rsc2", 1), "iotaE"], writes=[("rsc2", 1)])
        op("vector", lambda h: h.tensor_reduce(
            rsc[:, 3:4], rsc2[:, 1, :], AX.X, ALU.min),
            reads=[("rsc2", 1)], writes=[("rsc", 3)])
        op("vector", lambda h: h.tensor_scalar(
            rsc2[:, 2, :], iotaE[:], rsc[:, 3:4], None, ALU.is_equal),
            reads=["iotaE", ("rsc", 3)], writes=[("rsc2", 2)])

    def tail_b3b(j, q):
        cidx = j * 4 + q
        op("tensor", lambda h: h.matmul(
            pD[0:128, 1, 136:144], ones1[:], rc8[:], start=True, stop=False),
            reads=["ones1", "rc8"], writes=[("pD", 1)])
        op("tensor", lambda h: h.matmul(
            pD[0:128, 1, 136:144], tri_f[:], rsc2[:, 2, :], start=False, stop=True),
            reads=["tri_f", ("rsc2", 2)], writes=[("pD", 1)])
        op("tensor", lambda h: h.matmul(
            pD[0:1, 1, 144:152], onescol[:], rsc2[:, 2, :], start=True, stop=True),
            reads=["onescol", ("rsc2", 2)], writes=[("pD", 1)])
        op("vector", lambda h: h.tensor_tensor(
            rsc2[:, 3, :], rsc2[:, 2, :], pD[0:128, 1, 136:144], ALU.mult),
            reads=[("rsc2", 2), ("pD", 1)], writes=[("rsc2", 3)])
        op("vector", lambda h: h.tensor_reduce(
            rsc[:, 4:5], rsc2[:, 3, :], AX.X, ALU.add),
            reads=[("rsc2", 3)], writes=[("rsc", 4)])
        op("vector", lambda h: h.tensor_scalar(
            rsc[:, 5:6], rsc[:, 4:5], 1.0, -1.0, ALU.mult, ALU.add),
            reads=[("rsc", 4)], writes=[("rsc", 5)])
        op("vector", lambda h, cidx=cidx: h.tensor_scalar(
            pos_sb[:, cidx:cidx + 1], rsc[:, 3:4], float(T), rsc[:, 5:6],
            ALU.mult, ALU.add),
            reads=[("rsc", 3), ("rsc", 5)], writes=["pos_sb"])
        op("vector", lambda h: h.tensor_tensor(
            rc8[:], rc8[:], pD[0:1, 1, 144:152], ALU.add),
            reads=["rc8", ("pD", 1)], writes=["rc8"])
        op("vector", lambda h, cidx=cidx: h.tensor_copy(
            posU[:, 0:1], pos_sb[:, cidx:cidx + 1]),
            reads=["pos_sb"], writes=[("posU2", 0)])
        op("vector", lambda h, cidx=cidx: h.tensor_scalar(
            aux_sb[:, 0:1], iotaQ[:, 0:1], 1.0, float(cidx * 128),
            ALU.mult, ALU.add),
            reads=["iotaQ"], writes=["aux_sb"])
        op("vector", lambda h: h.tensor_copy(aux_sb[:, 1:2], rsc[:, 2:3]),
           reads=[("rsc", 2)], writes=["aux_sb"])
        dma("gpsimd", slotaux, aux_sb[:],
            reads=["aux_sb", ("posU2", 0)], writes=[("slotaux",)],
            indirect=("scatter", posU[:, 0:1]))
        dma("gpsimd", slotbuf, ln2row[:, q % 2, :],
            reads=[("ln2row", q % 2), ("posU2", 0)], writes=[("slotbuf",)],
            indirect=("scatter", posU[:, 0:1]))

    tail_stream = []

    def push_tail(j):
        def bchain(q):
            return [lambda q=q: tail_b1(j, q), lambda q=q: tail_b2(j, q),
                    lambda q=q: tail_b3a(j, q), lambda q=q: tail_b3b(j, q)]
        for q in range(4):
            tail_stream.append(lambda q=q: tail_a1(j, q))
            tail_stream.append(lambda q=q: tail_a2(j, q))
            if q >= 1:
                tail_stream.extend(bchain(q - 1))
        tail_stream.extend(bchain(3))

    def flush_tail(n):
        for _ in range(min(n, len(tail_stream))):
            tail_stream.pop(0)()

    emit_ln_p1ab(0)
    emit_ln_p1c()
    emit_ln_p2(0)
    for j in range(NT2):
        par = j % 2
        seq_t2 = j % T2_PER_SEQ
        if j > 0:
            push_tail(j - 1)
        for q in range(T2 // CH):
            qpar = q % 2
            for half in range(2):
                for dc in range(DC):
                    op("tensor", lambda h, q=q, half=half, dc=dc, par=par: h.matmul(
                        pA[:, half, :],
                        lnx[:, par, dc, q * 128:(q + 1) * 128],
                        wbA[:, dc * 1024 + half * 512: dc * 1024 + (half + 1) * 512],
                        start=(dc == 0), stop=(dc == DC - 1)),
                        reads=[("lnx", par, dc), ("wbA", 0)], writes=[("pA", half)],
                        inc=(dc == DC - 1))
                op("vector", lambda h, half=half: h.tensor_tensor(
                    xsq[:, half, :],
                    pA[:, half, :], lamNegPow[:, half * 512:(half + 1) * 512],
                    ALU.mult),
                    reads=[("pA", half), "lamNegPow"], writes=[("xsq", half)])
            op("vector", lambda h, qpar=qpar: h.tensor_tensor(
                gub[:, qpar, :], xsq[:].rearrange("p a b -> p (a b)"),
                gbLam[:], ALU.add),
                reads=[("xsq", 0), ("xsq", 1), "gbLam"], writes=[("gub", qpar)])
            for k in range(DC):
                op("tensor", lambda h, k=k, qpar=qpar: h.matmul(
                    pC[:, k * 128:(k + 1) * 128],
                    gub[:, qpar, k * 128:(k + 1) * 128], tri_b[:],
                    start=True, stop=True),
                    reads=[("gub", qpar), "tri_b"], writes=[("pC", k // 4)])
            first = (seq_t2 == 0 and q == 0)
            hsel = hsb[:].rearrange("p a (n b) -> p a n b", b=CH)
            if not first:
                hinit_bc = hinit[:].unsqueeze(2).broadcast_to([128, DC, CH])
                op("vector", lambda h, hinit_bc=hinit_bc: h.tensor_tensor(
                    scr[:, 0:1024].rearrange("p (a b) -> p a b", a=DC),
                    pC[:].rearrange("p (a b) -> p a b", a=DC), hinit_bc, ALU.add),
                    reads=[("pC", 0), ("pC", 1), "hinit"], writes=[("scr", 0)])
                op("vector", lambda h, q=q, hsel=hsel: h.tensor_tensor(
                    hsel[:, :, q, :],
                    scr[:, 0:1024].rearrange("p (a b) -> p a b", a=DC),
                    lamPow[:], ALU.mult),
                    reads=[("scr", 0), "lamPow"], writes=["hsb"])
            else:
                op("vector", lambda h, q=q, hsel=hsel: h.tensor_tensor(
                    hsel[:, :, q, :],
                    pC[:].rearrange("p (a b) -> p a b", a=DC),
                    lamPow[:], ALU.mult),
                    reads=[("pC", 0), ("pC", 1), "lamPow"], writes=["hsb"])
            op("vector", lambda h, q=q: h.tensor_tensor(
                hinit[:],
                hsb[:, :, q * 128 + 127:q * 128 + 128].squeeze(2),
                lamC64[:], ALU.mult),
                reads=["hsb", "lamC64"], writes=["hinit"])
            flush_tail(4)
        for k in range(DC):
            if k == 1 and j + 1 < NT2:
                emit_ln_p1ab(1 - par)
            if k == 2:
                flush_tail(4)
            if k == 4:
                flush_tail(4)
                if j + 1 < NT2:
                    emit_ln_p1c()
            if k == 5 and j + 1 < NT2:
                emit_ln_p2(1 - par)
            for dc in range(DC):
                op("tensor", lambda h, k=k, dc=dc: h.matmul(
                    pB[:, k % 2, :],
                    wbA[:, 8192 + dc * 1024 + k * 128: 8192 + dc * 1024 + (k + 1) * 128],
                    hsb[:, dc, :], start=(dc == 0), stop=(dc == DC - 1)),
                    reads=[("wbA", 1), "hsb"], writes=[("pB", k % 2)],
                    inc=(dc == DC - 1))
            op("scalar", lambda h, k=k, par=par: h.activation(
                x1d[par][:, k, :], pB[:, k % 2, :],
                AF.Identity, bias=boCol[:, k:k + 1]),
                reads=[("pB", k % 2), "boCol"], writes=[X1K(par, k)])
            if k % 2 == 1:
                op("vector", lambda h, k=k, par=par: h.tensor_tensor(
                    x1d[par][:, k - 1:k + 1, :], x1d[par][:, k - 1:k + 1, :],
                    lnx[:, par, k - 1:k + 1, :], ALU.add),
                    reads=[X1K(par, k - 1), X1K(par, k),
                           ("lnx", par, k - 1), ("lnx", par, k)],
                    writes=[X1K(par, k - 1), X1K(par, k)])
                op("vector", lambda h, k=k, par=par: h.tensor_tensor(
                    x1d[par][:, k - 1:k + 1, :], x1d[par][:, k - 1:k + 1, :],
                    bigX[:, par, k - 1:k + 1, :], ALU.add),
                    reads=[X1K(par, k - 1), X1K(par, k),
                           BX(par, k - 1), BX(par, k)],
                    writes=[X1K(par, k - 1), X1K(par, k)])
        if j + 2 < NT2:
            load_x(j + 2, par)
        if j >= 2:
            for _ in range(2):
                if conv_emits:
                    conv_emits.pop(0)()

    while conv_emits:
        conv_emits.pop(0)()
    _conv_total = cnt["gd3"]
    for e in range(E):
        regions[("w1s", e)]["w"] = ("gd3", _conv_total)
        regions[("w2s", e)]["w"] = ("gd3", _conv_total)

    push_tail(NT2 - 1)
    flush_tail(10 ** 9)

    op("vector", lambda h: h.tensor_tensor(
        grow[:, 0:128].rearrange("o (e k) -> o e k", e=E),
        rc8[:].unsqueeze(2).broadcast_to([1, E, 16]),
        thresh16[:].rearrange("o (e k) -> o e k", e=E), ALU.is_gt),
        reads=["rc8", "thresh16"], writes=["grow"])
    op("vector", lambda h: h.tensor_reduce(
        finrow[:, 0:E], grow[:, 0:128].rearrange("o (e k) -> o e k", e=E),
        AX.X, ALU.add),
        reads=["grow"], writes=["finrow"])
    op("vector", lambda h: h.tensor_scalar(
        finrow[:, 0:E], finrow[:, 0:E], float(T2), None, ALU.mult),
        reads=["finrow"], writes=["finrow"])
    op("vector", lambda h: h.tensor_scalar(
        startrow[:, 0:1], finrow[:, 0:1], 0.0, None, ALU.mult),
        reads=["finrow"], writes=["startrow"])
    for e in range(E):
        op("vector", lambda h, e=e: h.tensor_tensor(
            startrow[:, e + 1:e + 2], startrow[:, e:e + 1],
            finrow[:, e:e + 1], ALU.add),
            reads=["startrow", "finrow"], writes=["startrow"])
    op("vector", lambda h: h.tensor_tensor(
        grow[:, 0:NTILES * E].rearrange("o (t e) -> o t e", e=E),
        tau24[:].rearrange("o (t e) -> o t e", e=E),
        startrow[:, 0:E].unsqueeze(1).broadcast_to([1, NTILES, E]), ALU.is_ge),
        reads=["tau24", "startrow"], writes=["grow"])
    op("vector", lambda h: h.tensor_reduce(
        finrow[:, NTILES:2 * NTILES], grow[:, 0:NTILES * E].rearrange("o (t e) -> o t e", e=E),
        AX.X, ALU.add),
        reads=["grow"], writes=["finrow"])
    op("vector", lambda h: h.tensor_scalar(
        finrow[:, NTILES:2 * NTILES], finrow[:, NTILES:2 * NTILES], 1.0, -1.0,
        ALU.mult, ALU.add),
        reads=["finrow"], writes=["finrow"])
    op("vector", lambda h: h.tensor_tensor(
        grow[:, 0:NTILES * E].rearrange("o (t e) -> o t e", e=E),
        iotaE24[:].rearrange("o (t e) -> o t e", e=E),
        finrow[:, NTILES:2 * NTILES].unsqueeze(2).broadcast_to([1, NTILES, E]),
        ALU.is_equal),
        reads=["iotaE24", "finrow"], writes=["grow"])
    op("vector", lambda h: h.tensor_tensor(
        grow[:, 0:NTILES * E].rearrange("o (t e) -> o t e", e=E),
        grow[:, 0:NTILES * E].rearrange("o (t e) -> o t e", e=E),
        startrow[:, 0:E].unsqueeze(1).broadcast_to([1, NTILES, E]), ALU.mult),
        reads=["grow", "startrow"], writes=["grow"])
    op("vector", lambda h: h.tensor_reduce(
        finrow[:, 2 * NTILES:3 * NTILES],
        grow[:, 0:NTILES * E].rearrange("o (t e) -> o t e", e=E), AX.X, ALU.add),
        reads=["grow"], writes=["finrow"])
    op("vector", lambda h: h.tensor_tensor(
        grow[:, 0:NTILES * E].rearrange("o (t e) -> o t e", e=E),
        iotaE24[:].rearrange("o (t e) -> o t e", e=E),
        finrow[:, NTILES:2 * NTILES].unsqueeze(2).broadcast_to([1, NTILES, E]),
        ALU.is_equal),
        reads=["iotaE24", "finrow"], writes=["grow"])
    op("vector", lambda h: h.tensor_tensor(
        grow[:, 0:NTILES * E].rearrange("o (t e) -> o t e", e=E),
        grow[:, 0:NTILES * E].rearrange("o (t e) -> o t e", e=E),
        rc8[:].unsqueeze(1).broadcast_to([1, NTILES, E]), ALU.mult),
        reads=["grow", "rc8"], writes=["grow"])
    op("vector", lambda h: h.tensor_reduce(
        finrow[:, 0:NTILES],
        grow[:, 0:NTILES * E].rearrange("o (t e) -> o t e", e=E), AX.X, ALU.add),
        reads=["grow"], writes=["finrow"])
    op("vector", lambda h: h.tensor_scalar(
        grow[:, 0:NTILES], finrow[:, NTILES:2 * NTILES], float(T), None, ALU.mult),
        reads=["finrow"], writes=["grow"])
    op("vector", lambda h: h.tensor_tensor(
        finrow[:, 0:NTILES], finrow[:, 0:NTILES], grow[:, 0:NTILES], ALU.add),
        reads=["finrow", "grow"], writes=["finrow"])
    op("vector", lambda h: h.tensor_scalar(
        finrow[:, 3 * NTILES:4 * NTILES], finrow[:, NTILES:2 * NTILES],
        float(T), None, ALU.mult),
        reads=["finrow"], writes=["finrow"])
    op("vector", lambda h: h.tensor_tensor(
        finrow[:, 3 * NTILES:4 * NTILES], finrow[:, 3 * NTILES:4 * NTILES],
        tau24[:].rearrange("o (t e) -> o t e", e=E)[:, :, 0], ALU.add),
        reads=["finrow", "tau24"], writes=["finrow"])
    op("vector", lambda h: h.tensor_tensor(
        finrow[:, 3 * NTILES:4 * NTILES], finrow[:, 3 * NTILES:4 * NTILES],
        finrow[:, 2 * NTILES:3 * NTILES], ALU.subtract),
        reads=["finrow"], writes=["finrow"])

    def tile_front(tau):
        tp = tau % 2
        op("tensor", lambda h, tau=tau: h.matmul(
            pC[:, 0:1], ones1[:], finrow[:, NTILES + tau:NTILES + tau + 1],
            start=True, stop=True),
            reads=["ones1", "finrow"], writes=[("pC", 0)])
        op("tensor", lambda h, tau=tau: h.matmul(
            pC[:, 1:2], ones1[:], finrow[:, 3 * NTILES + tau:3 * NTILES + tau + 1],
            start=True, stop=True),
            reads=["ones1", "finrow"], writes=[("pC", 0)])
        op("tensor", lambda h, tau=tau: h.matmul(
            pC[:, 2:3], ones1[:], finrow[:, tau:tau + 1],
            start=True, stop=True),
            reads=["ones1", "finrow"], writes=[("pC", 0)])
        if tau > 0:
            op("tensor", lambda h, tau=tau: h.matmul(
                pC[:, 3:4], ones1[:], finrow[:, NTILES + tau - 1:NTILES + tau],
                start=True, stop=True),
                reads=["ones1", "finrow"], writes=[("pC", 0)])
        op("vector", lambda h: h.tensor_scalar(
            idxs_f[:, tp, 0:1], pC[:, 0:1], 128.0, iotaQ[:, 0:1], ALU.mult, ALU.add),
            reads=[("pC", 0), "iotaQ"], writes=[("idxs_f", tp)])
        if tau > 0:
            op("vector", lambda h: h.tensor_scalar(
                idxs_f[:, tp, 11:12], pC[:, 3:4], 1.0, None, ALU.mult),
                reads=[("pC", 0)], writes=[("idxs_f", tp)])
            op("vector", lambda h: h.tensor_tensor(
                idxs_f[:, tp, 11:12], idxs_f[:, tp, 11:12], pC[:, 0:1],
                ALU.is_equal),
                reads=[("idxs_f", tp), ("pC", 0)], writes=[("idxs_f", tp)])
            op("vector", lambda h: h.tensor_scalar(
                idxs_f[:, tp, 11:12], idxs_f[:, tp, 11:12], 16384.0, None,
                ALU.mult),
                reads=[("idxs_f", tp)], writes=[("idxs_f", tp)])
            op("vector", lambda h: h.tensor_tensor(
                idxs_f[:, tp, 0:1], idxs_f[:, tp, 0:1], idxs_f[:, tp, 11:12],
                ALU.add),
                reads=[("idxs_f", tp)], writes=[("idxs_f", tp)])
        for q in range(4):
            op("vector", lambda h, q=q: h.tensor_scalar(
                idxs_f[:, tp, 9:10], pC[:, 1:2], 1.0, iotaQ[:, q:q + 1],
                ALU.mult, ALU.add),
                reads=[("pC", 0), "iotaQ"], writes=[("idxs_f", tp)])
            op("vector", lambda h, q=q: h.tensor_scalar(
                idxs_f[:, tp, 10:11], idxs_f[:, tp, 9:10], float(E * T), None,
                ALU.is_lt),
                reads=[("idxs_f", tp)], writes=[("idxs_f", tp)])
            op("vector", lambda h, q=q: h.tensor_tensor(
                idxs_f[:, tp, 1 + q:2 + q], idxs_f[:, tp, 9:10], idxs_f[:, tp, 10:11],
                ALU.mult),
                reads=[("idxs_f", tp)], writes=[("idxs_f", tp)])
            op("vector", lambda h, q=q: h.tensor_scalar(
                idxs_f[:, tp, 11:12], idxs_f[:, tp, 10:11], -1.0, 1.0, ALU.mult, ALU.add),
                reads=[("idxs_f", tp)], writes=[("idxs_f", tp)])
            op("vector", lambda h, q=q, tau=tau: h.tensor_scalar(
                idxs_f[:, tp, 10:11], iotaQ[:, q:q + 1], 1.0,
                float(E * T + tau * T2), ALU.mult, ALU.add),
                reads=["iotaQ"], writes=[("idxs_f", tp)])
            op("vector", lambda h, q=q: h.tensor_tensor(
                idxs_f[:, tp, 11:12], idxs_f[:, tp, 11:12], idxs_f[:, tp, 10:11], ALU.mult),
                reads=[("idxs_f", tp)], writes=[("idxs_f", tp)])
            op("vector", lambda h, q=q: h.tensor_tensor(
                idxs_f[:, tp, 1 + q:2 + q], idxs_f[:, tp, 1 + q:2 + q], idxs_f[:, tp, 11:12],
                ALU.add),
                reads=[("idxs_f", tp)], writes=[("idxs_f", tp)])
            op("vector", lambda h, q=q: h.tensor_scalar(
                idxs_f[:, tp, 10:11], idxs_f[:, tp, 9:10], pC[:, 2:3], None, ALU.is_lt),
                reads=[("idxs_f", tp), ("pC", 0)], writes=[("idxs_f", tp)])
            op("vector", lambda h, q=q: h.tensor_tensor(
                idxs_f[:, tp, 5 + q:6 + q], idxs_f[:, tp, 9:10], idxs_f[:, tp, 10:11],
                ALU.mult),
                reads=[("idxs_f", tp)], writes=[("idxs_f", tp)])
            op("vector", lambda h, q=q: h.tensor_scalar(
                idxs_f[:, tp, 11:12], idxs_f[:, tp, 10:11], float(-ZROW), float(ZROW),
                ALU.mult, ALU.add),
                reads=[("idxs_f", tp)], writes=[("idxs_f", tp)])
            op("vector", lambda h, q=q: h.tensor_tensor(
                idxs_f[:, tp, 5 + q:6 + q], idxs_f[:, tp, 5 + q:6 + q], idxs_f[:, tp, 11:12],
                ALU.add),
                reads=[("idxs_f", tp)], writes=[("idxs_f", tp)])
        op("vector", lambda h: h.tensor_copy(idxs_u[:, tp, 0:12], idxs_f[:, tp, 0:12]),
           reads=[("idxs_f", tp)], writes=[("idxs_u", tp)])

    def tile_w1(tau):
        tp = tau % 2
        dma("gpsimd", wbA[:], w1s, reads=[("idxs_u", tp)] + [("w1s", e) for e in range(E)],
            writes=[("wbA", 0), ("wbA", 1)],
            indirect=("gather", idxs_u[:, tp, 0:1], breg), lane="gd")
        dma("gpsimd", b1tile[:], b1cols, reads=[("idxs_u", tp), "b1cols"],
            writes=["b1tile"], indirect=("gather", idxs_u[:, tp, 0:1], breg),
            lane="gd5")

    def tile_w2(tau):
        tp = tau % 2
        dma("gpsimd", wbB[:], w2s, reads=[("idxs_u", tp)] + [("w2s", e) for e in range(E)],
            writes=[("wbB",)] + ALL_X1K,
            indirect=("gather", idxs_u[:, tp, 0:1], breg), lane="gd2")
        dma("gpsimd", b2tile[:], b2cols, reads=[("idxs_u", tp), "b2cols"],
            writes=["b2tile"], indirect=("gather", idxs_u[:, tp, 0:1], breg),
            lane="gd5")

    def tile_tok_gather(tau):
        tp = tau % 2
        for q in range(4):
            qp = q
            dma("gpsimd", lnTg[:, qp, :], slotbuf,
                reads=[("idxs_u", tp), ("slotbuf",), ("slotzero",)], writes=[("lnTg", qp)],
                indirect=("gather", idxs_u[:, tp, 5 + q:6 + q]),
                lane=("gd3", "gd4")[q % 2])

    def tile_tokens(tau):
        tp = tau % 2
        banks = [pD[0:128, 0, 0:512], pD[0:128, 1, 0:512],
                 pC[:, 512:1024], pC[:, 0:512]]
        keys = [("pD", 0), ("pD", 1), ("pC", 1), ("pC", 0)]
        for q in range(4):
            for dc in range(DC):
                op("tensor", lambda h, dc=dc, q=q, banks=banks: h.transpose(
                    banks[q][:, dc * 64:(dc + 1) * 64].bitcast(BF16),
                    lnTg[:, q, dc * 128:(dc + 1) * 128], idenb[:]),
                    reads=[("lnTg", q), "idenb"], writes=[keys[q]],
                    inc=(dc == DC - 1))
        for q in range(4):
            op("scalar", lambda h, q=q, tp=tp, banks=banks: h.copy(
                lnx[:, tp, :, q * 128:(q + 1) * 128],
                banks[q].bitcast(BF16).rearrange("p (a b) -> p a b", a=DC)),
                reads=[keys[q]], writes=[("lnx", tp, dc) for dc in range(DC)])

    def tile_mm1(tau):
        tp = tau % 2
        for ht in range(HT):
            hp = ht % 2
            for dc in range(DC):
                op("tensor", lambda h, ht=ht, dc=dc, hp=hp, tp=tp: h.matmul(
                    pA[:, hp, :],
                    wbA[:, dc * H + ht * 128: dc * H + (ht + 1) * 128],
                    lnx[:, tp, dc, :], start=(dc == 0), stop=(dc == DC - 1)),
                    reads=[("wbA", 0), ("wbA", 1), ("lnx", tp, dc)],
                    writes=[("pA", hp)], inc=(dc == DC - 1))
            op("scalar", lambda h, ht=ht, hp=hp: h.activation(
                hm[:, ht, :], pA[:, hp, :], AF.Relu,
                bias=b1tile[:, ht:ht + 1]),
                reads=[("pA", hp), "b1tile"], writes=[("hm", ht)])

    def tile_mm2(tau):
        for dt in range(DC):
            dp = dt % 2
            for ht in range(HT):
                op("tensor", lambda h, dt=dt, ht=ht, dp=dp: h.matmul(
                    pB[:, dp, :],
                    wbB[:, ht * D + dt * 128: ht * D + (dt + 1) * 128],
                    hm[:, ht, :], start=(ht == 0), stop=(ht == HT - 1)),
                    reads=[("wbB",), ("hm", ht)], writes=[("pB", dp)],
                    inc=(ht == HT - 1))
            op("scalar", lambda h, dt=dt, dp=dp: h.activation(
                hsb[:, dt, :], pB[:, dp, :], AF.Identity,
                bias=b2tile[:, dt:dt + 1]),
                reads=[("pB", dp), "b2tile"], writes=["hsb"])

    def tile_y(tau):
        tp = tau % 2
        for q in range(4):
            qp = q % 2
            for dt in range(DC):
                op("tensor", lambda h, dt=dt, q=q: h.transpose(
                    pD[0:128, 1, dt * 64:(dt + 1) * 64].bitcast(BF16),
                    hsb[:, dt, q * 128:(q + 1) * 128], idenb[:]),
                    reads=["hsb", "idenb"], writes=[("pD", 1)],
                    inc=(dt == DC - 1))
            op("scalar", lambda h, q=q, qp=qp: h.copy(
                yrow[:, qp, :],
                pD[0:128, 1, 0:512].bitcast(BF16)),
                reads=[("pD", 1)], writes=[("yrow", qp)])
            x1g = bigX[:, qp, tp * 4 + 2 * (q // 2):tp * 4 + 2 * (q // 2) + 2,
                       :].rearrange("p a b -> p (a b)")
            op("vector", lambda h, q=q, qp=qp, tp=tp: h.tensor_scalar(
                ln2row[:, qp, :], yrow[:, qp, :], auxg[:, tp, q, 1:2], None,
                ALU.mult),
                reads=[("yrow", qp), ("auxg", tp, q)], writes=[("ln2row", qp)])
            op("vector", lambda h, q=q, qp=qp, x1g=x1g: h.tensor_tensor(
                x1g, x1g, ln2row[:, qp, :], ALU.add),
                reads=[("x1g", tp, q), ("ln2row", qp)],
                writes=[("x1g", tp, q)])
            dma("gpsimd", out_t, x1g,
                reads=[("x1g", tp, q), ("tokR", tp, q)], writes=[("out", 0)],
                indirect=("scatter", idxs_u[:, tp, 16 + q:17 + q], breg),
                lane="gd8")

    def tile_fin_pre(tau):
        tp = tau % 2
        for q in range(4):
            qp = q % 2
            dma("gpsimd", auxg[:, tp, q, :], slotaux,
                reads=[("idxs_u", tp), ("slotaux",), ("slotauxzero",)],
                writes=[("auxg", tp, q)],
                indirect=("gather", idxs_u[:, tp, 5 + q:6 + q]), lane="gd5")
            op("vector", lambda h, q=q, tp=tp: h.tensor_scalar(
                idxs_u[:, tp, 12 + q:13 + q], auxg[:, tp, q, 0:1],
                float(T - 1), None, ALU.min),
                reads=[("auxg", tp, q)], writes=[("tokU", tp, q)])
            op("vector", lambda h, q=q, tp=tp: h.tensor_copy(
                idxs_u[:, tp, 16 + q:17 + q], auxg[:, tp, q, 0:1]),
                reads=[("auxg", tp, q)], writes=[("tokR", tp, q)])
            x1g = bigX[:, qp, tp * 4 + 2 * (q // 2):tp * 4 + 2 * (q // 2) + 2,
                       :].rearrange("p a b -> p (a b)")
            xk = [BX(qp, tp * 4 + 2 * (q // 2)), BX(qp, tp * 4 + 2 * (q // 2) + 1)]
            dma("gpsimd", x1g, x1tok,
                reads=[("tokU", tp, q), ("x1tok",)],
                writes=[("x1g", tp, q)] + xk,
                indirect=("gather", idxs_u[:, tp, 12 + q:13 + q]),
                lane=("gd6", "gd7")[qp])

    tile_front(0)
    tile_w1(0)
    tile_w2(0)
    tile_tok_gather(0)
    tile_fin_pre(0)
    tile_tokens(0)
    for tau in range(NTILES):
        tile_mm1(tau)
        if tau + 1 < NTILES:
            tile_front(tau + 1)
            tile_w1(tau + 1)
            tile_tok_gather(tau + 1)
            tile_fin_pre(tau + 1)
        tile_mm2(tau)
        tile_y(tau)
        if tau + 1 < NTILES:
            tile_tokens(tau + 1)
            tile_w2(tau + 1)

    final_wait("sync", ["sd", "sd2", "sd3", "sd4", "gd", "gd2", "gd3", "gd4",
                        "gd5", "gd6", "gd7", "gd8"])

    @block.sync
    def _(sync):
        for fn in Q["sync"]:
            fn(sync)

    @block.tensor
    def _(tensor):
        for fn in Q["tensor"]:
            fn(tensor)

    @block.vector
    def _(vector):
        for fn in Q["vector"]:
            fn(vector)

    @block.scalar
    def _(scalar):
        for fn in Q["scalar"]:
            fn(scalar)

    @block.gpsimd
    def _(gpsimd):
        gpsimd.reg_mov(breg, T - 1)
        for fn in Q["gpsimd"]:
            fn(gpsimd)

    ctx.close()
    return nc


def make_inputs(core_x, params, NTILES):
    T = core_x.shape[0]
    tau = (np.arange(NTILES, dtype=np.float32) * T2)[:, None]
    d = {
        "xt": np.ascontiguousarray(core_x.T),
        "w_in_t": np.ascontiguousarray(params["W_in"].T),
        "w_out_t": np.ascontiguousarray(params["W_out"].T),
        "lam_log": params["lambda_log"].reshape(1, D),
        "gamma_d": params["gamma"].reshape(1, D),
        "b_in_d": params["b_in"].reshape(1, D),
        "b_out_d": params["b_out"].reshape(1, D),
        "wr_d": params["Wr"],
        "br_d": params["br"].reshape(E, 1),
        "w1_d": params["W1"],
        "b1_d": params["b1"],
        "w2_d": params["W2"],
        "b2_d": params["b2"],
        "tri_d": np.triu(np.ones((CH, CH), np.float32)),
        "iden_d": np.eye(128, dtype=np.float32),
        "iotaE_d": np.broadcast_to(np.arange(E, dtype=np.float32), (128, E)),
        "negsidx_d": (63.0 - np.arange(128, dtype=np.float32)).reshape(128, 1),
        "tix_d": (np.arange(128, dtype=np.float32) - 63.0).reshape(1, 128),
        "ones1_d": np.ones((1, 128), np.float32),
        "onescol_d": np.ones((128, 1), np.float32),
        "zrow_d": np.zeros((1, E), np.float32),
        "zrowD_d": np.zeros((1, D), np.float32),
        "zaux_d": np.array([[float(core_x.shape[0]), 0.0]], np.float32),
        "iotaQ_d": (np.arange(128, dtype=np.float32)[:, None]
                    + 128.0 * np.arange(4, dtype=np.float32)[None, :]),
        "thresh_d": np.broadcast_to(
            (np.arange(16, dtype=np.float32) * T2)[None, :], (E, 16)
        ).reshape(1, E * 16),
        "tau_d": np.broadcast_to(tau, (NTILES, E)).reshape(1, NTILES * E),
        "iotaE24_d": np.broadcast_to(
            np.arange(E, dtype=np.float32)[None, :], (NTILES, E)
        ).reshape(1, NTILES * E),
    }
    return {k: np.ascontiguousarray(np.asarray(v, dtype=np.float32)) for k, v in d.items()}


def kernel(L0_attn_x, L0_attn_lambda_log, L0_attn_gamma, L0_attn_W_in, L0_attn_b_in,
           L0_attn_W_out, L0_attn_b_out, L0_ffn_Wr, L0_ffn_br, L0_ffn_W1, L0_ffn_b1,
           L0_ffn_W2, L0_ffn_b2):
    x = np.asarray(L0_attn_x, dtype=np.float32)
    B, L, Dm = x.shape
    assert Dm == D and B % NCORES == 0
    n_seq = B // NCORES
    NTILES = (n_seq * L) // T2 + E
    params = {
        "lambda_log": np.asarray(L0_attn_lambda_log, np.float32),
        "gamma": np.asarray(L0_attn_gamma, np.float32),
        "W_in": np.asarray(L0_attn_W_in, np.float32),
        "b_in": np.asarray(L0_attn_b_in, np.float32),
        "W_out": np.asarray(L0_attn_W_out, np.float32),
        "b_out": np.asarray(L0_attn_b_out, np.float32),
        "Wr": np.asarray(L0_ffn_Wr, np.float32),
        "br": np.asarray(L0_ffn_br, np.float32),
        "W1": np.asarray(L0_ffn_W1, np.float32),
        "b1": np.asarray(L0_ffn_b1, np.float32),
        "W2": np.asarray(L0_ffn_W2, np.float32),
        "b2": np.asarray(L0_ffn_b2, np.float32),
    }
    nc = build_program(n_seq, L)
    in_maps = []
    for c in range(NCORES):
        core_x = x[c * n_seq:(c + 1) * n_seq].reshape(n_seq * L, D)
        in_maps.append(make_inputs(core_x, params, NTILES))
    res = run_bass_kernel_spmd(nc, in_maps, core_ids=list(range(NCORES)))
    outs = []
    for c in range(NCORES):
        o = np.asarray(res.results[c]["out"])
        outs.append(o.reshape(n_seq, L, D))
    return np.concatenate(outs, axis=0).astype(np.float32)


# version 31
# speedup vs baseline: 1.1652x; 1.0538x over previous
import sys
sys.path.insert(0, '/opt/trn_rl_repo')
import contextlib
import numpy as np
import concourse.bass as bass
import concourse.mybir as mybir
from concourse.bass_utils import run_bass_kernel_spmd

F32 = mybir.dt.float32
BF16 = mybir.dt.bfloat16
U32 = mybir.dt.uint32
F32R = mybir.dt.float32r
AF = mybir.ActivationFunctionType
ALU = mybir.AluOpType
AX = mybir.AxisListType

D = 1024
DC = 8
H = 2048
HT = 16
E = 8
T2 = 512
CH = 128
NCORES = 8
OP_TRACE = {"tensor": [], "note": None}


def build_program(n_seq, L):
    OP_TRACE["tensor"] = []
    T = n_seq * L
    NT2 = T // T2
    T2_PER_SEQ = L // T2
    NTILES = T // T2 + E
    SLOTROWS = E * T + NTILES * T2 + 1
    ZROW = SLOTROWS - 1
    NCHUNK = T // CH
    assert L % T2 == 0 and T2 % CH == 0

    nc = bass.Bass("TRN2", target_bir_lowering=False)

    xt = nc.dram_tensor("xt", [D, T], F32, kind="ExternalInput").ap()
    w_in_t = nc.dram_tensor("w_in_t", [D, D], F32, kind="ExternalInput").ap()
    w_out_t = nc.dram_tensor("w_out_t", [D, D], F32, kind="ExternalInput").ap()
    lam_log = nc.dram_tensor("lam_log", [1, D], F32, kind="ExternalInput").ap()
    gamma_d = nc.dram_tensor("gamma_d", [1, D], F32, kind="ExternalInput").ap()
    b_in_d = nc.dram_tensor("b_in_d", [1, D], F32, kind="ExternalInput").ap()
    b_out_d = nc.dram_tensor("b_out_d", [1, D], F32, kind="ExternalInput").ap()
    wr_d = nc.dram_tensor("wr_d", [D, E], F32, kind="ExternalInput").ap()
    br_d = nc.dram_tensor("br_d", [E, 1], F32, kind="ExternalInput").ap()
    w1_d = nc.dram_tensor("w1_d", [E, D, H], F32, kind="ExternalInput").ap()
    b1_d = nc.dram_tensor("b1_d", [E, H], F32, kind="ExternalInput").ap()
    w2_d = nc.dram_tensor("w2_d", [E, H, D], F32, kind="ExternalInput").ap()
    b2_d = nc.dram_tensor("b2_d", [E, D], F32, kind="ExternalInput").ap()
    tri_d = nc.dram_tensor("tri_d", [CH, CH], F32, kind="ExternalInput").ap()
    iden_d = nc.dram_tensor("iden_d", [128, 128], F32, kind="ExternalInput").ap()
    iotaE_d = nc.dram_tensor("iotaE_d", [128, E], F32, kind="ExternalInput").ap()
    negsidx_d = nc.dram_tensor("negsidx_d", [128, 1], F32, kind="ExternalInput").ap()
    tix_d = nc.dram_tensor("tix_d", [1, 128], F32, kind="ExternalInput").ap()
    ones1_d = nc.dram_tensor("ones1_d", [1, 128], F32, kind="ExternalInput").ap()
    onescol_d = nc.dram_tensor("onescol_d", [128, 1], F32, kind="ExternalInput").ap()
    zrow_d = nc.dram_tensor("zrow_d", [1, E], F32, kind="ExternalInput").ap()
    zrowD_d = nc.dram_tensor("zrowD_d", [1, D], F32, kind="ExternalInput").ap()
    iotaQ_d = nc.dram_tensor("iotaQ_d", [128, 4], F32, kind="ExternalInput").ap()
    thresh_d = nc.dram_tensor("thresh_d", [1, E * 16], F32, kind="ExternalInput").ap()
    tau_d = nc.dram_tensor("tau_d", [1, NTILES * E], F32, kind="ExternalInput").ap()
    iotaE24_d = nc.dram_tensor("iotaE24_d", [1, NTILES * E], F32, kind="ExternalInput").ap()
    zaux_d = nc.dram_tensor("zaux_d", [1, 2], F32, kind="ExternalInput").ap()
    out_t = nc.dram_tensor("out", [T, D], F32, kind="ExternalOutput").ap()
    slotaux = nc.dram_tensor("slotaux", [SLOTROWS, 2], F32, kind="Internal").ap()
    x1tok = nc.dram_tensor("x1tok", [T, D], F32, kind="Internal").ap()
    slotbuf = nc.dram_tensor("slotbuf", [SLOTROWS, D], BF16, kind="Internal").ap()
    w1s = nc.dram_tensor("w1s", [E * 128, DC * H], BF16, kind="Internal").ap()
    w2s = nc.dram_tensor("w2s", [E * 128, HT * D], BF16, kind="Internal").ap()
    b1cols = nc.dram_tensor("b1cols", [E * 128, HT], F32, kind="Internal").ap()
    b2cols = nc.dram_tensor("b2cols", [E * 128, DC], F32, kind="Internal").ap()

    ctx = contextlib.ExitStack()
    EN = ctx.enter_context
    wbA = EN(nc.sbuf_tensor("wbA", [128, 16384], BF16))
    wbB = EN(nc.sbuf_tensor("wbB", [128, 16384], BF16))
    hm = EN(nc.sbuf_tensor("hm", [128, HT, T2], BF16))
    bigX = EN(nc.sbuf_tensor("bigX", [128, 2, DC, T2], F32))
    xsq = EN(nc.sbuf_tensor("xsq", [128, 2, T2], F32))
    xb = EN(nc.sbuf_tensor("xb", [128, 2, T2], BF16))
    xsqb = EN(nc.sbuf_tensor("xsqb", [128, 2, T2], BF16))
    onescol_b = EN(nc.sbuf_tensor("onescol_b", [128, 1], BF16))
    lnx = EN(nc.sbuf_tensor("lnx", [128, 2, DC, T2], BF16))
    gub = EN(nc.sbuf_tensor("gub", [128, 2, D], BF16))
    hsb = EN(nc.sbuf_tensor("hsb", [128, DC, T2], BF16))
    scr = EN(nc.sbuf_tensor("scr", [128, 2048], F32))
    logits_sb = EN(nc.sbuf_tensor("logits_sb", [E, 128], F32))
    rowbuf = EN(nc.sbuf_tensor("rowbuf", [1, 3 * T2], F32))
    rsc = EN(nc.sbuf_tensor("rsc", [128, 16], F32))
    rsc2 = EN(nc.sbuf_tensor("rsc2", [128, 4, E], F32))
    hinit = EN(nc.sbuf_tensor("hinit", [128, DC], F32))
    pos_sb = EN(nc.sbuf_tensor("pos_sb", [128, NCHUNK], F32))
    mp_sb = EN(nc.sbuf_tensor("mp_sb", [128, NCHUNK], F32))
    x1row = EN(nc.sbuf_tensor("x1row", [128, 2, D], F32))
    ln2row = EN(nc.sbuf_tensor("ln2row", [128, 2, D], BF16))
    ln2d = EN(nc.sbuf_tensor("ln2d", [128, D], BF16))
    lnTg = EN(nc.sbuf_tensor("lnTg", [128, 4, D], BF16))
    yrow = EN(nc.sbuf_tensor("yrow", [128, 2, D], BF16))
    idxs_f = EN(nc.sbuf_tensor("idxs_f", [128, 2, 20], F32))
    idxs_u = EN(nc.sbuf_tensor("idxs_u", [128, 2, 20], U32))
    auxg = EN(nc.sbuf_tensor("auxg", [128, 2, 4, 2], F32))
    aux_sb = EN(nc.sbuf_tensor("aux_sb", [128, 2], F32))
    posU = EN(nc.sbuf_tensor("posU", [128, 2], U32))
    posUall = EN(nc.sbuf_tensor("posUall", [128, 64], U32))
    b1tile = EN(nc.sbuf_tensor("b1tile", [128, HT], F32))
    b2tile = EN(nc.sbuf_tensor("b2tile", [128, DC], F32))
    rc8 = EN(nc.sbuf_tensor("rc8", [1, E], F32))
    finrow = EN(nc.sbuf_tensor("finrow", [1, 4 * NTILES], F32))
    startrow = EN(nc.sbuf_tensor("startrow", [1, 16], F32))
    grow = EN(nc.sbuf_tensor("grow", [1, max(128, NTILES * E)], F32))
    tri_f = EN(nc.sbuf_tensor("tri_f", [128, 128], F32))
    tri_b = EN(nc.sbuf_tensor("tri_b", [128, 128], BF16))
    iden = EN(nc.sbuf_tensor("iden", [128, 128], F32))
    idenb = EN(nc.sbuf_tensor("idenb", [128, 128], BF16))
    iotaE = EN(nc.sbuf_tensor("iotaE", [128, E], F32))
    negsidx = EN(nc.sbuf_tensor("negsidx", [128, 1], F32))
    tix = EN(nc.sbuf_tensor("tix", [1, 128], F32))
    ones1 = EN(nc.sbuf_tensor("ones1", [1, 128], F32))
    onescol = EN(nc.sbuf_tensor("onescol", [128, 1], F32))
    iotaQ = EN(nc.sbuf_tensor("iotaQ", [128, 4], F32))
    thresh16 = EN(nc.sbuf_tensor("thresh16", [1, E * 16], F32))
    tau24 = EN(nc.sbuf_tensor("tau24", [1, NTILES * E], F32))
    iotaE24 = EN(nc.sbuf_tensor("iotaE24", [1, NTILES * E], F32))
    lamNegPow = EN(nc.sbuf_tensor("lamNegPow", [128, D], BF16))
    gbLam = EN(nc.sbuf_tensor("gbLam", [128, D], BF16))
    lamPow = EN(nc.sbuf_tensor("lamPow", [128, DC, CH], F32))
    lamC64 = EN(nc.sbuf_tensor("lamC64", [128, DC], F32))
    LCol = EN(nc.sbuf_tensor("LCol", [128, DC], F32))
    L8 = EN(nc.sbuf_tensor("L8", [DC, 128], F32))
    b1Col = EN(nc.sbuf_tensor("b1Col", [128, HT * E], F32))
    b2ColE = EN(nc.sbuf_tensor("b2ColE", [128, E * DC], F32))
    boCol = EN(nc.sbuf_tensor("boCol", [128, DC], F32))
    br_col = EN(nc.sbuf_tensor("br_col", [E, 1], F32))
    wr_b = EN(nc.sbuf_tensor("wr_b", [128, DC * E], BF16))
    pA = EN(nc.psum_tensor("pA", [128, 2, 512], F32))
    pB = EN(nc.psum_tensor("pB", [128, 2, 512], F32))
    pC = EN(nc.psum_tensor("pC", [128, 1024], F32))
    pD = EN(nc.psum_tensor("pD", [128, 2, 512], F32))
    semh = {}
    for name in ("sd", "sd2", "gd", "gd2", "gd3", "gd4", "gd5", "gd6", "gd7", "gd8", "sd3", "sd4", "pe", "act", "dve"):
        semh[name] = EN(nc.semaphore("s_" + name))
    breg = EN(nc.gpsimd.register("breg"))
    block = EN(nc.Block())

    ENG_SEM = {"sync": "sd", "gpsimd": "gd", "tensor": "pe", "vector": "dve",
               "scalar": "act"}
    QLANES = {"sync": ["sd", "sd2"], "gpsimd": ["gd", "gd2"]}
    cnt = {k: 0 for k in semh}
    Q = {k: [] for k in ENG_SEM}
    regions = {}
    pend = {k: {"w": [], "r": []} for k in ENG_SEM}
    satisfied = {k: {} for k in ENG_SEM}
    lane_rr = {"sync": 0, "gpsimd": 0}

    def R(k):
        if k not in regions:
            regions[k] = {"w": None, "r": {}}
        return regions[k]

    def op(eng, emit, reads=(), writes=(), inc=True, extra=(), sem=None, inc_k=1):
        sem = sem or ENG_SEM[eng]
        want = {}

        def addw(s, v):
            if v and want.get(s, 0) < v:
                want[s] = v
        for k in reads:
            reg = R(k)
            if reg["w"] is not None:
                addw(*reg["w"])
            assert all(k not in pend[e]["w"] for e in pend if e != eng), \
                f"read of pending write {k}"
        for k in writes:
            reg = R(k)
            if reg["w"] is not None:
                addw(*reg["w"])
            for s, v in reg["r"].items():
                addw(s, v)
            assert all(k not in pend[e]["w"] and k not in pend[e]["r"]
                       for e in pend if e != eng), f"write over pending {k}"
        for s, v in extra:
            addw(s, v)
        waits = tuple((s, v) for s, v in want.items()
                      if v > satisfied[eng].get(s, 0))
        for s, v in waits:
            satisfied[eng][s] = v
        post = None
        if inc:
            cnt[sem] += inc_k
            post = cnt[sem]
            for k in pend[eng]["w"] + list(writes):
                R(k)["w"] = (sem, post)
                R(k)["r"] = {}
            for k in pend[eng]["r"] + list(reads):
                R(k)["r"][sem] = post
            pend[eng] = {"w": [], "r": []}
        else:
            pend[eng]["w"] += list(writes)
            pend[eng]["r"] += list(reads)

        if eng == "tensor":
            OP_TRACE["tensor"].append((waits, tuple(reads), tuple(writes)))

        def fn(h, emit=emit, waits=waits, inc=inc, sem=sem, inc_k=inc_k):
            for s, v in waits:
                h.wait_ge(semh[s], v)
            ins = emit(h)
            if inc:
                ins.then_inc(semh[sem], inc_k)
        Q[eng].append(fn)
        return post

    def dma(queue, dst, src, reads=(), writes=(), indirect=None, lane=None):
        if lane is None:
            lane = QLANES[queue][lane_rr[queue]]
            lane_rr[queue] ^= 1
        prev = cnt[lane]
        if indirect is None:
            emit = lambda h, d=dst, s=src: h.dma_start(out=d, in_=s)
        else:
            kind, idx_ap = indirect[0], indirect[1]
            bc = indirect[2] if len(indirect) > 2 else None
            if kind == "gather":
                emit = lambda h, d=dst, s=src, i=idx_ap, bc=bc: h.indirect_dma_start(
                    out=d, out_offset=None, in_=s,
                    in_offset=bass.IndirectOffsetOnAxis(ap=i, axis=0),
                    **({} if bc is None else
                       {"bounds_check": bc, "oob_is_err": False}))
            else:
                emit = lambda h, d=dst, s=src, i=idx_ap, bc=bc: h.indirect_dma_start(
                    out=d, out_offset=bass.IndirectOffsetOnAxis(ap=i, axis=0),
                    in_=s, in_offset=None,
                    **({} if bc is None else
                       {"bounds_check": bc, "oob_is_err": False}))
        return op(queue, emit, reads=reads, writes=writes, sem=lane, inc_k=16,
                  extra=([(lane, prev)] if prev else []))

    def final_wait(eng, sems_to_wait):
        vals = [(s, cnt[s]) for s in sems_to_wait if cnt[s] > 0]

        def fn(h, vals=tuple(vals)):
            for s, v in vals:
                h.wait_ge(semh[s], v)
        Q[eng].append(fn)

    xt_r = xt.rearrange("(c p) t -> p c t", c=DC)
    BX = lambda par, dc: ("bigX", par, dc)
    x1d = [wbB[:, 0:8192].bitcast(F32).rearrange("p (a b) -> p a b", a=DC),
           wbB[:, 8192:16384].bitcast(F32).rearrange("p (a b) -> p a b", a=DC)]
    X1K = lambda par, dc: ("x1d", par, dc)
    ALL_X1K = [("x1d", p, d) for p in range(2) for d in range(DC)]

    def load_x(j, par):
        dma("sync", bigX[:, par, 0:4, :], xt_r[:, 0:4, j * T2:(j + 1) * T2],
            writes=[BX(par, dc) for dc in range(4)])
        dma("sync", bigX[:, par, 4:8, :], xt_r[:, 4:8, j * T2:(j + 1) * T2],
            writes=[BX(par, dc) for dc in range(4, 8)])

    load_x(0, 0)
    load_x(1, 1)

    for dst, src, key in [
            (tri_f[:], tri_d, "tri_f"), (iden[:], iden_d, "iden"),
            (iotaE[:], iotaE_d, "iotaE"), (negsidx[:], negsidx_d, "negsidx"),
            (tix[:], tix_d, "tix"), (ones1[:], ones1_d, "ones1"),
            (onescol[:], onescol_d, "onescol"),
            (br_col[:], br_d, "br_col"), (rc8[:], zrow_d, "rc8"),
            (iotaQ[:], iotaQ_d, "iotaQ"), (thresh16[:], thresh_d, "thresh16"),
            (tau24[:], tau_d, "tau24"), (iotaE24[:], iotaE24_d, "iotaE24")]:
        dma("sync", dst, src, writes=[key])
    dma("sync", L8[:], lam_log.rearrange("o (a p) -> (o a) p", a=DC), writes=["L8"])

    op("vector", lambda h: h.tensor_copy(tri_b[:], tri_f[:]),
       reads=["tri_f"], writes=["tri_b"])
    op("vector", lambda h: h.tensor_copy(idenb[:], iden[:]),
       reads=["iden"], writes=["idenb"])
    op("vector", lambda h: h.tensor_copy(onescol_b[:], onescol[:]),
       reads=["onescol"], writes=["onescol_b"])

    dma("gpsimd", slotbuf[ZROW:ZROW + 1, :], zrowD_d, writes=[("slotzero",)])
    dma("gpsimd", slotaux[ZROW:ZROW + 1, :], zaux_d, writes=[("slotauxzero",)])
    dma("gpsimd", wbA[:, 0:8192].rearrange("p (c t) -> p c t", c=DC),
        w_in_t.rearrange("(c p) t -> p c t", c=DC), writes=[("wbA", 0)])
    dma("gpsimd", wbA[:, 8192:16384].rearrange("p (c t) -> p c t", c=DC),
        w_out_t.rearrange("(c p) t -> p c t", c=DC), writes=[("wbA", 1)])
    dma("gpsimd", wr_b[:].rearrange("p (c e) -> p c e", c=DC),
        wr_d.rearrange("(c p) e -> p c e", c=DC), writes=["wr_b"])

    op("tensor", lambda h: h.transpose(pC[:, 0:DC], L8[:], iden[0:DC, 0:DC]),
       reads=["L8", "iden"], writes=[("pC", 0)])
    op("scalar", lambda h: h.copy(LCol[:], pC[:, 0:DC]),
       reads=[("pC", 0)], writes=["LCol"])
    op("scalar", lambda h: h.activation(lamC64[:], LCol[:], AF.Exp, scale=64.0),
       reads=["LCol"], writes=["lamC64"])
    dma("sync", scr[0:DC, 1024:1152],
        b_out_d.rearrange("o (a p) -> (o a) p", a=DC), writes=[("scr", 1)])
    op("tensor", lambda h: h.transpose(pC[:, 16:24], scr[0:DC, 1024:1152],
                                       iden[0:DC, 0:DC]),
       reads=[("scr", 1), "iden"], writes=[("pC", 0)])
    op("scalar", lambda h: h.copy(boCol[:], pC[:, 16:24]),
       reads=[("pC", 0)], writes=["boCol"])
    for half in range(2):
        dma("sync", scr[0:E, 1024:2048], b1_d[:, half * 1024:(half + 1) * 1024],
            writes=[("scr", 1)])
        for hh in range(8):
            ht = half * 8 + hh
            op("tensor", lambda h, ht=ht, hh=hh: h.transpose(
                pC[:, 512 + ht * 8: 512 + (ht + 1) * 8],
                scr[0:E, 1024 + hh * 128:1024 + (hh + 1) * 128], iden[0:E, 0:E]),
                reads=[("scr", 1), "iden"], writes=[("pC", 1)], inc=(hh == 7))
    op("scalar", lambda h: h.copy(
        b1Col[:].rearrange("p (e t) -> p t e", e=E),
        pC[:, 512:640].rearrange("p (t e) -> p t e", t=HT)),
        reads=[("pC", 1)], writes=["b1Col"])
    dma("sync", b1cols.rearrange("(e p) t -> p e t", e=E),
        b1Col[:].rearrange("p (e t) -> p e t", e=E),
        reads=["b1Col"], writes=["b1cols"])
    dma("sync", scr[0:E, 1024:2048], b2_d, writes=[("scr", 1)])
    for dt in range(DC):
        op("tensor", lambda h, dt=dt: h.transpose(
            pC[:, 512 + dt * 8:512 + (dt + 1) * 8],
            scr[0:E, 1024 + dt * 128:1024 + (dt + 1) * 128], iden[0:E, 0:E]),
            reads=[("scr", 1), "iden"], writes=[("pC", 1)], inc=(dt == DC - 1))
    op("scalar", lambda h: h.copy(
        b2ColE[:].rearrange("p (e t) -> p t e", e=E),
        pC[:, 512:576].rearrange("p (t e) -> p t e", t=DC)),
        reads=[("pC", 1)], writes=["b2ColE"])
    dma("sync", b2cols.rearrange("(e p) t -> p e t", e=E),
        b2ColE[:].rearrange("p (e t) -> p e t", e=E),
        reads=["b2ColE"], writes=["b2cols"])

    dma("sync", rowbuf[:, 0:1024], lam_log, writes=["rowbuf"])
    op("tensor", lambda h: h.matmul(pA[:, 0, :], ones1[:], rowbuf[:, 0:512],
                                    start=True, stop=True),
       reads=["ones1", "rowbuf"], writes=[("pA", 0)])
    op("tensor", lambda h: h.matmul(pA[:, 1, :], ones1[:], rowbuf[:, 512:1024],
                                    start=True, stop=True),
       reads=["ones1", "rowbuf"], writes=[("pA", 1)])
    op("vector", lambda h: h.tensor_scalar(
        scr[:, 1024:2048], pA[:].rearrange("p a b -> p (a b)"), negsidx[:],
        None, ALU.mult),
        reads=[("pA", 0), ("pA", 1), "negsidx"], writes=[("scr", 1)])
    op("scalar", lambda h: h.activation(lamNegPow[:], scr[:, 1024:2048], AF.Exp),
       reads=[("scr", 1)], writes=["lamNegPow"])
    dma("sync", rowbuf[:, 0:1024], gamma_d, writes=["rowbuf"])
    dma("sync", scr[0:1, 1024:2048], b_in_d, writes=[("scr", 1)])
    op("tensor", lambda h: h.matmul(pD[:, 0, :], ones1[:], rowbuf[:, 0:512],
                                    start=True, stop=True),
       reads=["ones1", "rowbuf"], writes=[("pD", 0)])
    op("tensor", lambda h: h.matmul(pD[:, 1, :], ones1[:], rowbuf[:, 512:1024],
                                    start=True, stop=True),
       reads=["ones1", "rowbuf"], writes=[("pD", 1)])
    op("vector", lambda h: h.tensor_tensor(
        rowbuf[:, 0:1024], rowbuf[:, 0:1024], scr[0:1, 1024:2048], ALU.mult),
        reads=["rowbuf", ("scr", 1)], writes=["rowbuf"])
    op("tensor", lambda h: h.matmul(pB[:, 0, :], ones1[:], rowbuf[:, 0:512],
                                    start=True, stop=True),
       reads=["ones1", "rowbuf"], writes=[("pB", 0)])
    op("tensor", lambda h: h.matmul(pB[:, 1, :], ones1[:], rowbuf[:, 512:1024],
                                    start=True, stop=True),
       reads=["ones1", "rowbuf"], writes=[("pB", 1)])
    op("vector", lambda h: h.tensor_tensor(
        gbLam[:], pB[:].rearrange("p a b -> p (a b)"), lamNegPow[:], ALU.mult),
        reads=[("pB", 0), ("pB", 1), "lamNegPow"], writes=["gbLam"])
    op("vector", lambda h: h.tensor_tensor(
        lamNegPow[:], lamNegPow[:], pD[:].rearrange("p a b -> p (a b)"), ALU.mult),
        reads=["lamNegPow", ("pD", 0), ("pD", 1)], writes=["lamNegPow"])
    op("tensor", lambda h: h.matmul(pC[:, 0:128], ones1[:], tix[:],
                                    start=True, stop=True),
       reads=["ones1", "tix"], writes=[("pC", 0)])
    for k in range(DC):
        op("vector", lambda h, k=k: h.tensor_scalar(
            scr[:, 1024 + k * 128:1024 + (k + 1) * 128], pC[:, 0:128],
            LCol[:, k:k + 1], None, ALU.mult),
            reads=[("pC", 0), "LCol"], writes=[("scr", 1)])
    op("scalar", lambda h: h.activation(
        lamPow[:].rearrange("p a b -> p (a b)"), scr[:, 1024:2048], AF.Exp),
        reads=[("scr", 1)], writes=["lamPow"])
    conv_emits = []
    for e in range(E):
        conv_emits.append(lambda e=e: op("gpsimd", lambda h, e=e: h.dma_start(
            out=w1s[e * 128:(e + 1) * 128, :].rearrange("p (c h) -> p c h", c=DC),
            in_=w1_d[e].rearrange("(c p) h -> p c h", c=DC)),
            writes=[("w1s", e)], sem="gd3", inc_k=16))
        for half in range(2):
            conv_emits.append(lambda e=e, half=half: op(
                "gpsimd", lambda h, e=e, half=half: h.dma_start(
                    out=w2s[e * 128:(e + 1) * 128,
                            half * 8192:(half + 1) * 8192].rearrange("p (c t) -> p c t", c=8),
                    in_=w2_d[e].rearrange("(c p) t -> p c t", c=HT)[:, half * 8:(half + 1) * 8, :]),
                writes=[("w2s", e)], sem="gd3", inc_k=16))

    def emit_ln_p1ab(par):
        for dc in range(DC):
            op("scalar", lambda h, dc=dc, par=par: h.copy(
                xb[:, dc % 2, :], bigX[:, par, dc, :]),
                reads=[BX(par, dc)], writes=[("xb", dc % 2)])
            op("vector", lambda h, dc=dc: h.tensor_tensor(
                xsqb[:, dc % 2, :], xb[:, dc % 2, :], xb[:, dc % 2, :], ALU.mult),
                reads=[("xb", dc % 2)], writes=[("xsqb", dc % 2)])
            op("tensor", lambda h, dc=dc: h.matmul(
                pA[0:1, 0, :], onescol_b[:], xb[:, dc % 2, :],
                start=(dc == 0), stop=(dc == DC - 1)),
                reads=["onescol_b", ("xb", dc % 2)], writes=[("pA", 0)],
                inc=True)
            op("tensor", lambda h, dc=dc: h.matmul(
                pA[0:1, 1, :], onescol_b[:], xsqb[:, dc % 2, :],
                start=(dc == 0), stop=(dc == DC - 1)),
                reads=["onescol_b", ("xsqb", dc % 2)], writes=[("pA", 1)],
                inc=True)
        op("vector", lambda h: h.tensor_scalar(
            rowbuf[:, 0:T2], pA[0:1, 0, :], 1.0 / D, None, ALU.mult),
            reads=[("pA", 0)], writes=["rowbuf"])
        op("vector", lambda h: h.tensor_scalar(
            rowbuf[:, T2:2 * T2], pA[0:1, 1, :], 1.0 / D, 1e-5, ALU.mult, ALU.add),
            reads=[("pA", 1)], writes=["rowbuf"])
        op("vector", lambda h: h.tensor_tensor(
            rowbuf[:, 2 * T2:3 * T2], rowbuf[:, 0:T2], rowbuf[:, 0:T2], ALU.mult),
            reads=["rowbuf"], writes=["rowbuf"])
        op("vector", lambda h: h.tensor_tensor(
            rowbuf[:, T2:2 * T2], rowbuf[:, T2:2 * T2], rowbuf[:, 2 * T2:3 * T2],
            ALU.subtract),
            reads=["rowbuf"], writes=["rowbuf"])
        op("scalar", lambda h: h.activation(
            rowbuf[:, T2:2 * T2], rowbuf[:, T2:2 * T2], AF.Ln),
            reads=["rowbuf"], writes=["rowbuf"])
        op("scalar", lambda h: h.activation(
            rowbuf[:, 2 * T2:3 * T2], rowbuf[:, T2:2 * T2], AF.Exp, scale=-0.5),
            reads=["rowbuf"], writes=["rowbuf"])
        op("vector", lambda h: h.tensor_tensor(
            rowbuf[:, T2:2 * T2], rowbuf[:, 0:T2], rowbuf[:, 2 * T2:3 * T2],
            ALU.mult),
            reads=["rowbuf"], writes=["rowbuf"])

    def emit_ln_p1c():
        op("tensor", lambda h: h.matmul(
            pC[:, 0:512], ones1[:], rowbuf[:, 2 * T2:3 * T2], start=True, stop=True),
            reads=["ones1", "rowbuf"], writes=[("pC", 0)])
        op("tensor", lambda h: h.matmul(
            pC[:, 512:1024], ones1[:], rowbuf[:, T2:2 * T2], start=True, stop=True),
            reads=["ones1", "rowbuf"], writes=[("pC", 1)])

    def emit_ln_p2(par):
        rbc = pC[:, 0:512].unsqueeze(1).broadcast_to([128, DC, 512])
        mbc = pC[:, 512:1024].unsqueeze(1).broadcast_to([128, DC, 512])
        op("vector", lambda h, par=par, rbc=rbc: h.tensor_tensor(
            lnx[:, par], bigX[:, par], rbc, ALU.mult),
            reads=[("pC", 0)] + [BX(par, dc) for dc in range(DC)],
            writes=[("lnx", par, dc) for dc in range(DC)])
        op("vector", lambda h, par=par, mbc=mbc: h.tensor_tensor(
            lnx[:, par], lnx[:, par], mbc, ALU.subtract),
            reads=[("pC", 1)] + [("lnx", par, dc) for dc in range(DC)],
            writes=[("lnx", par, dc) for dc in range(DC)])

    def tail_a1(j, q):
        par = j % 2
        cidx = j * 4 + q
        qp = q % 2
        for g in range(2):
            for dd in range(4):
                dc = g * 4 + dd
                op("tensor", lambda h, dc=dc, dd=dd, q=q, par=par: h.transpose(
                    pD[:, 0, dd * 128:(dd + 1) * 128],
                    x1d[par][:, dc, q * 128:(q + 1) * 128], iden[:]),
                    reads=[X1K(par, dc), "iden"], writes=[("pD", 0)],
                    inc=(dd == 3))
            op("scalar", lambda h, g=g, qp=qp: h.copy(
                x1row[:, qp, g * 512:(g + 1) * 512], pD[:, 0, :]),
                reads=[("pD", 0)], writes=[("x1row", qp)])
        dma("sync", x1tok[cidx * 128:(cidx + 1) * 128, :], x1row[:, qp, :],
            reads=[("x1row", qp)], writes=[("x1tok",)])
        op("scalar", lambda h, qp=qp: h.activation(
            lnTg[:, 0, :], x1row[:, qp, :], AF.Identity,
            accum_out=rsc[:, 6 + qp:7 + qp]),
            reads=[("x1row", qp)], writes=[("lnTg", 0), ("rsc", 6 + qp)])
        op("scalar", lambda h, qp=qp: h.activation(
            lnTg[:, 1, :], x1row[:, qp, :], AF.Square,
            accum_out=rsc[:, 8 + qp:9 + qp]),
            reads=[("x1row", qp)], writes=[("lnTg", 1), ("rsc", 8 + qp)])

    def tail_a2(j, q):
        qp = q % 2
        op("vector", lambda h, qp=qp: h.tensor_scalar(
            rsc[:, 10 + qp:11 + qp], rsc[:, 6 + qp:7 + qp], 1.0 / D, None,
            ALU.mult),
            reads=[("rsc", 6 + qp)], writes=[("rsc", 10 + qp)])
        op("vector", lambda h, qp=qp: h.tensor_scalar(
            rsc[:, 12 + qp:13 + qp], rsc[:, 8 + qp:9 + qp], 1.0 / D, 1e-5,
            ALU.mult, ALU.add),
            reads=[("rsc", 8 + qp)], writes=[("rsc", 12 + qp)])
        op("vector", lambda h, qp=qp: h.tensor_tensor(
            rsc[:, 8 + qp:9 + qp], rsc[:, 10 + qp:11 + qp],
            rsc[:, 10 + qp:11 + qp], ALU.mult),
            reads=[("rsc", 10 + qp)], writes=[("rsc", 8 + qp)])
        op("vector", lambda h, qp=qp: h.tensor_tensor(
            rsc[:, 12 + qp:13 + qp], rsc[:, 12 + qp:13 + qp],
            rsc[:, 8 + qp:9 + qp], ALU.subtract),
            reads=[("rsc", 8 + qp), ("rsc", 12 + qp)], writes=[("rsc", 12 + qp)])
        op("scalar", lambda h, qp=qp: h.activation(
            rsc[:, 12 + qp:13 + qp], rsc[:, 12 + qp:13 + qp], AF.Ln),
            reads=[("rsc", 12 + qp)], writes=[("rsc", 12 + qp)])
        op("scalar", lambda h, qp=qp: h.activation(
            rsc[:, 12 + qp:13 + qp], rsc[:, 12 + qp:13 + qp], AF.Exp,
            scale=-0.5),
            reads=[("rsc", 12 + qp)], writes=[("rsc", 12 + qp)])
        op("vector", lambda h, qp=qp: h.tensor_tensor(
            rsc[:, 10 + qp:11 + qp], rsc[:, 10 + qp:11 + qp],
            rsc[:, 12 + qp:13 + qp], ALU.mult),
            reads=[("rsc", 12 + qp), ("rsc", 10 + qp)], writes=[("rsc", 10 + qp)])
        op("vector", lambda h, qp=qp: h.tensor_scalar(
            rsc[:, 10 + qp:11 + qp], rsc[:, 10 + qp:11 + qp], -1.0, None,
            ALU.mult),
            reads=[("rsc", 10 + qp)], writes=[("rsc", 10 + qp)])
        op("scalar", lambda h, qp=qp: h.activation(
            ln2row[:, qp, :], x1row[:, qp, :], AF.Identity,
            bias=rsc[:, 10 + qp:11 + qp], scale=rsc[:, 12 + qp:13 + qp]),
            reads=[("x1row", qp), ("rsc", 10 + qp), ("rsc", 12 + qp)],
            writes=[("ln2row", qp)])

    def tail_b1(j, q):
        for g in range(2):
            for dd in range(4):
                dc = g * 4 + dd
                op("tensor", lambda h, dc=dc, dd=dd, qp=q % 2: h.transpose(
                    pD[0:128, 1, 256 + dd * 64:256 + (dd + 1) * 64].bitcast(BF16),
                    ln2row[:, qp, dc * 128:(dc + 1) * 128], idenb[:]),
                    reads=[("ln2row", q % 2), "idenb"], writes=[("pD", 1)],
                    inc=(dd == 3))
            op("scalar", lambda h, g=g: h.copy(
                ln2d[:, g * 512:(g + 1) * 512].rearrange("p (a b) -> p a b", a=4),
                pD[0:128, 1, 256:512].bitcast(BF16).rearrange("p (a b) -> p a b", a=4)),
                reads=[("pD", 1)], writes=["ln2d"])

    def tail_b2(j, q):
        for dc in range(DC):
            op("tensor", lambda h, dc=dc: h.matmul(
                pD[0:E, 1, 0:128],
                wr_b[:, dc * E:(dc + 1) * E],
                ln2d[:, dc * 128:(dc + 1) * 128],
                start=(dc == 0), stop=(dc == DC - 1)),
                reads=["wr_b", "ln2d"], writes=[("pD", 1)],
                inc=(dc == DC - 1))
        op("scalar", lambda h: h.activation(
            logits_sb[:], pD[0:E, 1, 0:128],
            AF.Identity, bias=br_col[:]),
            reads=[("pD", 1), "br_col"], writes=["logits_sb"])
        op("tensor", lambda h: h.transpose(
            pD[:, 1, 128:136],
            logits_sb[:], iden[0:E, 0:E]),
            reads=["logits_sb", "iden"], writes=[("pD", 1)])

    def tail_b3a(j, q):
        cidx = j * 4 + q
        op("vector", lambda h: h.tensor_reduce(
            rsc[:, 0:1], pD[:, 1, 128:136], AX.X, ALU.max, negate=True),
            reads=[("pD", 1)], writes=[("rsc", 0)])
        op("scalar", lambda h: h.activation(
            lnTg[:, 2, 0:8], pD[:, 1, 128:136], AF.Exp,
            bias=rsc[:, 0:1], accum_out=rsc[:, 1:2]),
            reads=[("pD", 1), ("rsc", 0)], writes=[("lnTg", 2), ("rsc", 1)])
        op("vector", lambda h: h.reciprocal(rsc[:, 2:3], rsc[:, 1:2]),
           reads=[("rsc", 1)], writes=[("rsc", 2)])
        op("vector", lambda h: h.tensor_scalar(
            rsc2[:, 0, :], pD[:, 1, 128:136],
            rsc[:, 0:1], 0.0, ALU.add, ALU.is_ge),
            reads=[("pD", 1), ("rsc", 0)], writes=[("rsc2", 0)])
        op("vector", lambda h: h.tensor_scalar(
            rsc2[:, 1, :], rsc2[:, 0, :], -1e9, 1e9, ALU.mult, ALU.add),
            reads=[("rsc2", 0)], writes=[("rsc2", 1)])
        op("vector", lambda h: h.tensor_tensor(
            rsc2[:, 1, :], rsc2[:, 1, :], iotaE[:], ALU.add),
            reads=[("rsc2", 1), "iotaE"], writes=[("rsc2", 1)])
        op("vector", lambda h: h.tensor_reduce(
            rsc[:, 3:4], rsc2[:, 1, :], AX.X, ALU.min),
            reads=[("rsc2", 1)], writes=[("rsc", 3)])
        op("vector", lambda h: h.tensor_scalar(
            rsc2[:, 2, :], iotaE[:], rsc[:, 3:4], None, ALU.is_equal),
            reads=["iotaE", ("rsc", 3)], writes=[("rsc2", 2)])

    def tail_b3b(j, q):
        cidx = j * 4 + q
        op("tensor", lambda h: h.matmul(
            pD[0:128, 1, 136:144], ones1[:], rc8[:], start=True, stop=False),
            reads=["ones1", "rc8"], writes=[("pD", 1)])
        op("tensor", lambda h: h.matmul(
            pD[0:128, 1, 136:144], tri_f[:], rsc2[:, 2, :], start=False, stop=True),
            reads=["tri_f", ("rsc2", 2)], writes=[("pD", 1)])
        op("tensor", lambda h: h.matmul(
            pD[0:1, 1, 144:152], onescol[:], rsc2[:, 2, :], start=True, stop=True),
            reads=["onescol", ("rsc2", 2)], writes=[("pD", 1)])
        op("vector", lambda h: h.tensor_tensor(
            rsc2[:, 3, :], rsc2[:, 2, :], pD[0:128, 1, 136:144], ALU.mult),
            reads=[("rsc2", 2), ("pD", 1)], writes=[("rsc2", 3)])
        op("vector", lambda h: h.tensor_reduce(
            rsc[:, 4:5], rsc2[:, 3, :], AX.X, ALU.add),
            reads=[("rsc2", 3)], writes=[("rsc", 4)])
        op("vector", lambda h: h.tensor_scalar(
            rsc[:, 5:6], rsc[:, 4:5], 1.0, -1.0, ALU.mult, ALU.add),
            reads=[("rsc", 4)], writes=[("rsc", 5)])
        op("vector", lambda h, cidx=cidx: h.tensor_scalar(
            pos_sb[:, cidx:cidx + 1], rsc[:, 3:4], float(T), rsc[:, 5:6],
            ALU.mult, ALU.add),
            reads=[("rsc", 3), ("rsc", 5)], writes=["pos_sb"])
        op("vector", lambda h: h.tensor_tensor(
            rc8[:], rc8[:], pD[0:1, 1, 144:152], ALU.add),
            reads=["rc8", ("pD", 1)], writes=["rc8"])
        op("vector", lambda h, cidx=cidx: h.tensor_copy(
            posU[:, 0:1], pos_sb[:, cidx:cidx + 1]),
            reads=["pos_sb"], writes=[("posU2", 0)])
        op("vector", lambda h, cidx=cidx: h.tensor_scalar(
            aux_sb[:, 0:1], iotaQ[:, 0:1], 1.0, float(cidx * 128),
            ALU.mult, ALU.add),
            reads=["iotaQ"], writes=["aux_sb"])
        op("vector", lambda h: h.tensor_copy(aux_sb[:, 1:2], rsc[:, 2:3]),
           reads=[("rsc", 2)], writes=["aux_sb"])
        dma("gpsimd", slotaux, aux_sb[:],
            reads=["aux_sb", ("posU2", 0)], writes=[("slotaux",)],
            indirect=("scatter", posU[:, 0:1]))
        dma("gpsimd", slotbuf, ln2row[:, q % 2, :],
            reads=[("ln2row", q % 2), ("posU2", 0)], writes=[("slotbuf",)],
            indirect=("scatter", posU[:, 0:1]))

    tail_stream = []

    def push_tail(j):
        def bchain(q):
            return [lambda q=q: tail_b1(j, q), lambda q=q: tail_b2(j, q),
                    lambda q=q: tail_b3a(j, q), lambda q=q: tail_b3b(j, q)]
        for q in range(4):
            tail_stream.append(lambda q=q: tail_a1(j, q))
            tail_stream.append(lambda q=q: tail_a2(j, q))
            if q >= 1:
                tail_stream.extend(bchain(q - 1))
        tail_stream.extend(bchain(3))

    def flush_tail(n):
        for _ in range(min(n, len(tail_stream))):
            tail_stream.pop(0)()

    emit_ln_p1ab(0)
    emit_ln_p1c()
    emit_ln_p2(0)
    for j in range(NT2):
        par = j % 2
        seq_t2 = j % T2_PER_SEQ
        if j > 0:
            push_tail(j - 1)
        for q in range(T2 // CH):
            qpar = q % 2
            for half in range(2):
                for dc in range(DC):
                    op("tensor", lambda h, q=q, half=half, dc=dc, par=par: h.matmul(
                        pA[:, half, :],
                        lnx[:, par, dc, q * 128:(q + 1) * 128],
                        wbA[:, dc * 1024 + half * 512: dc * 1024 + (half + 1) * 512],
                        start=(dc == 0), stop=(dc == DC - 1)),
                        reads=[("lnx", par, dc), ("wbA", 0)], writes=[("pA", half)],
                        inc=(dc == DC - 1))
                op("vector", lambda h, half=half: h.tensor_tensor(
                    xsq[:, half, :],
                    pA[:, half, :], lamNegPow[:, half * 512:(half + 1) * 512],
                    ALU.mult),
                    reads=[("pA", half), "lamNegPow"], writes=[("xsq", half)])
            op("vector", lambda h, qpar=qpar: h.tensor_tensor(
                gub[:, qpar, :], xsq[:].rearrange("p a b -> p (a b)"),
                gbLam[:], ALU.add),
                reads=[("xsq", 0), ("xsq", 1), "gbLam"], writes=[("gub", qpar)])
            for k in range(DC):
                op("tensor", lambda h, k=k, qpar=qpar: h.matmul(
                    pC[:, k * 128:(k + 1) * 128],
                    gub[:, qpar, k * 128:(k + 1) * 128], tri_b[:],
                    start=True, stop=True),
                    reads=[("gub", qpar), "tri_b"], writes=[("pC", k // 4)])
            first = (seq_t2 == 0 and q == 0)
            hsel = hsb[:].rearrange("p a (n b) -> p a n b", b=CH)
            if not first:
                hinit_bc = hinit[:].unsqueeze(2).broadcast_to([128, DC, CH])
                op("vector", lambda h, hinit_bc=hinit_bc: h.tensor_tensor(
                    scr[:, 0:1024].rearrange("p (a b) -> p a b", a=DC),
                    pC[:].rearrange("p (a b) -> p a b", a=DC), hinit_bc, ALU.add),
                    reads=[("pC", 0), ("pC", 1), "hinit"], writes=[("scr", 0)])
                op("vector", lambda h, q=q, hsel=hsel: h.tensor_tensor(
                    hsel[:, :, q, :],
                    scr[:, 0:1024].rearrange("p (a b) -> p a b", a=DC),
                    lamPow[:], ALU.mult),
                    reads=[("scr", 0), "lamPow"], writes=["hsb"])
            else:
                op("vector", lambda h, q=q, hsel=hsel: h.tensor_tensor(
                    hsel[:, :, q, :],
                    pC[:].rearrange("p (a b) -> p a b", a=DC),
                    lamPow[:], ALU.mult),
                    reads=[("pC", 0), ("pC", 1), "lamPow"], writes=["hsb"])
            op("vector", lambda h, q=q: h.tensor_tensor(
                hinit[:],
                hsb[:, :, q * 128 + 127:q * 128 + 128].squeeze(2),
                lamC64[:], ALU.mult),
                reads=["hsb", "lamC64"], writes=["hinit"])
            flush_tail(4)
        for k in range(DC):
            if k == 1 and j + 1 < NT2:
                emit_ln_p1ab(1 - par)
            if k == 2:
                flush_tail(4)
            if k == 4:
                flush_tail(4)
                if j + 1 < NT2:
                    emit_ln_p1c()
            if k == 5 and j + 1 < NT2:
                emit_ln_p2(1 - par)
            for dc in range(DC):
                op("tensor", lambda h, k=k, dc=dc: h.matmul(
                    pB[:, k % 2, :],
                    wbA[:, 8192 + dc * 1024 + k * 128: 8192 + dc * 1024 + (k + 1) * 128],
                    hsb[:, dc, :], start=(dc == 0), stop=(dc == DC - 1)),
                    reads=[("wbA", 1), "hsb"], writes=[("pB", k % 2)],
                    inc=(dc == DC - 1))
            op("scalar", lambda h, k=k, par=par: h.activation(
                x1d[par][:, k, :], pB[:, k % 2, :],
                AF.Identity, bias=boCol[:, k:k + 1]),
                reads=[("pB", k % 2), "boCol"], writes=[X1K(par, k)])
            if k % 2 == 1:
                op("vector", lambda h, k=k, par=par: h.tensor_tensor(
                    x1d[par][:, k - 1:k + 1, :], x1d[par][:, k - 1:k + 1, :],
                    lnx[:, par, k - 1:k + 1, :], ALU.add),
                    reads=[X1K(par, k - 1), X1K(par, k),
                           ("lnx", par, k - 1), ("lnx", par, k)],
                    writes=[X1K(par, k - 1), X1K(par, k)])
                op("vector", lambda h, k=k, par=par: h.tensor_tensor(
                    x1d[par][:, k - 1:k + 1, :], x1d[par][:, k - 1:k + 1, :],
                    bigX[:, par, k - 1:k + 1, :], ALU.add),
                    reads=[X1K(par, k - 1), X1K(par, k),
                           BX(par, k - 1), BX(par, k)],
                    writes=[X1K(par, k - 1), X1K(par, k)])
        if j + 2 < NT2:
            load_x(j + 2, par)
        if j >= 2:
            for _ in range(2):
                if conv_emits:
                    conv_emits.pop(0)()

    while conv_emits:
        conv_emits.pop(0)()
    _conv_total = cnt["gd3"]
    for e in range(E):
        regions[("w1s", e)]["w"] = ("gd3", _conv_total)
        regions[("w2s", e)]["w"] = ("gd3", _conv_total)

    push_tail(NT2 - 1)
    flush_tail(10 ** 9)

    op("vector", lambda h: h.tensor_tensor(
        grow[:, 0:128].rearrange("o (e k) -> o e k", e=E),
        rc8[:].unsqueeze(2).broadcast_to([1, E, 16]),
        thresh16[:].rearrange("o (e k) -> o e k", e=E), ALU.is_gt),
        reads=["rc8", "thresh16"], writes=["grow"])
    op("vector", lambda h: h.tensor_reduce(
        finrow[:, 0:E], grow[:, 0:128].rearrange("o (e k) -> o e k", e=E),
        AX.X, ALU.add),
        reads=["grow"], writes=["finrow"])
    op("vector", lambda h: h.tensor_scalar(
        finrow[:, 0:E], finrow[:, 0:E], float(T2), None, ALU.mult),
        reads=["finrow"], writes=["finrow"])
    op("vector", lambda h: h.tensor_scalar(
        startrow[:, 0:1], finrow[:, 0:1], 0.0, None, ALU.mult),
        reads=["finrow"], writes=["startrow"])
    for e in range(E):
        op("vector", lambda h, e=e: h.tensor_tensor(
            startrow[:, e + 1:e + 2], startrow[:, e:e + 1],
            finrow[:, e:e + 1], ALU.add),
            reads=["startrow", "finrow"], writes=["startrow"])
    op("vector", lambda h: h.tensor_tensor(
        grow[:, 0:NTILES * E].rearrange("o (t e) -> o t e", e=E),
        tau24[:].rearrange("o (t e) -> o t e", e=E),
        startrow[:, 0:E].unsqueeze(1).broadcast_to([1, NTILES, E]), ALU.is_ge),
        reads=["tau24", "startrow"], writes=["grow"])
    op("vector", lambda h: h.tensor_reduce(
        finrow[:, NTILES:2 * NTILES], grow[:, 0:NTILES * E].rearrange("o (t e) -> o t e", e=E),
        AX.X, ALU.add),
        reads=["grow"], writes=["finrow"])
    op("vector", lambda h: h.tensor_scalar(
        finrow[:, NTILES:2 * NTILES], finrow[:, NTILES:2 * NTILES], 1.0, -1.0,
        ALU.mult, ALU.add),
        reads=["finrow"], writes=["finrow"])
    op("vector", lambda h: h.tensor_tensor(
        grow[:, 0:NTILES * E].rearrange("o (t e) -> o t e", e=E),
        iotaE24[:].rearrange("o (t e) -> o t e", e=E),
        finrow[:, NTILES:2 * NTILES].unsqueeze(2).broadcast_to([1, NTILES, E]),
        ALU.is_equal),
        reads=["iotaE24", "finrow"], writes=["grow"])
    op("vector", lambda h: h.tensor_tensor(
        grow[:, 0:NTILES * E].rearrange("o (t e) -> o t e", e=E),
        grow[:, 0:NTILES * E].rearrange("o (t e) -> o t e", e=E),
        startrow[:, 0:E].unsqueeze(1).broadcast_to([1, NTILES, E]), ALU.mult),
        reads=["grow", "startrow"], writes=["grow"])
    op("vector", lambda h: h.tensor_reduce(
        finrow[:, 2 * NTILES:3 * NTILES],
        grow[:, 0:NTILES * E].rearrange("o (t e) -> o t e", e=E), AX.X, ALU.add),
        reads=["grow"], writes=["finrow"])
    op("vector", lambda h: h.tensor_tensor(
        grow[:, 0:NTILES * E].rearrange("o (t e) -> o t e", e=E),
        iotaE24[:].rearrange("o (t e) -> o t e", e=E),
        finrow[:, NTILES:2 * NTILES].unsqueeze(2).broadcast_to([1, NTILES, E]),
        ALU.is_equal),
        reads=["iotaE24", "finrow"], writes=["grow"])
    op("vector", lambda h: h.tensor_tensor(
        grow[:, 0:NTILES * E].rearrange("o (t e) -> o t e", e=E),
        grow[:, 0:NTILES * E].rearrange("o (t e) -> o t e", e=E),
        rc8[:].unsqueeze(1).broadcast_to([1, NTILES, E]), ALU.mult),
        reads=["grow", "rc8"], writes=["grow"])
    op("vector", lambda h: h.tensor_reduce(
        finrow[:, 0:NTILES],
        grow[:, 0:NTILES * E].rearrange("o (t e) -> o t e", e=E), AX.X, ALU.add),
        reads=["grow"], writes=["finrow"])
    op("vector", lambda h: h.tensor_scalar(
        grow[:, 0:NTILES], finrow[:, NTILES:2 * NTILES], float(T), None, ALU.mult),
        reads=["finrow"], writes=["grow"])
    op("vector", lambda h: h.tensor_tensor(
        finrow[:, 0:NTILES], finrow[:, 0:NTILES], grow[:, 0:NTILES], ALU.add),
        reads=["finrow", "grow"], writes=["finrow"])
    op("vector", lambda h: h.tensor_scalar(
        finrow[:, 3 * NTILES:4 * NTILES], finrow[:, NTILES:2 * NTILES],
        float(T), None, ALU.mult),
        reads=["finrow"], writes=["finrow"])
    op("vector", lambda h: h.tensor_tensor(
        finrow[:, 3 * NTILES:4 * NTILES], finrow[:, 3 * NTILES:4 * NTILES],
        tau24[:].rearrange("o (t e) -> o t e", e=E)[:, :, 0], ALU.add),
        reads=["finrow", "tau24"], writes=["finrow"])
    op("vector", lambda h: h.tensor_tensor(
        finrow[:, 3 * NTILES:4 * NTILES], finrow[:, 3 * NTILES:4 * NTILES],
        finrow[:, 2 * NTILES:3 * NTILES], ALU.subtract),
        reads=["finrow"], writes=["finrow"])

    def tile_front(tau):
        tp = tau % 2
        op("tensor", lambda h, tau=tau: h.matmul(
            pC[:, 0:1], ones1[:], finrow[:, NTILES + tau:NTILES + tau + 1],
            start=True, stop=True),
            reads=["ones1", "finrow"], writes=[("pC", 0)])
        op("tensor", lambda h, tau=tau: h.matmul(
            pC[:, 1:2], ones1[:], finrow[:, 3 * NTILES + tau:3 * NTILES + tau + 1],
            start=True, stop=True),
            reads=["ones1", "finrow"], writes=[("pC", 0)])
        op("tensor", lambda h, tau=tau: h.matmul(
            pC[:, 2:3], ones1[:], finrow[:, tau:tau + 1],
            start=True, stop=True),
            reads=["ones1", "finrow"], writes=[("pC", 0)])
        if tau > 0:
            op("tensor", lambda h, tau=tau: h.matmul(
                pC[:, 3:4], ones1[:], finrow[:, NTILES + tau - 1:NTILES + tau],
                start=True, stop=True),
                reads=["ones1", "finrow"], writes=[("pC", 0)])
        op("vector", lambda h: h.tensor_scalar(
            idxs_f[:, tp, 0:1], pC[:, 0:1], 128.0, iotaQ[:, 0:1], ALU.mult, ALU.add),
            reads=[("pC", 0), "iotaQ"], writes=[("idxs_f", tp)])
        if tau > 0:
            op("vector", lambda h: h.tensor_scalar(
                idxs_f[:, tp, 11:12], pC[:, 3:4], 1.0, None, ALU.mult),
                reads=[("pC", 0)], writes=[("idxs_f", tp)])
            op("vector", lambda h: h.tensor_tensor(
                idxs_f[:, tp, 11:12], idxs_f[:, tp, 11:12], pC[:, 0:1],
                ALU.is_equal),
                reads=[("idxs_f", tp), ("pC", 0)], writes=[("idxs_f", tp)])
            op("vector", lambda h: h.tensor_scalar(
                idxs_f[:, tp, 11:12], idxs_f[:, tp, 11:12], 16384.0, None,
                ALU.mult),
                reads=[("idxs_f", tp)], writes=[("idxs_f", tp)])
            op("vector", lambda h: h.tensor_tensor(
                idxs_f[:, tp, 0:1], idxs_f[:, tp, 0:1], idxs_f[:, tp, 11:12],
                ALU.add),
                reads=[("idxs_f", tp)], writes=[("idxs_f", tp)])
        for q in range(4):
            op("vector", lambda h, q=q: h.tensor_scalar(
                idxs_f[:, tp, 9:10], pC[:, 1:2], 1.0, iotaQ[:, q:q + 1],
                ALU.mult, ALU.add),
                reads=[("pC", 0), "iotaQ"], writes=[("idxs_f", tp)])
            op("vector", lambda h, q=q: h.tensor_scalar(
                idxs_f[:, tp, 10:11], idxs_f[:, tp, 9:10], float(E * T), None,
                ALU.is_lt),
                reads=[("idxs_f", tp)], writes=[("idxs_f", tp)])
            op("vector", lambda h, q=q: h.tensor_tensor(
                idxs_f[:, tp, 1 + q:2 + q], idxs_f[:, tp, 9:10], idxs_f[:, tp, 10:11],
                ALU.mult),
                reads=[("idxs_f", tp)], writes=[("idxs_f", tp)])
            op("vector", lambda h, q=q: h.tensor_scalar(
                idxs_f[:, tp, 11:12], idxs_f[:, tp, 10:11], -1.0, 1.0, ALU.mult, ALU.add),
                reads=[("idxs_f", tp)], writes=[("idxs_f", tp)])
            op("vector", lambda h, q=q, tau=tau: h.tensor_scalar(
                idxs_f[:, tp, 10:11], iotaQ[:, q:q + 1], 1.0,
                float(E * T + tau * T2), ALU.mult, ALU.add),
                reads=["iotaQ"], writes=[("idxs_f", tp)])
            op("vector", lambda h, q=q: h.tensor_tensor(
                idxs_f[:, tp, 11:12], idxs_f[:, tp, 11:12], idxs_f[:, tp, 10:11], ALU.mult),
                reads=[("idxs_f", tp)], writes=[("idxs_f", tp)])
            op("vector", lambda h, q=q: h.tensor_tensor(
                idxs_f[:, tp, 1 + q:2 + q], idxs_f[:, tp, 1 + q:2 + q], idxs_f[:, tp, 11:12],
                ALU.add),
                reads=[("idxs_f", tp)], writes=[("idxs_f", tp)])
            op("vector", lambda h, q=q: h.tensor_scalar(
                idxs_f[:, tp, 10:11], idxs_f[:, tp, 9:10], pC[:, 2:3], None, ALU.is_lt),
                reads=[("idxs_f", tp), ("pC", 0)], writes=[("idxs_f", tp)])
            op("vector", lambda h, q=q: h.tensor_tensor(
                idxs_f[:, tp, 5 + q:6 + q], idxs_f[:, tp, 9:10], idxs_f[:, tp, 10:11],
                ALU.mult),
                reads=[("idxs_f", tp)], writes=[("idxs_f", tp)])
            op("vector", lambda h, q=q: h.tensor_scalar(
                idxs_f[:, tp, 11:12], idxs_f[:, tp, 10:11], float(-ZROW), float(ZROW),
                ALU.mult, ALU.add),
                reads=[("idxs_f", tp)], writes=[("idxs_f", tp)])
            op("vector", lambda h, q=q: h.tensor_tensor(
                idxs_f[:, tp, 5 + q:6 + q], idxs_f[:, tp, 5 + q:6 + q], idxs_f[:, tp, 11:12],
                ALU.add),
                reads=[("idxs_f", tp)], writes=[("idxs_f", tp)])
        op("vector", lambda h: h.tensor_copy(idxs_u[:, tp, 0:12], idxs_f[:, tp, 0:12]),
           reads=[("idxs_f", tp)], writes=[("idxs_u", tp)])

    def tile_w1(tau):
        tp = tau % 2
        dma("gpsimd", wbA[:], w1s, reads=[("idxs_u", tp)] + [("w1s", e) for e in range(E)],
            writes=[("wbA", 0), ("wbA", 1)],
            indirect=("gather", idxs_u[:, tp, 0:1], breg), lane="gd")
        dma("gpsimd", b1tile[:], b1cols, reads=[("idxs_u", tp), "b1cols"],
            writes=["b1tile"], indirect=("gather", idxs_u[:, tp, 0:1], breg),
            lane="gd5")

    def tile_w2(tau):
        tp = tau % 2
        dma("gpsimd", wbB[:], w2s, reads=[("idxs_u", tp)] + [("w2s", e) for e in range(E)],
            writes=[("wbB",)] + ALL_X1K,
            indirect=("gather", idxs_u[:, tp, 0:1], breg), lane="gd2")
        dma("gpsimd", b2tile[:], b2cols, reads=[("idxs_u", tp), "b2cols"],
            writes=["b2tile"], indirect=("gather", idxs_u[:, tp, 0:1], breg),
            lane="gd5")

    def tile_tok_gather(tau):
        tp = tau % 2
        for q in range(4):
            qp = q
            dma("gpsimd", lnTg[:, qp, :], slotbuf,
                reads=[("idxs_u", tp), ("slotbuf",), ("slotzero",)], writes=[("lnTg", qp)],
                indirect=("gather", idxs_u[:, tp, 5 + q:6 + q]),
                lane=("gd3", "gd4")[q % 2])

    def tile_tokens(tau):
        tp = tau % 2
        banks = [pD[0:128, 0, 0:512], pD[0:128, 1, 0:512],
                 pC[:, 512:1024], pC[:, 0:512]]
        keys = [("pD", 0), ("pD", 1), ("pC", 1), ("pC", 0)]
        for q in range(4):
            for dc in range(DC):
                op("tensor", lambda h, dc=dc, q=q, banks=banks: h.transpose(
                    banks[q][:, dc * 64:(dc + 1) * 64].bitcast(BF16),
                    lnTg[:, q, dc * 128:(dc + 1) * 128], idenb[:]),
                    reads=[("lnTg", q), "idenb"], writes=[keys[q]],
                    inc=(dc == DC - 1))
        for q in range(4):
            op("scalar", lambda h, q=q, tp=tp, banks=banks: h.copy(
                lnx[:, tp, :, q * 128:(q + 1) * 128],
                banks[q].bitcast(BF16).rearrange("p (a b) -> p a b", a=DC)),
                reads=[keys[q]], writes=[("lnx", tp, dc) for dc in range(DC)])

    def tile_mm1(tau):
        tp = tau % 2
        for ht in range(HT):
            hp = ht % 2
            for dc in range(DC):
                op("tensor", lambda h, ht=ht, dc=dc, hp=hp, tp=tp: h.matmul(
                    pA[:, hp, :],
                    wbA[:, dc * H + ht * 128: dc * H + (ht + 1) * 128],
                    lnx[:, tp, dc, :], start=(dc == 0), stop=(dc == DC - 1)),
                    reads=[("wbA", 0), ("wbA", 1), ("lnx", tp, dc)],
                    writes=[("pA", hp)], inc=(dc == DC - 1))
            op("scalar", lambda h, ht=ht, hp=hp: h.activation(
                hm[:, ht, :], pA[:, hp, :], AF.Relu,
                bias=b1tile[:, ht:ht + 1]),
                reads=[("pA", hp), "b1tile"], writes=[("hm", ht)])

    def tile_mm2(tau):
        for dt in range(DC):
            dp = dt % 2
            for ht in range(HT):
                op("tensor", lambda h, dt=dt, ht=ht, dp=dp: h.matmul(
                    pB[:, dp, :],
                    wbB[:, ht * D + dt * 128: ht * D + (dt + 1) * 128],
                    hm[:, ht, :], start=(ht == 0), stop=(ht == HT - 1)),
                    reads=[("wbB",), ("hm", ht)], writes=[("pB", dp)],
                    inc=(ht == HT - 1))
            op("scalar", lambda h, dt=dt, dp=dp: h.activation(
                hsb[:, dt, :], pB[:, dp, :], AF.Identity,
                bias=b2tile[:, dt:dt + 1]),
                reads=[("pB", dp), "b2tile"], writes=["hsb"])

    def tile_y(tau):
        tp = tau % 2
        for q in range(4):
            qp = q % 2
            for dt in range(DC):
                op("tensor", lambda h, dt=dt, q=q: h.transpose(
                    pD[0:128, 1, dt * 64:(dt + 1) * 64].bitcast(BF16),
                    hsb[:, dt, q * 128:(q + 1) * 128], idenb[:]),
                    reads=["hsb", "idenb"], writes=[("pD", 1)],
                    inc=(dt == DC - 1))
            op("scalar", lambda h, q=q, qp=qp: h.copy(
                yrow[:, qp, :],
                pD[0:128, 1, 0:512].bitcast(BF16)),
                reads=[("pD", 1)], writes=[("yrow", qp)])
            x1g = bigX[:, qp, tp * 4 + 2 * (q // 2):tp * 4 + 2 * (q // 2) + 2,
                       :].rearrange("p a b -> p (a b)")
            op("vector", lambda h, q=q, qp=qp, tp=tp: h.tensor_scalar(
                ln2row[:, qp, :], yrow[:, qp, :], auxg[:, tp, q, 1:2], None,
                ALU.mult),
                reads=[("yrow", qp), ("auxg", tp, q)], writes=[("ln2row", qp)])
            op("vector", lambda h, q=q, qp=qp, x1g=x1g: h.tensor_tensor(
                x1g, x1g, ln2row[:, qp, :], ALU.add),
                reads=[("x1g", tp, q), ("ln2row", qp)],
                writes=[("x1g", tp, q)])
            dma("gpsimd", out_t, x1g,
                reads=[("x1g", tp, q), ("tokR", tp, q)], writes=[("out", 0)],
                indirect=("scatter", idxs_u[:, tp, 16 + q:17 + q], breg),
                lane="gd8")

    def tile_fin_pre(tau):
        tp = tau % 2
        for q in range(4):
            qp = q % 2
            dma("gpsimd", auxg[:, tp, q, :], slotaux,
                reads=[("idxs_u", tp), ("slotaux",), ("slotauxzero",)],
                writes=[("auxg", tp, q)],
                indirect=("gather", idxs_u[:, tp, 5 + q:6 + q]), lane="gd5")
            op("vector", lambda h, q=q, tp=tp: h.tensor_scalar(
                idxs_u[:, tp, 12 + q:13 + q], auxg[:, tp, q, 0:1],
                float(T - 1), None, ALU.min),
                reads=[("auxg", tp, q)], writes=[("tokU", tp, q)])
            op("vector", lambda h, q=q, tp=tp: h.tensor_copy(
                idxs_u[:, tp, 16 + q:17 + q], auxg[:, tp, q, 0:1]),
                reads=[("auxg", tp, q)], writes=[("tokR", tp, q)])
            x1g = bigX[:, qp, tp * 4 + 2 * (q // 2):tp * 4 + 2 * (q // 2) + 2,
                       :].rearrange("p a b -> p (a b)")
            xk = [BX(qp, tp * 4 + 2 * (q // 2)), BX(qp, tp * 4 + 2 * (q // 2) + 1)]
            dma("gpsimd", x1g, x1tok,
                reads=[("tokU", tp, q), ("x1tok",)],
                writes=[("x1g", tp, q)] + xk,
                indirect=("gather", idxs_u[:, tp, 12 + q:13 + q]),
                lane=("gd6", "gd7")[qp])

    tile_front(0)
    tile_w1(0)
    tile_w2(0)
    tile_tok_gather(0)
    tile_fin_pre(0)
    tile_tokens(0)
    for tau in range(NTILES):
        tile_mm1(tau)
        if tau + 1 < NTILES:
            tile_front(tau + 1)
            tile_w1(tau + 1)
            tile_tok_gather(tau + 1)
        tile_mm2(tau)
        tile_y(tau)
        if tau + 1 < NTILES:
            tile_fin_pre(tau + 1)
            tile_tokens(tau + 1)
            tile_w2(tau + 1)

    final_wait("sync", ["sd", "sd2", "sd3", "sd4", "gd", "gd2", "gd3", "gd4",
                        "gd5", "gd6", "gd7", "gd8"])

    @block.sync
    def _(sync):
        for fn in Q["sync"]:
            fn(sync)

    @block.tensor
    def _(tensor):
        for fn in Q["tensor"]:
            fn(tensor)

    @block.vector
    def _(vector):
        for fn in Q["vector"]:
            fn(vector)

    @block.scalar
    def _(scalar):
        for fn in Q["scalar"]:
            fn(scalar)

    @block.gpsimd
    def _(gpsimd):
        gpsimd.reg_mov(breg, T - 1)
        for fn in Q["gpsimd"]:
            fn(gpsimd)

    ctx.close()
    return nc


def make_inputs(core_x, params, NTILES):
    T = core_x.shape[0]
    tau = (np.arange(NTILES, dtype=np.float32) * T2)[:, None]
    d = {
        "xt": np.ascontiguousarray(core_x.T),
        "w_in_t": np.ascontiguousarray(params["W_in"].T),
        "w_out_t": np.ascontiguousarray(params["W_out"].T),
        "lam_log": params["lambda_log"].reshape(1, D),
        "gamma_d": params["gamma"].reshape(1, D),
        "b_in_d": params["b_in"].reshape(1, D),
        "b_out_d": params["b_out"].reshape(1, D),
        "wr_d": params["Wr"],
        "br_d": params["br"].reshape(E, 1),
        "w1_d": params["W1"],
        "b1_d": params["b1"],
        "w2_d": params["W2"],
        "b2_d": params["b2"],
        "tri_d": np.triu(np.ones((CH, CH), np.float32)),
        "iden_d": np.eye(128, dtype=np.float32),
        "iotaE_d": np.broadcast_to(np.arange(E, dtype=np.float32), (128, E)),
        "negsidx_d": (63.0 - np.arange(128, dtype=np.float32)).reshape(128, 1),
        "tix_d": (np.arange(128, dtype=np.float32) - 63.0).reshape(1, 128),
        "ones1_d": np.ones((1, 128), np.float32),
        "onescol_d": np.ones((128, 1), np.float32),
        "zrow_d": np.zeros((1, E), np.float32),
        "zrowD_d": np.zeros((1, D), np.float32),
        "zaux_d": np.array([[float(core_x.shape[0]), 0.0]], np.float32),
        "iotaQ_d": (np.arange(128, dtype=np.float32)[:, None]
                    + 128.0 * np.arange(4, dtype=np.float32)[None, :]),
        "thresh_d": np.broadcast_to(
            (np.arange(16, dtype=np.float32) * T2)[None, :], (E, 16)
        ).reshape(1, E * 16),
        "tau_d": np.broadcast_to(tau, (NTILES, E)).reshape(1, NTILES * E),
        "iotaE24_d": np.broadcast_to(
            np.arange(E, dtype=np.float32)[None, :], (NTILES, E)
        ).reshape(1, NTILES * E),
    }
    return {k: np.ascontiguousarray(np.asarray(v, dtype=np.float32)) for k, v in d.items()}


def kernel(L0_attn_x, L0_attn_lambda_log, L0_attn_gamma, L0_attn_W_in, L0_attn_b_in,
           L0_attn_W_out, L0_attn_b_out, L0_ffn_Wr, L0_ffn_br, L0_ffn_W1, L0_ffn_b1,
           L0_ffn_W2, L0_ffn_b2):
    x = np.asarray(L0_attn_x, dtype=np.float32)
    B, L, Dm = x.shape
    assert Dm == D and B % NCORES == 0
    n_seq = B // NCORES
    NTILES = (n_seq * L) // T2 + E
    params = {
        "lambda_log": np.asarray(L0_attn_lambda_log, np.float32),
        "gamma": np.asarray(L0_attn_gamma, np.float32),
        "W_in": np.asarray(L0_attn_W_in, np.float32),
        "b_in": np.asarray(L0_attn_b_in, np.float32),
        "W_out": np.asarray(L0_attn_W_out, np.float32),
        "b_out": np.asarray(L0_attn_b_out, np.float32),
        "Wr": np.asarray(L0_ffn_Wr, np.float32),
        "br": np.asarray(L0_ffn_br, np.float32),
        "W1": np.asarray(L0_ffn_W1, np.float32),
        "b1": np.asarray(L0_ffn_b1, np.float32),
        "W2": np.asarray(L0_ffn_W2, np.float32),
        "b2": np.asarray(L0_ffn_b2, np.float32),
    }
    nc = build_program(n_seq, L)
    in_maps = []
    for c in range(NCORES):
        core_x = x[c * n_seq:(c + 1) * n_seq].reshape(n_seq * L, D)
        in_maps.append(make_inputs(core_x, params, NTILES))
    res = run_bass_kernel_spmd(nc, in_maps, core_ids=list(range(NCORES)))
    outs = []
    for c in range(NCORES):
        o = np.asarray(res.results[c]["out"])
        outs.append(o.reshape(n_seq, L, D))
    return np.concatenate(outs, axis=0).astype(np.float32)
